# version 07
# baseline (speedup 1.0000x reference)
import numpy as np
import ml_dtypes
import concourse.bass as bass
import concourse.mybir as mybir
from concourse.bass_utils import run_bass_kernel_spmd

F32 = mybir.dt.float32
BF16 = mybir.dt.bfloat16
AF = mybir.ActivationFunctionType
ALU = mybir.AluOpType
AX = mybir.AxisListType

E = 1024
D = 64
H = 16
HID = 512
NEXP = 16
C = 128
TILE = 512
NCT = TILE // C
LN_EPS = 1e-5
RMS_EPS = 1e-5


class _RecInstr:
    __slots__ = ("incs",)

    def __init__(self):
        self.incs = []

    def then_inc(self, sem_handle, k):
        self.incs.append((sem_handle, k))
        return self


_SEQ_OPS = ("wait_ge", "nop", "drain")


def _ap_tensor_names(x):
    t = getattr(x, "tensor", None)
    if t is not None:
        return [t.name]
    return []


class Rec:

    def __init__(self, queue, hazard=False):
        self._q = queue
        self._hazard = hazard
        self._prev_outs = set()

    def __getattr__(self, op):
        q = self._q
        rec = self

        def call(*a, **kw):
            if rec._hazard and op not in _SEQ_OPS:
                aps = []
                outs = set()
                ins = set()
                pos_aps = [x for x in a if getattr(x, "tensor", None) is not None]
                if "out" in kw and getattr(kw["out"], "tensor", None) is not None:
                    outs.add(kw["out"].tensor.name)
                    pos_in_start = 0
                elif pos_aps:
                    outs.add(pos_aps[0].tensor.name)
                    pos_aps = pos_aps[1:]
                for x in pos_aps:
                    ins.add(x.tensor.name)
                for kname, x in kw.items():
                    if getattr(x, "tensor", None) is not None:
                        if kname in ("out", "accum_out", "out_max", "out_indices"):
                            outs.add(x.tensor.name)
                        else:
                            ins.add(x.tensor.name)
                if rec._prev_outs & (ins | outs):
                    q.append(("drain", (), {}, _RecInstr()))
                rec._prev_outs = outs
            ri = _RecInstr()
            q.append((op, a, kw, ri))
            return ri

        return call


class Sem:
    def __init__(self, handle):
        self.h = handle
        self.n = 0

    def inc(self, instr, k=1):
        instr.then_inc(self.h, k)
        self.n += k
        return self.n


def _replay(engine, queue):
    for op, a, kw, ri in queue:
        instr = getattr(engine, op)(*a, **kw)
        for sem_handle, k in ri.incs:
            instr.then_inc(sem_handle, k)


class Scope:
    def __init__(self, nc):
        self.nc = nc
        self.items = []

    def sbuf(self, name, shape, dtype):
        t = self.nc.sbuf_tensor(name, shape, dtype)
        h = t.__enter__()
        self.items.append(t)
        return h

    def close(self):
        for t in reversed(self.items):
            t.__exit__(None, None, None)
        self.items = []


def build_program(tpc, use_ag, stop_after=None, debug_tap=None, paranoid=0, aux=False):
    ntile = tpc // TILE
    nchunk = tpc // C
    FT = min(1024, tpc)
    nftile = tpc // FT
    nsub = FT // 128
    nth = FT // 512

    nc = bass.Bass("TRN2", target_bir_lowering=False, detect_race_conditions=False)

    def din(name, shape, dtype=F32):
        return nc.dram_tensor(name, shape, dtype, kind="ExternalInput").ap()

    def dint(name, shape, dtype=F32):
        return nc.dram_tensor(name, shape, dtype, kind="Internal").ap()

    x_in = din("x", [tpc, E])
    Wq_in = din("Wq", [E, E])
    Wk_in = din("Wk", [E, E])
    Wv_in = din("Wv", [E, E])
    Wo_in = din("Wo", [E, E])
    G1_in = din("G1", [E, D])
    G2_in = din("G2", [D, E])
    delta_in = din("delta_c", [128, 8])
    normw_in = din("normw_c", [128, 8])
    normw_row_in = din("normw_row", [1, E])
    gateT_in = din("gateT", [E, NEXP])
    w1T_in = din("w1T", [NEXP, E, HID])
    w3T_in = din("w3T", [NEXP, E, HID])
    w2T_in = din("w2T", [NEXP, HID, E])
    sw1T_in = din("sw1T", [E, HID])
    sw3T_in = din("sw3T", [E, HID])
    sw2T_in = din("sw2T", [HID, E])
    ident_bf_in = din("ident_bf", [128, 128], BF16)
    ident_f32_in = din("ident_f32", [128, 128])
    mask_st_in = din("mask_st", [128, 128])
    emask_in = din("emask", [128, 4])

    out_d = nc.dram_tensor("out", [tpc, E], F32, kind="ExternalOutput").ap()
    aux_d = nc.dram_tensor("aux", [1536, 4096], F32, kind="ExternalOutput").ap() if aux else None

    olocal_d = dint("olocal_d", [tpc, E])
    sgate_d = dint("sgate_d", [tpc, E])
    qte_d = dint("qte_d", [128, 8 * tpc], BF16)
    qto_d = dint("qto_d", [128, 8 * tpc], BF16)
    xp_d = dint("xp_d", [tpc, E])
    ln2T_d = dint("ln2T_d", [128, 8 * tpc], BF16)
    combT_d = dint("combT_d", [NEXP, tpc])
    w1b_d = dint("w1b_d", [NEXP, E, HID], BF16)
    w3b_d = dint("w3b_d", [NEXP, E, HID], BF16)
    w2b_d = dint("w2b_d", [NEXP, HID, E], BF16)
    if use_ag:
        cc_in_d = dint("cc_in", [128, 520])
        cc_out_d = dint("cc_out", [512, 520])

    P0 = Scope(nc)
    delta_neg = P0.sbuf("delta_neg", [128, 8], F32)
    normw_sb = P0.sbuf("normw_sb", [128, 8], F32)
    ident_bf = P0.sbuf("ident_bf_sb", [128, 128], BF16)
    ident_f32 = P0.sbuf("ident_f32_sb", [128, 128], F32)
    mask_st = P0.sbuf("mask_st_sb", [128, 128], F32)
    emask_sb = P0.sbuf("emask_sb", [128, 4], F32)
    ones_col = P0.sbuf("ones_col", [128, 1], F32)
    eps_col = P0.sbuf("eps_col", [128, 1], F32)
    S_sb = P0.sbuf("S_sb", [128, 512], F32)
    S_bf = P0.sbuf("S_bf_sb", [128, 512], BF16)
    Gamma_sb = P0.sbuf("Gamma_sb", [128, 8], F32)
    Gres_sb = P0.sbuf("Gres_sb", [128, 8 * nchunk], F32)
    lnS = P0.sbuf("lnS", [128, 32], F32)
    sq_scr = P0.sbuf("sq_scr", [128, E], F32)
    nwbc_sb = P0.sbuf("nwbc_sb", [128, E], F32)
    Wo_sb = P0.sbuf("Wo_sb", [128, 8 * E], BF16)
    gateT_sb = P0.sbuf("gateT_sb", [128, 8 * NEXP], BF16)

    psum_items = []

    def psum(name, shape, dtype):
        t = nc.psum_tensor(name, shape, dtype)
        h = t.__enter__()
        psum_items.append(t)
        return h

    pA = psum("pA", [128, 512], F32)
    pB = psum("pB", [128, 512], F32)
    pT = psum("pT", [128, 1024], BF16)
    pP = [psum("pP0", [128, 512], F32), psum("pP1", [128, 512], F32)]
    pO = psum("pO", [128, 1024], F32)
    pS = psum("pS", [128, 512], F32)

    sem_items = []

    def sem(name):
        t = nc.semaphore(name)
        h = t.__enter__()
        sem_items.append(t)
        return Sem(h)

    s_ld = sem("s_ld")
    s_gp = sem("s_gp")
    s_t = sem("s_t")
    s_v = sem("s_v")
    s_s = sem("s_s")
    s_cc = sem("s_cc")
    s_st = sem("s_st")

    s_bar = sem("s_bar")

    Q = {k: [] for k in ("tensor", "vector", "scalar", "sync", "gpsimd")}
    TE, VE, SC, SY, GP = (Rec(Q["tensor"]), Rec(Q["vector"], hazard=True),
                          Rec(Q["scalar"], hazard=True),
                          Rec(Q["sync"]), Rec(Q["gpsimd"]))
    ENGS = {"tensor": (TE, s_t), "vector": (VE, s_v), "scalar": (SC, s_s),
            "sync": (SY, s_ld), "gpsimd": (GP, s_gp)}

    def mark_last(qname, sem_obj, k=1):
        for op, a, kw, ri in reversed(Q[qname]):
            if op not in ("wait_ge", "nop", "drain", "engine_nop"):
                assert not ri.incs, f"double sync update on {qname}:{op}"
                ri.incs.append((sem_obj.h, k))
                sem_obj.n += k
                return sem_obj.n
        raise RuntimeError(f"no real instruction to mark in {qname}")

    def barrier():
        for name, s in (("tensor", s_t), ("vector", s_v), ("scalar", s_s)):
            for op, a, kw, ri in reversed(Q[name]):
                if op not in ("wait_ge", "nop", "drain", "engine_nop"):
                    if not any(h is s.h for h, _ in ri.incs):
                        assert not ri.incs, f"barrier collision on {name}:{op}"
                        ri.incs.append((s.h, 1))
                        s.n += 1
                    break
        waits = [(s_t, s_t.n), (s_v, s_v.n), (s_s, s_s.n),
                 (s_ld, s_ld.n), (s_st, s_st.n), (s_gp, s_gp.n), (s_cc, s_cc.n)]
        for name, (R_, _) in ENGS.items():
            for s, v in waits:
                R_.wait_ge(s.h, v)

    def finish(scopes):
        SY.wait_ge(s_st.h, s_st.n)
        block_cm = nc.Block()
        block = block_cm.__enter__()

        @block.tensor
        def _(t):
            _replay(t, Q["tensor"])

        @block.vector
        def _(v):
            _replay(v, Q["vector"])

        @block.scalar
        def _(sc):
            _replay(sc, Q["scalar"])

        @block.sync
        def _(sy):
            _replay(sy, Q["sync"])

        @block.gpsimd
        def _(g):
            _replay(g, Q["gpsimd"])

        block_cm.__exit__(None, None, None)
        for sc_ in scopes:
            sc_.close()
        for t in reversed(psum_items):
            t.__exit__(None, None, None)
        P0.close()
        for t in reversed(sem_items):
            t.__exit__(None, None, None)
        return nc

    P1 = Scope(nc)
    Wq_sb = P1.sbuf("Wq_sb", [128, 8 * E], BF16)
    Wk_sb = P1.sbuf("Wk_sb", [128, 8 * E], BF16)
    Wv_sb = P1.sbuf("Wv_sb", [128, 8 * E], BF16)
    G1_sb = P1.sbuf("G1_sb", [128, 8 * D], BF16)
    G2_sb = P1.sbuf("G2_sb", [64, E], BF16)
    xs_sb = P1.sbuf("xs", [128, 4 * E], F32)
    ln1_bf = [P1.sbuf(f"ln1_bf{i}", [128, E], BF16) for i in range(2)]
    ln1T_sb = P1.sbuf("ln1T_sb", [128, 8 * TILE], BF16)
    kT_j = [P1.sbuf(f"kTj{i}", [128, TILE], F32) for i in range(2)]
    qs_j = [P1.sbuf(f"qsj{i}", [128, TILE], F32) for i in range(2)]
    f_j = P1.sbuf("fj", [128, TILE], F32)
    Lraw_j = [P1.sbuf(f"Lrawj{i}", [128, TILE], F32) for i in range(2)]
    rrev_j = [P1.sbuf(f"rrevj{i}", [128, TILE + 4], F32) for i in range(2)]
    tmpq_j = [P1.sbuf(f"tmpqj{i}", [128, TILE], F32) for i in range(2)]
    colst = P1.sbuf("colst", [128, 64], F32)
    colst2_dbg = P1.sbuf("colst2_dbg", [128, 48], F32)
    Dcols = P1.sbuf("Dcols", [128, 32], F32)
    kscale = P1.sbuf("kscale", [128, 32], F32)
    qt_pf = [P1.sbuf("qt_pe", [128, 8 * TILE], BF16), P1.sbuf("qt_po", [128, 8 * TILE], BF16)]
    kt_pf = [P1.sbuf("kt_pe", [128, 8 * TILE], BF16), P1.sbuf("kt_po", [128, 8 * TILE], BF16)]
    khat_bf = P1.sbuf("khat_bf", [128, 8 * TILE], BF16)
    khat_tok = P1.sbuf("khat_tok", [128, E], BF16)
    V_tok = [P1.sbuf(f"V_tok{i}", [128, E], BF16) for i in range(2)]
    P_bf = P1.sbuf("P_bf", [128, 16 * C], BF16)
    o_scr = [P1.sbuf(f"o_scr{i}", [128, E], F32) for i in range(2)]
    sg_scr = [P1.sbuf(f"sg_scr{i}", [128, E], F32) for i in range(2)]
    ghT_sb = P1.sbuf("ghT_sb", [64, TILE], BF16)
    wcb_sb = [P1.sbuf(f"wcb{i}", [128, 2048], BF16) for i in range(2)]

    s_ld.inc(SY.dma_start(out=ident_bf[:, :], in_=ident_bf_in), 16)
    s_ld.inc(SY.dma_start(out=ident_f32[:, :], in_=ident_f32_in), 16)
    s_ld.inc(SY.dma_start(out=mask_st[:, :], in_=mask_st_in), 16)
    s_ld.inc(SY.dma_start(out=emask_sb[:, :], in_=emask_in), 16)
    s_ld.inc(SY.dma_start(out=delta_neg[:, :], in_=delta_in), 16)
    s_ld.inc(SY.dma_start(out=normw_sb[:, :], in_=normw_in), 16)
    ld_const = s_ld.inc(SY.dma_start(out=nwbc_sb[:, :], in_=normw_row_in.broadcast_to([128, E])), 16)

    s_gp.inc(GP.dma_start(out=Wq_sb.rearrange("p (a n) -> p a n", a=8), in_=Wq_in.rearrange("(a p) n -> p a n", p=128)), 16)
    s_gp.inc(GP.dma_start(out=Wk_sb.rearrange("p (a n) -> p a n", a=8), in_=Wk_in.rearrange("(a p) n -> p a n", p=128)), 16)
    s_gp.inc(GP.dma_start(out=Wv_sb.rearrange("p (a n) -> p a n", a=8), in_=Wv_in.rearrange("(a p) n -> p a n", p=128)), 16)
    s_gp.inc(GP.dma_start(out=G1_sb.rearrange("p (a n) -> p a n", a=8), in_=G1_in.rearrange("(a p) n -> p a n", p=128)), 16)
    gp_attnw = s_gp.inc(GP.dma_start(out=G2_sb[:, :], in_=G2_in), 16)
    gp_wo = s_gp.inc(GP.dma_start(out=Wo_sb.rearrange("p (a n) -> p a n", a=8),
                                  in_=Wo_in.rearrange("(a p) n -> p a n", p=128)), 16)
    gp_gateT = s_gp.inc(GP.dma_start(out=gateT_sb.rearrange("p (a n) -> p a n", a=8),
                                     in_=gateT_in.rearrange("(a p) n -> p a n", p=128)), 16)
    if stop_after == "p0b":
        GP.wait_ge(s_gp.h, s_gp.n)
        return finish([P1])

    VE.wait_ge(s_ld.h, ld_const)
    VE.tensor_scalar_mul(delta_neg[:, :], delta_neg[:, :], -1.0)
    VE.memset(ones_col[:, :], 1.0)
    VE.memset(eps_col[:, :], RMS_EPS)
    VE.memset(S_sb[:, :], 0.0)
    VE.memset(S_bf[:, :], 0.0)
    VE.memset(Gamma_sb[:, :], 1.0)
    VE.memset(rrev_j[0][:, TILE:], 1.0)
    VE.memset(rrev_j[1][:, TILE:], 1.0)
    VE.memset(qt_pf[0][:, :], 0.0)
    VE.memset(qt_pf[1][:, :], 0.0)
    VE.memset(kt_pf[0][:, :], 0.0)
    VE.memset(kt_pf[1][:, :], 0.0)
    v_const = mark_last("vector", s_v)
    if stop_after == "p0a":
        SY.wait_ge(s_ld.h, s_ld.n)
        return finish([P1])

    if stop_after == "sent1":
        VE.wait_ge(s_gp.h, gp_attnw)
        sentinel(1, 7.0)
        return finish([P1])
    cast_jobs = []
    for src, dst in [(w1T_in, w1b_d), (w3T_in, w3b_d), (w2T_in, w2b_d)]:
        fs = src.rearrange("e a b -> (e a b)")
        fd = dst.rearrange("e a b -> (e a b)")
        total = NEXP * E * HID
        step = 128 * 2048
        for off in range(0, total, step):
            cast_jobs.append((fs[off:off + step].rearrange("(p n) -> p n", p=128),
                              fd[off:off + step].rearrange("(p n) -> p n", p=128)))
    cast_marks = []
    for i, (src, dst) in enumerate(cast_jobs):
        buf = wcb_sb[i % 2]
        if i >= 2:
            GP.wait_ge(s_gp.h, cast_marks[i - 2])
        m = s_gp.inc(GP.dma_start(out=buf[:, :], in_=src), 16)
        GP.wait_ge(s_gp.h, m)
        cast_marks.append(s_gp.inc(GP.dma_start(out=dst, in_=buf[:, :]), 16))
    def sentinel(blk, val, eng_wait=None):
        if eng_wait is not None:
            VE.wait_ge(eng_wait[0].h, eng_wait[1])
        m = s_v.inc(VE.memset(sq_scr[:, :], float(val)))
        SY.wait_ge(s_v.h, m)
        s_st.inc(SY.dma_start(out=out_d[blk * 128:(blk + 1) * 128, :], in_=sq_scr[:, :]), 16)

    if stop_after in ("sent", "sent0", "sent1", "sent2"):
        sentinel(0, 7.0)
    if stop_after == "sent0":
        return finish([P1])
    gp_cast_done = s_gp.n
    if stop_after == "sent2":
        VE.wait_ge(s_gp.h, gp_cast_done)
        sentinel(1, 7.0)
        return finish([P1])
    if stop_after == "p0":
        GP.wait_ge(s_gp.h, s_gp.n)
        SY.wait_ge(s_ld.h, s_ld.n)
        return finish([P1])

    x_free = v_const
    pT_free = v_const
    pS_free = v_const
    pO_free = v_const
    pAB_free = None
    last_P_evacs = None
    st_p1 = [None] * ntile
    ln1bf_free = [0, 0]
    Vtok_free = [0, 0]
    ost_free = [0, 0]
    sgst_free = [0, 0]
    s_kT_m = [None, None]
    s_qs_m = [None, None]
    v_j_m = [None, None]
    scale_free = [None, None]
    p1_te_prev = 0

    for T in range(ntile):
        SY.wait_ge(s_v.h, x_free)
        ld_x = s_ld.inc(SY.dma_start(
            out=xs_sb.rearrange("p (c e) -> p c e", c=4),
            in_=x_in[T * TILE:(T + 1) * TILE, :].rearrange("(c p) e -> p c e", p=128)), 16)
        if T == 0:
            TE.wait_ge(s_gp.h, gp_attnw)
            TE.wait_ge(s_ld.h, ld_const)
        if stop_after == "probe_stats" and T == 0:
            VE.wait_ge(s_ld.h, ld_x)
            SC.wait_ge(s_ld.h, ld_x)
            for c in range(4):
                VE.reduce_sum(colst[:, c:c + 1], xs_sb[:, c * E:(c + 1) * E], axis=AX.X)
                SC.activation(sq_scr[:, :], xs_sb[:, c * E:(c + 1) * E], AF.Square,
                              accum_out=colst2_dbg[:, c:c + 1])
            mv = mark_last("vector", s_v)
            ms = mark_last("scalar", s_s)
            VE.wait_ge(s_s.h, ms)
            mv2 = s_v.inc(VE.tensor_copy(xs_sb[:, 0:4], colst[:, 0:4]))
            mv3 = s_v.inc(VE.tensor_copy(xs_sb[:, 4:8], colst2_dbg[:, 0:4]))
            SY.wait_ge(s_v.h, mv3)
            s_st.inc(SY.dma_start(out=out_d[0:128, 0:8], in_=xs_sb[:, 0:8]), 16)
            SY.wait_ge(s_st.h, s_st.n)
            return finish([P1])

        if stop_after == "sent" and T == 0:
            sentinel(1, 7.0)
        v_ln = [None] * NCT
        for c in range(NCT):
            xsl = xs_sb[:, c * E:(c + 1) * E]
            SC.wait_ge(s_ld.h, ld_x)
            SC.activation(sq_scr[:, :], xsl, AF.Square, accum_out=lnS[:, 2:3])
            m_ssq = mark_last("scalar", s_s)
            VE.wait_ge(s_ld.h, ld_x)
            VE.reduce_sum(lnS[:, 0:1], xsl, axis=AX.X)
            VE.tensor_scalar_mul(lnS[:, 1:2], lnS[:, 0:1], 1.0 / E)
            VE.wait_ge(s_s.h, m_ssq)
            VE.tensor_mul(lnS[:, 5:6], lnS[:, 1:2], lnS[:, 1:2])
            VE.tensor_scalar(lnS[:, 6:7], lnS[:, 2:3], 1.0 / E, None, ALU.mult)
            VE.tensor_sub(lnS[:, 7:8], lnS[:, 6:7], lnS[:, 5:6])
            m_var = s_v.inc(VE.tensor_scalar(lnS[:, 3:4], lnS[:, 7:8], LN_EPS, None, ALU.add))
            SC.wait_ge(s_v.h, m_var)
            m_sqrt = s_s.inc(SC.activation(lnS[:, 4:5], lnS[:, 3:4], AF.Sqrt))
            VE.wait_ge(s_s.h, m_sqrt)
            VE.reciprocal(lnS[:, 8:9], lnS[:, 4:5])
            VE.tensor_mul(lnS[:, 9:10], lnS[:, 1:2], lnS[:, 8:9])
            if ln1bf_free[c % 2] > 0:
                VE.wait_ge(s_t.h, ln1bf_free[c % 2])
            if stop_after == "probe_lnx" and T == 0:
                VE.tensor_copy(colst2_dbg[:, c * 10:(c + 1) * 10], lnS[:, 0:10])
            v_ln[c] = s_v.inc(VE.tensor_scalar(ln1_bf[c % 2][:, :], xsl,
                                               lnS[:, 8:9], lnS[:, 9:10], ALU.mult, ALU.subtract))
            TE.wait_ge(s_v.h, v_ln[c])
            TE.wait_ge(s_v.h, pT_free)
            for j in range(8):
                TE.transpose(pT[:, j * 128:(j + 1) * 128],
                             ln1_bf[c % 2][:, j * 128:(j + 1) * 128], ident_bf[:, :])
            m_t = mark_last("tensor", s_t)
            VE.wait_ge(s_t.h, m_t)
            if stop_after == "probe_pT" and T == 0 and c == 0:
                mv = s_v.inc(VE.tensor_copy(xs_sb[:, 0:E], pT[:, :]))
                SY.wait_ge(s_v.h, mv)
                m2 = s_st.inc(SY.dma_start(out=out_d[0:128, :], in_=xs_sb[:, 0:E]), 16)
                VE.wait_ge(s_st.h, m2)
                mv = s_v.inc(VE.tensor_copy(xs_sb[:, 0:128], ident_bf[:, :]))
                SY.wait_ge(s_v.h, mv)
                m2 = s_st.inc(SY.dma_start(out=out_d[128:256, 0:128], in_=xs_sb[:, 0:128]), 16)
                VE.wait_ge(s_st.h, m2)
                mv = s_v.inc(VE.tensor_copy(xs_sb[:, 0:E], ln1_bf[0][:, :]))
                SY.wait_ge(s_v.h, mv)
                s_st.inc(SY.dma_start(out=out_d[256:384, :], in_=xs_sb[:, 0:E]), 16)
                SY.wait_ge(s_st.h, s_st.n)
                return finish([P1])
            VE.tensor_copy(
                ln1T_sb[:, :].rearrange("p (j s) -> p j s", j=8)[:, :, c * 128:(c + 1) * 128],
                pT[:, :].rearrange("p (j s) -> p j s", j=8))
            pT_free = mark_last("vector", s_v)
            ln1bf_free[c % 2] = m_t
            barrier() if paranoid else None
        v_lnT_done = s_v.n
        if stop_after == "probe_lnx" and T == 0:
            VE.wait_ge(s_v.h, s_v.n)
            mv = s_v.inc(VE.tensor_copy(xs_sb[:, 0:40], colst2_dbg[:, 0:40]))
            SY.wait_ge(s_v.h, mv)
            s_st.inc(SY.dma_start(out=out_d[0:128, 0:40], in_=xs_sb[:, 0:40]), 16)
            VE.wait_ge(s_st.h, s_st.n)
            mv = s_v.inc(VE.tensor_copy(xs_sb[:, 0:E], ln1_bf[1][:, :]))
            SY.wait_ge(s_v.h, mv)
            s_st.inc(SY.dma_start(out=out_d[128:256, :], in_=xs_sb[:, 0:E]), 16)
            SY.wait_ge(s_st.h, s_st.n)
            return finish([P1])
        if stop_after == "probe_V2":
            TE.wait_ge(s_v.h, v_lnT_done)
            for nh in range(2):
                for j_in in range(8):
                    TE.matmul(pO[:, nh * 512:(nh + 1) * 512],
                              ln1T_sb[:, j_in * TILE + 0 * 128: j_in * TILE + 1 * 128],
                              Wv_sb[:, j_in * E + nh * 512: j_in * E + (nh + 1) * 512],
                              start=(j_in == 0), stop=(j_in == 7))
            mV = mark_last("tensor", s_t)
            VE.wait_ge(s_t.h, mV)
            mv = s_v.inc(VE.tensor_copy(xs_sb[:, 0:E], pO[:, :]))
            SY.wait_ge(s_v.h, mv)
            s_st.inc(SY.dma_start(out=out_d[0:128, :], in_=xs_sb[:, 0:E]), 16)
            SY.wait_ge(s_st.h, s_st.n)
            return finish([P1])
        if stop_after == "p1a":
            sentinel(1, 9.0, (s_v, v_lnT_done))
            return finish([P1])
        if stop_after == "probe_ln1T":
            SY.wait_ge(s_v.h, v_lnT_done)
            mv = s_v.inc(VE.tensor_copy(xs_sb[:, 0:E], ln1T_sb[:, 0:E]))
            SY.wait_ge(s_v.h, mv)
            s_st.inc(SY.dma_start(out=out_d[0:128, :], in_=xs_sb[:, 0:E]), 16)
            mv2 = s_v.inc(VE.tensor_copy(xs_sb[:, 0:E], ln1_bf[1][:, :]))
            SY.wait_ge(s_v.h, mv2)
            s_st.inc(SY.dma_start(out=out_d[128:256, :], in_=xs_sb[:, 0:E]), 16)
            SY.wait_ge(s_st.h, s_st.n)
            return finish([P1])

        jlvl = {"j_mm": 0, "j_ve1": 1, "j_ve2": 2, "j_sc": 3}.get(stop_after, 99)
        s_scale = [None] * 8
        TE.wait_ge(s_v.h, v_lnT_done)
        if p1_te_prev > 0:
            VE.wait_ge(s_t.h, p1_te_prev)
        for j in range(8):
            jb = j % 2
            if j >= 1:
                TE.wait_ge(s_v.h, s_qs_m[(j - 1) % 2])
            elif pAB_free is not None:
                TE.wait_ge(s_s.h, pAB_free)
            for j_in in range(8):
                TE.matmul(pA[:, :], Wk_sb[:, j_in * E + j * 128: j_in * E + j * 128 + 128],
                          ln1T_sb[:, j_in * TILE:(j_in + 1) * TILE],
                          start=(j_in == 0), stop=(j_in == 7))
            m_xk = mark_last("tensor", s_t)
            for j_in in range(8):
                TE.matmul(pB[:, :], Wq_sb[:, j_in * E + j * 128: j_in * E + j * 128 + 128],
                          ln1T_sb[:, j_in * TILE:(j_in + 1) * TILE],
                          start=(j_in == 0), stop=(j_in == 7))
            m_q = mark_last("tensor", s_t)
            SC.wait_ge(s_t.h, m_xk)
            if v_j_m[jb] is not None:
                SC.wait_ge(s_v.h, v_j_m[jb])
            SC.activation(kT_j[jb][:, :], pA[:, :], AF.Sigmoid,
                          bias=delta_neg[:, j:j + 1], scale=-1.0)
            s_kT_m[jb] = mark_last("scalar", s_s)
            SC.wait_ge(s_t.h, m_q)
            SC.activation(qs_j[jb][:, :], pB[:, :], AF.Sigmoid)
            sq_sc = mark_last("scalar", s_s)
            VE.wait_ge(s_s.h, sq_sc)
            VE.tensor_mul(qs_j[jb][:, :], qs_j[jb][:, :], pB[:, :])
            s_qs_m[jb] = mark_last("vector", s_v)
            if jlvl < 1:
                v_j_m[jb] = s_v.n if v_j_m[jb] is None else v_j_m[jb]
                s_scale[j] = s_v.n
                scale_free[jb] = s_scale[j]
                continue
            VE.wait_ge(s_s.h, s_kT_m[jb])
            VE.tensor_scalar(f_j[:, :], kT_j[jb][:, :], -1.0, 1.0, ALU.mult, ALU.add)
            if scale_free[jb] is not None:
                VE.wait_ge(s_s.h, scale_free[jb])
            VE.tensor_tensor_scan(Lraw_j[jb][:, :], f_j[:, :], f_j[:, :], 1.0, ALU.mult, ALU.bypass)
            VE.tensor_tensor_scan(rrev_j[jb][:, 0:TILE][:, ::-1], f_j[:, ::-1], f_j[:, ::-1],
                                  1.0, ALU.mult, ALU.bypass)
            if jlvl < 2:
                v_j_m[jb] = mark_last("vector", s_v)
                s_scale[j] = s_v.n
                scale_free[jb] = None
                continue
            ecols = Lraw_j[jb][:, 127::128]
            VE.reciprocal(colst[:, j * 8: j * 8 + 4], ecols)
            VE.reciprocal(colst[:, j * 8 + 4: j * 8 + 7], rrev_j[jb][:, 128::128][:, 0:3])
            for c in range(NCT):
                rr = colst[:, j * 8 + 4 + c: j * 8 + 5 + c] if c < 3 else ones_col[:, 0:1]
                VE.scalar_tensor_tensor(khat_bf[:, j * TILE + c * 128: j * TILE + (c + 1) * 128],
                                        rrev_j[jb][:, c * 128 + 1:(c + 1) * 128 + 1],
                                        rr, kT_j[jb][:, c * 128:(c + 1) * 128],
                                        ALU.mult, ALU.mult)
                bl = colst[:, j * 8 + c - 1: j * 8 + c] if c > 0 else ones_col[:, 0:1]
                VE.scalar_tensor_tensor(Dcols[:, c * 8 + j: c * 8 + j + 1],
                                        Lraw_j[jb][:, (c + 1) * 128 - 1:(c + 1) * 128],
                                        1.0, bl, ALU.mult, ALU.mult)
                blv = Lraw_j[jb][:, c * 128 - 1: c * 128] if c > 0 else ones_col[:, 0:1]
                VE.scalar_tensor_tensor(kscale[:, c * 8 + j: c * 8 + j + 1],
                                        blv, 1.0, colst[:, j * 8 + c: j * 8 + c + 1],
                                        ALU.mult, ALU.mult)
            if jlvl < 3:
                v_j_m[jb] = mark_last("vector", s_v)
                s_scale[j] = s_v.n
                scale_free[jb] = None
                continue
            if T >= 1 and j == 0:
                VE.wait_ge(s_st.h, st_p1[T - 1])
            for c in range(NCT):
                sl = slice(j * TILE + c * 128, j * TILE + (c + 1) * 128)
                lsl = slice(c * 128, (c + 1) * 128)
                bl = colst[:, j * 8 + c - 1: j * 8 + c] if c > 0 else ones_col[:, 0:1]
                el = kscale[:, c * 8 + j: c * 8 + j + 1]
                VE.scalar_tensor_tensor(qt_pf[0][0:64, sl], qs_j[jb][0:64, lsl],
                                        bl[0:64, :], Lraw_j[jb][0:64, lsl], ALU.mult, ALU.mult)
                VE.scalar_tensor_tensor(qt_pf[1][64:128, sl], qs_j[jb][64:128, lsl],
                                        bl[64:128, :], Lraw_j[jb][64:128, lsl], ALU.mult, ALU.mult)
                VE.tensor_scalar_mul(kt_pf[0][0:64, sl], khat_bf[0:64, sl], el[0:64, :])
                VE.tensor_scalar_mul(kt_pf[1][64:128, sl], khat_bf[64:128, sl], el[64:128, :])
            v_j_m[jb] = mark_last("vector", s_v)
            s_scale[j] = v_j_m[jb]
            scale_free[jb] = None
            barrier() if paranoid else None
        pAB_free = s_qs_m[1]
        if stop_after == "probe_wv":
            SY.wait_ge(s_s.h, s_s.n)
            for half in range(4):
                VE.wait_ge(s_st.h, s_st.n)
                mv = s_v.inc(VE.tensor_copy(xs_sb[:, 0:E], Wv_sb[:, half * E:(half + 1) * E]))
                SY.wait_ge(s_v.h, mv)
                s_st.inc(SY.dma_start(out=out_d[half * 128:(half + 1) * 128, :], in_=xs_sb[:, 0:E]), 16)
            SY.wait_ge(s_st.h, s_st.n)
            return finish([P1])
        if stop_after in ("probe_j", "j_mm", "j_ve1", "j_ve2", "j_sc"):
            SY.wait_ge(s_s.h, s_s.n)
            SY.wait_ge(s_v.h, s_v.n)
            dumps = [qs_j[0][:, :], kT_j[0][:, :], Lraw_j[0][:, :], tmpq_j[0][:, :]]
            for i, dsrc in enumerate(dumps):
                VE.wait_ge(s_st.h, s_st.n)
                mv = s_v.inc(VE.tensor_copy(xs_sb[:, i * TILE:(i + 1) * TILE], dsrc))
                SY.wait_ge(s_v.h, mv)
                s_st.inc(SY.dma_start(out=out_d[i * 128:(i + 1) * 128, 0:TILE],
                                      in_=xs_sb[:, i * TILE:(i + 1) * TILE]), 16)
            VE.wait_ge(s_st.h, s_st.n)
            mv = s_v.inc(VE.tensor_copy(xs_sb[:, 0:TILE], qt_pf[0][:, 7 * TILE:8 * TILE]))
            SY.wait_ge(s_v.h, mv)
            s_st.inc(SY.dma_start(out=out_d[0:128, TILE:2 * TILE], in_=xs_sb[:, 0:TILE]), 16)
            VE.wait_ge(s_st.h, s_st.n)
            mv = s_v.inc(VE.tensor_copy(xs_sb[:, 0:64], colst[:, :]))
            SY.wait_ge(s_v.h, mv)
            s_st.inc(SY.dma_start(out=out_d[128:256, TILE:TILE + 64], in_=xs_sb[:, 0:64]), 16)
            SY.wait_ge(s_st.h, s_st.n)
            return finish([P1])
        if stop_after == "p1b":
            VE.wait_ge(s_v.h, s_scale[7])
            sentinel(1, 9.0)
            return finish([P1])

        if stop_after == "sent" and T == 0:
            sentinel(2, 7.0)
        TE.wait_ge(s_s.h, s_kT_m[0])
        TE.wait_ge(s_s.h, s_kT_m[1])
        for j_in in range(8):
            TE.matmul(pA[0:64, :], G1_sb[:, j_in * D:(j_in + 1) * D],
                      ln1T_sb[:, j_in * TILE:(j_in + 1) * TILE],
                      start=(j_in == 0), stop=(j_in == 7))
        m_g1 = mark_last("tensor", s_t)
        VE.wait_ge(s_t.h, m_g1)
        v_ghT = s_v.inc(VE.tensor_copy(ghT_sb[:, :], pA[0:64, :]))
        barrier() if paranoid else None
        if stop_after == "p1c":
            VE.wait_ge(s_v.h, v_ghT)
            sentinel(1, 9.0)
            return finish([P1])

        if stop_after == "sent" and T == 0:
            sentinel(3, 7.0)
        glvl = {"g0": 0, "g1": 1, "g2": 2, "g3": 3}.get(stop_after, 99)
        v_O_evac = [None] * NCT
        s_gate = [None] * NCT
        for c in range(NCT):
            g = T * NCT + c
            cb2 = c % 2
            TE.wait_ge(s_v.h, pO_free)
            TE.wait_ge(s_v.h, s_scale[7])
            for nh in range(2):
                for j_in in range(8):
                    TE.matmul(pO[:, nh * 512:(nh + 1) * 512],
                              ln1T_sb[:, j_in * TILE + c * 128: j_in * TILE + (c + 1) * 128],
                              Wv_sb[:, j_in * E + nh * 512: j_in * E + (nh + 1) * 512],
                              start=(j_in == 0), stop=(j_in == 7))
            m_V = mark_last("tensor", s_t)
            VE.wait_ge(s_t.h, m_V)
            if Vtok_free[cb2] > 0:
                VE.wait_ge(s_t.h, Vtok_free[cb2])
            VE.tensor_copy(V_tok[cb2][:, :], pO[:, :])
            pO_free = mark_last("vector", s_v)
            v_V = pO_free
            barrier() if paranoid else None
            TE.wait_ge(s_v.h, v_ghT)
            if c == 0:
                TE.wait_ge(s_v.h, s_qs_m[1])
            if c > 0:
                TE.wait_ge(s_s.h, s_gate[c - 1])
            for nh in range(2):
                TE.matmul(pB[:, :] if nh == 0 else pA[:, :],
                          ghT_sb[:, c * 128:(c + 1) * 128],
                          G2_sb[:, nh * 512:(nh + 1) * 512], start=True, stop=True)
            m_gt = mark_last("tensor", s_t)
            SC.wait_ge(s_t.h, m_gt)
            if sgst_free[cb2] > 0:
                SC.wait_ge(s_st.h, sgst_free[cb2])
            SC.activation(sg_scr[cb2][:, 0:512], pB[:, :], AF.Sigmoid)
            SC.activation(sg_scr[cb2][:, 512:1024], pA[:, :], AF.Sigmoid)
            s_gate[c] = mark_last("scalar", s_s)
            SY.wait_ge(s_s.h, s_gate[c])
            sgst_free[cb2] = s_st.inc(SY.dma_start(
                out=sgate_d[(T * NCT + c) * C:(T * NCT + c + 1) * C, :],
                in_=sg_scr[cb2][:, :]), 16)

            if glvl < 1:
                continue
            VE.tensor_copy(Gres_sb[:, g * 8:(g + 1) * 8], Gamma_sb[:, :])
            v_Sbf = s_v.inc(VE.tensor_copy(S_bf[:, :], S_sb[:, :]))

            TE.wait_ge(s_v.h, pT_free)
            for j in range(8):
                TE.transpose(pT[:, j * 128:(j + 1) * 128],
                             khat_bf[:, j * TILE + c * 128: j * TILE + (c + 1) * 128],
                             ident_bf[:, :])
            m_kh = mark_last("tensor", s_t)
            VE.wait_ge(s_t.h, m_kh)
            VE.tensor_copy(khat_tok[:, :], pT[:, :])
            pT_free = mark_last("vector", s_v)
            v_khat = pT_free
            barrier() if paranoid else None

            if glvl < 2:
                continue
            v_P = [None] * 4
            for wv in range(4):
                pw = pP[wv % 2]
                if wv >= 2:
                    TE.wait_ge(s_v.h, v_P[wv - 2])
                elif last_P_evacs is not None:
                    TE.wait_ge(s_v.h, last_P_evacs[wv % 2])
                for hh in range(4):
                    h = wv * 4 + hh
                    j, par = h // 2, h % 2
                    TE.matmul(pw[:, hh * 128:(hh + 1) * 128],
                              kt_pf[par][:, j * TILE + c * 128: j * TILE + (c + 1) * 128],
                              qt_pf[par][:, j * TILE + c * 128: j * TILE + (c + 1) * 128],
                              start=True, stop=True)
                m_P = mark_last("tensor", s_t)
                VE.wait_ge(s_t.h, m_P)
                VE.tensor_tensor(P_bf[:, wv * 512:(wv + 1) * 512], pw[:, :],
                                 mask_st[:, :].unsqueeze(1).broadcast_to([128, 4, 128]),
                                 ALU.mult)
                v_P[wv] = mark_last("vector", s_v)
            last_P_evacs = [v_P[2], v_P[3]]
            barrier() if paranoid else None
            if stop_after == "probe_P" and T == 0 and c == 0:
                VE.wait_ge(s_v.h, v_P[3])
                for half in range(2):
                    mv = s_v.inc(VE.tensor_copy(xs_sb[:, 0:E], P_bf[:, half * 1024:(half + 1) * 1024]))
                    SY.wait_ge(s_v.h, mv)
                    m2 = s_st.inc(SY.dma_start(out=out_d[half * 128:(half + 1) * 128, :],
                                               in_=xs_sb[:, 0:E]), 16)
                    VE.wait_ge(s_st.h, m2)
                mv = s_v.inc(VE.tensor_copy(xs_sb[:, 0:E], V_tok[cb2][:, :]))
                SY.wait_ge(s_v.h, mv)
                s_st.inc(SY.dma_start(out=out_d[256:384, :], in_=xs_sb[:, 0:E]), 16)
                mv = s_v.inc(VE.tensor_copy(xs_sb[:, 0:E], qt_pf[0][:, 0:E].bitcast(F32).bitcast(BF16)[:, 0:1024] if False else qt_pf[0][:, 0:1024]))
                SY.wait_ge(s_v.h, mv)
                s_st.inc(SY.dma_start(out=out_d[384:512, :], in_=xs_sb[:, 0:E]), 16)
                SY.wait_ge(s_st.h, s_st.n)
                return finish([P1])

            if glvl < 3:
                continue
            TE.wait_ge(s_v.h, pO_free)
            TE.wait_ge(s_v.h, v_Sbf)
            TE.wait_ge(s_v.h, v_P[3])
            for h in range(H):
                j, par = h // 2, h % 2
                TE.matmul(pO[:, h * D:(h + 1) * D],
                          P_bf[:, h * 128:(h + 1) * 128],
                          V_tok[cb2][:, h * D:(h + 1) * D],
                          start=True, stop=False)
                TE.matmul(pO[:, h * D:(h + 1) * D],
                          qt_pf[par][:, j * TILE + c * 128: j * TILE + (c + 1) * 128],
                          S_bf[:, j * D:(j + 1) * D],
                          start=False, stop=True)
            m_O = mark_last("tensor", s_t)
            VE.wait_ge(s_t.h, m_O)
            if ost_free[cb2] > 0:
                VE.wait_ge(s_st.h, ost_free[cb2])
            VE.tensor_copy(o_scr[cb2][:, :], pO[:, :])
            pO_free = mark_last("vector", s_v)
            v_O_evac[c] = pO_free
            barrier() if paranoid else None
            SY.wait_ge(s_v.h, v_O_evac[c])
            ost_free[cb2] = s_st.inc(SY.dma_start(
                out=olocal_d[(T * NCT + c) * C:(T * NCT + c + 1) * C, :],
                in_=o_scr[cb2][:, :]), 16)

            if glvl < 4:
                continue
            TE.wait_ge(s_v.h, v_khat)
            for r in range(2):
                TE.wait_ge(s_v.h, pS_free)
                for jb2 in range(4):
                    j = 4 * r + jb2
                    for par in range(2):
                        h = 2 * j + par
                        po = par * 64
                        TE.matmul(pS[po:po + 64, jb2 * 128: jb2 * 128 + D],
                                  khat_tok[:, h * D:(h + 1) * D],
                                  V_tok[cb2][:, h * D:(h + 1) * D],
                                  start=True, stop=True,
                                  tile_position=(0, po))
                m_Sg = mark_last("tensor", s_t)
                VE.wait_ge(s_t.h, m_Sg)
                for jb2 in range(4):
                    j = 4 * r + jb2
                    VE.scalar_tensor_tensor(S_sb[:, j * D:(j + 1) * D], S_sb[:, j * D:(j + 1) * D],
                                            Dcols[:, c * 8 + j: c * 8 + j + 1],
                                            pS[:, jb2 * 128: jb2 * 128 + D],
                                            ALU.mult, ALU.add)
                    VE.tensor_scalar_mul(Gamma_sb[:, j:j + 1], Gamma_sb[:, j:j + 1],
                                         Dcols[:, c * 8 + j: c * 8 + j + 1])
                pS_free = mark_last("vector", s_v)
            Vtok_free[cb2] = m_Sg
            barrier() if paranoid else None

        x_free = v_ln[NCT - 1]
        pAB_free = s_gate[NCT - 1]
        p1_te_prev = s_t.n
        SY.wait_ge(s_v.h, s_scale[7])
        s_st.inc(SY.dma_start(out=qte_d[:, T * 8 * TILE:(T + 1) * 8 * TILE], in_=qt_pf[0][:, :]), 16)
        if glvl < 99 and T == ntile - 1:
            sentinel(1, 9.0, (s_v, s_v.n))
            VE.wait_ge(s_s.h, s_s.n)
            VE.wait_ge(s_t.h, s_t.n)
            sentinel(2, 9.0)
            return finish([P1])
        elif glvl < 99:
            pass
        st_p1[T] = s_st.inc(SY.dma_start(out=qto_d[:, T * 8 * TILE:(T + 1) * 8 * TILE], in_=qt_pf[1][:, :]), 16)

    if stop_after == "p1":
        sentinel(1, 9.0, (s_v, s_v.n))
        VE.wait_ge(s_s.h, s_s.n)
        VE.wait_ge(s_t.h, s_t.n)
        sentinel(2, 9.0)
        return finish([P1])

    if stop_after == "sent":
        sentinel(0, 17.0)
    if use_ag:
        SY.wait_ge(s_v.h, s_v.n)
        s_ld.inc(SY.dma_start(out=cc_in_d[:, 0:512], in_=S_sb[:, :]), 16)
        ag_in = s_ld.inc(SY.dma_start(out=cc_in_d[:, 512:520], in_=Gamma_sb[:, :]), 16)
        GP.wait_ge(s_ld.h, ag_in)
        cc_mark = s_cc.inc(GP.collective_compute(
            "AllGather", ALU.bypass,
            replica_groups=[[0, 1, 2, 3], [4, 5, 6, 7]],
            ins=[cc_in_d[:, :].opt()],
            outs=[cc_out_d[:, :].opt()],
        ))

    barrier()
    P1.close()

    P2 = Scope(nc)
    agland = P2.sbuf("agland", [128, 4 * 520], F32)
    Sinit_sb = P2.sbuf("Sinit_sb", [128, 512], F32)
    Srun_sb = P2.sbuf("Srun_sb", [128, 512], F32)
    St_sb = [P2.sbuf(f"St_sb{i}", [128, 512], F32) for i in range(2)]
    St_bf = [P2.sbuf(f"St_bf{i}", [128, 512], BF16) for i in range(2)]
    x2_sb = P2.sbuf("x2", [128, 4 * E], F32)
    ot_sb = P2.sbuf("ot", [128, 4 * E], F32)
    sg2_sb = P2.sbuf("sg2", [128, 4 * E], F32)
    qt2_sb = [P2.sbuf("qt2e", [128, 8 * TILE], BF16), P2.sbuf("qt2o", [128, 8 * TILE], BF16)]
    xp_sb = P2.sbuf("xp", [128, 4 * E], F32)
    on_bf = [P2.sbuf(f"on_bf{i}", [128, E], BF16) for i in range(2)]
    onT_sb = [P2.sbuf(f"onT_sb{i}", [128, 8 * C], BF16) for i in range(2)]
    o2_sb = [P2.sbuf(f"o2_sb{i}", [128, E], F32) for i in range(2)]
    osq_sb = P2.sbuf("osq_sb", [128, E], F32)
    ln2_bf = [P2.sbuf(f"ln2_bf{i}", [128, E], BF16) for i in range(2)]
    ln2T_sb = P2.sbuf("ln2T_sb", [128, 8 * TILE], BF16)
    probs_sb = P2.sbuf("probs_sb", [128, 4 * 16], F32)
    mx8_sb = P2.sbuf("mx8_sb", [128, 8], F32)
    rt_sc = P2.sbuf("rt_sc", [128, 8], F32)
    comb_sb = P2.sbuf("comb_sb", [128, 16], F32)
    combT_tile = P2.sbuf("combT_tile", [16, 4 * C], F32)
    rms_sc = P2.sbuf("rms_sc", [128, 96], F32)

    if aux and use_ag:
        SY.wait_ge(s_gp.h, gp_wo)
        s_st.inc(SY.dma_start(out=aux_d[1024:1152, 0:4096], in_=Wo_sb[:, :].bitcast(F32)), 16)
        SY.wait_ge(s_gp.h, gp_gateT)
        s_st.inc(SY.dma_start(out=aux_d[1280:1408, 0:64], in_=gateT_sb[:, :].bitcast(F32)), 16)

    def tile_loads(T):
        s_ld.inc(SY.dma_start(out=x2_sb.rearrange("p (c e) -> p c e", c=4),
                     in_=x_in[T * TILE:(T + 1) * TILE, :].rearrange("(c p) e -> p c e", p=128)), 16)
        s_ld.inc(SY.dma_start(out=ot_sb.rearrange("p (c e) -> p c e", c=4),
                     in_=olocal_d[T * TILE:(T + 1) * TILE, :].rearrange("(c p) e -> p c e", p=128)), 16)
        s_ld.inc(SY.dma_start(out=sg2_sb.rearrange("p (c e) -> p c e", c=4),
                     in_=sgate_d[T * TILE:(T + 1) * TILE, :].rearrange("(c p) e -> p c e", p=128)), 16)
        s_ld.inc(SY.dma_start(out=qt2_sb[0][:, :], in_=qte_d[:, T * 8 * TILE:(T + 1) * 8 * TILE]), 16)
        return s_ld.inc(SY.dma_start(out=qt2_sb[1][:, :],
                                     in_=qto_d[:, T * 8 * TILE:(T + 1) * 8 * TILE]), 16)

    ld2_t0 = tile_loads(0)

    if use_ag:
        SY.wait_ge(s_cc.h, cc_mark)
        ag_out = None
        for j in range(4):
            ag_out = s_ld.inc(SY.dma_start(out=agland[:, j * 520:(j + 1) * 520],
                                           in_=cc_out_d[j * 128:(j + 1) * 128, :]), 16)
        VE.wait_ge(s_ld.h, ag_out)
        VE.memset(Srun_sb[:, :], 0.0)
        VE.memset(Sinit_sb[:, :], 0.0)
        for j in range(4):
            Lj = agland[:, j * 520: j * 520 + 512]
            Gj = agland[:, j * 520 + 512: j * 520 + 520]
            VE.scalar_tensor_tensor(Sinit_sb[:, :], Srun_sb[:, :], emask_sb[:, j:j + 1],
                                    Sinit_sb[:, :], ALU.mult, ALU.add)
            for b in range(8):
                VE.scalar_tensor_tensor(Srun_sb[:, b * D:(b + 1) * D], Srun_sb[:, b * D:(b + 1) * D],
                                        Gj[:, b:b + 1], Lj[:, b * D:(b + 1) * D], ALU.mult, ALU.add)
        v_sinit = mark_last("vector", s_v)
    else:
        VE.memset(Sinit_sb[:, :], 0.0)
        v_sinit = mark_last("vector", s_v)

    if aux and use_ag:
        SY.wait_ge(s_v.h, v_sinit)
        s_st.inc(SY.dma_start(out=aux_d[0:128, 0:512], in_=Sinit_sb[:, :]), 16)
        s_st.inc(SY.dma_start(out=aux_d[0:128, 512:516], in_=emask_sb[:, :]), 16)
        s_st.inc(SY.dma_start(out=aux_d[0:128, 516:2596], in_=agland[:, :]), 16)

    if stop_after == "sent":
        sentinel(1, 17.0)
    st_p2 = [None] * ntile
    buf2_free = v_sinit
    pT_free2 = v_const
    pS_free2 = v_const
    pAB_free2 = 0
    corr_par_te = [0, 0]
    corr_par_ve = [0, 0]
    onbf_free = [0, 0]
    ln2bf_free = [0, 0]

    TE.wait_ge(s_gp.h, gp_wo)
    TE.wait_ge(s_gp.h, gp_gateT)
    for T in range(ntile):
        if T == 0:
            ld2 = ld2_t0
        else:
            SY.wait_ge(s_v.h, buf2_free)
            ld2 = tile_loads(T)

        v2_ln2T = [None] * NCT
        v2_xp = [None] * NCT
        m_lg = [None] * NCT
        m_ssq16_arr = [None] * NCT
        m_rms_arr = [None] * NCT
        v_on_arr = [None] * NCT
        m_ssq_arr = [None] * NCT
        m_var_arr = [None] * NCT
        m_sqrt_arr = [None] * NCT
        v_ln2_arr = [None] * NCT

        def front(c):
            nonlocal corr_par_te, corr_par_ve
            p2 = c % 2
            g = T * NCT + c
            if c == 0:
                VE.wait_ge(s_ld.h, ld2)
            if corr_par_te[p2] > 0:
                VE.wait_ge(s_t.h, corr_par_te[p2])
            if c >= 2 and m_rms_arr[c - 2] is not None:
                VE.wait_ge(s_s.h, m_rms_arr[c - 2])
            for blk in range(8):
                VE.tensor_scalar_mul(St_sb[p2][:, blk * D:(blk + 1) * D],
                                     Sinit_sb[:, blk * D:(blk + 1) * D],
                                     Gres_sb[:, g * 8 + blk: g * 8 + blk + 1])
            v_St = s_v.inc(VE.tensor_copy(St_bf[p2][:, :], St_sb[p2][:, :]))
            TE.wait_ge(s_v.h, v_St)
            if c == 0:
                TE.wait_ge(s_ld.h, ld2)
            free_m = corr_par_ve[p2]
            for r in range(2):
                if free_m > 0:
                    TE.wait_ge(s_v.h, free_m)
                for i in range(8):
                    h = 8 * r + i
                    j, par = h // 2, h % 2
                    if p2 == 0:
                        dst = pO[:, i * 128: i * 128 + D]
                    else:
                        dst = pP[i // 4][:, (i % 4) * 128: (i % 4) * 128 + D]
                    TE.matmul(dst,
                              qt2_sb[par][:, j * TILE + c * 128: j * TILE + (c + 1) * 128],
                              St_bf[p2][:, j * D:(j + 1) * D],
                              start=True, stop=True)
                m_corr = mark_last("tensor", s_t)
                VE.wait_ge(s_t.h, m_corr)
                if p2 == 0:
                    VE.tensor_add(o2_sb[p2][:, :].rearrange("p (h d) -> p h d", h=16)[:, 8 * r:8 * r + 8, :],
                                  ot_sb[:, c * E:(c + 1) * E].rearrange("p (h d) -> p h d", h=16)[:, 8 * r:8 * r + 8, :],
                                  pO[:, :].rearrange("p (i s) -> p i s", i=8)[:, :, 0:D])
                else:
                    for half in range(2):
                        hsl = slice(8 * r + 4 * half, 8 * r + 4 * half + 4)
                        VE.tensor_add(o2_sb[p2][:, :].rearrange("p (h d) -> p h d", h=16)[:, hsl, :],
                                      ot_sb[:, c * E:(c + 1) * E].rearrange("p (h d) -> p h d", h=16)[:, hsl, :],
                                      pP[half][:, :].rearrange("p (i s) -> p i s", i=4)[:, :, 0:D])
                free_m = mark_last("vector", s_v)
            corr_par_te[p2] = m_corr
            corr_par_ve[p2] = free_m
            VE.tensor_mul(o2_sb[p2][:, :], o2_sb[p2][:, :], sg2_sb[:, c * E:(c + 1) * E])
            VE.tensor_mul(osq_sb[:, :], o2_sb[p2][:, :], o2_sb[p2][:, :])
            m_ssq16_arr[c] = s_v.inc(VE.reduce_sum(rms_sc[:, 48 * p2: 48 * p2 + 16],
                                                   osq_sb[:, :].rearrange("p (h d) -> p h d", h=16),
                                                   axis=AX.X))

        def b_rms(c):
            SC.wait_ge(s_v.h, m_ssq16_arr[c])
            rb = 48 * (c % 2)
            m_rms_arr[c] = s_s.inc(SC.activation(rms_sc[:, rb + 16: rb + 32], rms_sc[:, rb: rb + 16],
                                                 AF.Sqrt, scale=1.0 / D, bias=eps_col[:, 0:1]))

        def b_on(c):
            p2 = c % 2
            rb = 48 * p2
            VE.wait_ge(s_s.h, m_rms_arr[c])
            VE.reciprocal(rms_sc[:, rb + 32: rb + 48], rms_sc[:, rb + 16: rb + 32])
            if isinstance(onbf_free[p2], int) and onbf_free[p2] > 0:
                VE.wait_ge(s_t.h, onbf_free[p2])
            for h in range(H):
                VE.scalar_tensor_tensor(on_bf[p2][:, h * D:(h + 1) * D],
                                        o2_sb[p2][:, h * D:(h + 1) * D], rms_sc[:, rb + 32 + h: rb + 33 + h],
                                        nwbc_sb[:, h * D:(h + 1) * D], ALU.mult, ALU.mult)
            v_on_arr[c] = mark_last("vector", s_v)

        def b_onT(c):
            nonlocal pT_free2
            p2 = c % 2
            TE.wait_ge(s_v.h, v_on_arr[c])
            TE.wait_ge(s_v.h, pT_free2)
            for j in range(8):
                TE.transpose(pT[:, j * 128:(j + 1) * 128],
                             on_bf[p2][:, j * 128:(j + 1) * 128], ident_bf[:, :])
            m_onT = mark_last("tensor", s_t)
            VE.wait_ge(s_t.h, m_onT)
            VE.tensor_copy(onT_sb[p2][:, :], pT[:, :])
            pT_free2 = mark_last("vector", s_v)
            onbf_free[p2] = m_onT

        def b_wo(c):
            nonlocal pAB_free2
            p2 = c % 2
            TE.wait_ge(s_v.h, pT_free2)
            if pAB_free2 > 0:
                TE.wait_ge(s_v.h, pAB_free2)
            for nh, bank in ((0, pA), (1, pB)):
                for j in range(8):
                    TE.matmul(bank[:, :],
                              onT_sb[p2][:, j * 128:(j + 1) * 128],
                              Wo_sb[:, j * E + nh * 512: j * E + (nh + 1) * 512],
                              start=(j == 0), stop=(j == 7))
            m_wo = mark_last("tensor", s_t)
            VE.wait_ge(s_t.h, m_wo)
            if T >= 1 and c == 0:
                VE.wait_ge(s_st.h, st_p2[T - 1])
            VE.tensor_add(xp_sb[:, c * E: c * E + 512], x2_sb[:, c * E: c * E + 512], pA[:, :])
            VE.tensor_add(xp_sb[:, c * E + 512:(c + 1) * E], x2_sb[:, c * E + 512:(c + 1) * E], pB[:, :])
            pAB_free2 = mark_last("vector", s_v)
            v2_xp[c] = pAB_free2

        def b_ln2stats(c):
            lb = 16 * (c % 2)
            xsl = xp_sb[:, c * E:(c + 1) * E]
            SC.wait_ge(s_v.h, v2_xp[c])
            SC.activation(sq_scr[:, :], xsl, AF.Square, accum_out=lnS[:, lb + 2: lb + 3])
            m_ssq_arr[c] = mark_last("scalar", s_s)
            VE.reduce_sum(lnS[:, lb + 0: lb + 1], xsl, axis=AX.X)
            VE.tensor_scalar_mul(lnS[:, lb + 1: lb + 2], lnS[:, lb + 0: lb + 1], 1.0 / E)

        def b_ln2var(c):
            lb = 16 * (c % 2)
            VE.wait_ge(s_s.h, m_ssq_arr[c])
            VE.tensor_mul(lnS[:, lb + 5: lb + 6], lnS[:, lb + 1: lb + 2], lnS[:, lb + 1: lb + 2])
            VE.tensor_scalar(lnS[:, lb + 6: lb + 7], lnS[:, lb + 2: lb + 3], 1.0 / E, None, ALU.mult)
            VE.tensor_sub(lnS[:, lb + 7: lb + 8], lnS[:, lb + 6: lb + 7], lnS[:, lb + 5: lb + 6])
            m_var_arr[c] = s_v.inc(VE.tensor_scalar(lnS[:, lb + 3: lb + 4], lnS[:, lb + 7: lb + 8],
                                                    LN_EPS, None, ALU.add))

        def b_ln2sqrt(c):
            lb = 16 * (c % 2)
            SC.wait_ge(s_v.h, m_var_arr[c])
            m_sqrt_arr[c] = s_s.inc(SC.activation(lnS[:, lb + 4: lb + 5], lnS[:, lb + 3: lb + 4], AF.Sqrt))

        def b_ln2scale(c):
            p2 = c % 2
            lb = 16 * p2
            xsl = xp_sb[:, c * E:(c + 1) * E]
            VE.wait_ge(s_s.h, m_sqrt_arr[c])
            VE.reciprocal(lnS[:, lb + 8: lb + 9], lnS[:, lb + 4: lb + 5])
            VE.tensor_mul(lnS[:, lb + 9: lb + 10], lnS[:, lb + 1: lb + 2], lnS[:, lb + 8: lb + 9])
            if isinstance(ln2bf_free[p2], int) and ln2bf_free[p2] > 0:
                VE.wait_ge(s_t.h, ln2bf_free[p2])
            v_ln2_arr[c] = s_v.inc(VE.tensor_scalar(ln2_bf[p2][:, :], xsl,
                                                    lnS[:, lb + 8: lb + 9], lnS[:, lb + 9: lb + 10],
                                                    ALU.mult, ALU.subtract))

        def b_ln2T(c):
            nonlocal pT_free2
            p2 = c % 2
            TE.wait_ge(s_v.h, v_ln2_arr[c])
            TE.wait_ge(s_v.h, pT_free2)
            for j in range(8):
                TE.transpose(pT[:, j * 128:(j + 1) * 128],
                             ln2_bf[p2][:, j * 128:(j + 1) * 128], ident_bf[:, :])
            m_l2T = mark_last("tensor", s_t)
            VE.wait_ge(s_t.h, m_l2T)
            if T >= 1 and c == 0:
                VE.wait_ge(s_st.h, st_p2[T - 1])
            VE.tensor_copy(
                ln2T_sb[:, :].rearrange("p (j s) -> p j s", j=8)[:, :, c * 128:(c + 1) * 128],
                pT[:, :].rearrange("p (j s) -> p j s", j=8))
            pT_free2 = mark_last("vector", s_v)
            ln2bf_free[p2] = m_l2T
            v2_ln2T[c] = pT_free2

        def b_probs(c):
            nonlocal pS_free2
            TE.wait_ge(s_v.h, v2_ln2T[c])
            if c == 0:
                TE.wait_ge(s_v.h, pS_free2)
            for j in range(8):
                TE.matmul(pS[:, c * 128: c * 128 + 16],
                          ln2T_sb[:, j * TILE + c * 128: j * TILE + (c + 1) * 128],
                          gateT_sb[:, j * NEXP:(j + 1) * NEXP],
                          start=(j == 0), stop=(j == 7))
            m_lg[c] = mark_last("tensor", s_t)

        for pair in range(NCT // 2):
            c0, c1 = 2 * pair, 2 * pair + 1
            front(c0)
            front(c1)
            b_rms(c0); b_rms(c1)
            b_on(c0); b_on(c1)
            b_onT(c0); b_onT(c1)
            b_wo(c0); b_wo(c1)
            b_ln2stats(c0); b_ln2stats(c1)
            b_ln2var(c0); b_ln2var(c1)
            b_ln2sqrt(c0); b_ln2sqrt(c1)
            b_ln2scale(c0); b_ln2scale(c1)
            b_ln2T(c0); b_ln2T(c1)
            b_probs(c0); b_probs(c1)

        VE.wait_ge(s_t.h, m_lg[NCT - 1])
        VE.reduce_max(rt_sc[:, 0:4], pS[:, 0:512].rearrange("p (c k) -> p c k", c=4)[:, :, 0:16],
                      axis=AX.X, negate=True)
        for c in range(NCT):
            VE.tensor_scalar(probs_sb[:, c * 16:(c + 1) * 16], pS[:, c * 128: c * 128 + 16],
                             rt_sc[:, c:c + 1], None, ALU.add)
        m_pre = mark_last("vector", s_v)
        SC.wait_ge(s_v.h, m_pre)
        m_exp = s_s.inc(SC.activation(probs_sb[:, :], probs_sb[:, :], AF.Exp))
        VE.wait_ge(s_s.h, m_exp)
        if T >= 1:
            VE.wait_ge(s_st.h, st_p2[T - 1])
        for c in range(NCT):
            psl = probs_sb[:, c * 16:(c + 1) * 16]
            VE.max(out=mx8_sb[:, :], in_=psl)
            VE.drain()
            VE.tensor_scalar(comb_sb[:, :], psl, mx8_sb[:, 3:4], None, ALU.is_ge)
            VE.tensor_mul(comb_sb[:, :], comb_sb[:, :], psl)
            VE.reduce_sum(rt_sc[:, 4:5], comb_sb[:, :], axis=AX.X)
            VE.reciprocal(rt_sc[:, 5:6], rt_sc[:, 4:5])
            m_cb = s_v.inc(VE.tensor_scalar_mul(comb_sb[:, :], comb_sb[:, :], rt_sc[:, 5:6]))
            TE.wait_ge(s_v.h, m_cb)
            if c > 0:
                TE.wait_ge(s_v.h, pS_free2)
            m_cT = s_t.inc(TE.transpose(pS[0:16, 0:128], comb_sb[:, :], ident_f32[:, :]))
            VE.wait_ge(s_t.h, m_cT)
            VE.tensor_copy(combT_tile[:, c * 128:(c + 1) * 128], pS[0:16, 0:128])
            pS_free2 = mark_last("vector", s_v)

        SY.wait_ge(s_v.h, v2_xp[NCT - 1])
        s_st.inc(SY.dma_start(out=xp_d[T * TILE:(T + 1) * TILE, :].rearrange("(c p) e -> p c e", p=128),
                              in_=xp_sb.rearrange("p (c e) -> p c e", c=4)), 16)
        SY.wait_ge(s_v.h, v2_ln2T[NCT - 1])
        s_st.inc(SY.dma_start(out=ln2T_d[:, T * 8 * TILE:(T + 1) * 8 * TILE], in_=ln2T_sb[:, :]), 16)
        SY.wait_ge(s_v.h, pS_free2)
        st_p2[T] = s_st.inc(SY.dma_start(out=combT_d[:, T * TILE:(T + 1) * TILE], in_=combT_tile[:, :]), 16)
        buf2_free = s_v.n

    if aux and use_ag:
        SY.wait_ge(s_st.h, s_st.n)
        aux_srcs = [
            (128, xp_d[0:128, 0:1024], False),
            (256, xp_d[2048:2176, 0:1024], False),
            (384, ln2T_d[:, 0:2048], True),
            (640, olocal_d[0:128, 0:1024], False),
            (768, sgate_d[0:128, 0:1024], False),
            (896, qte_d[:, 0:2048], True),
        ]
        aux_last = None
        for row, src, isbf in aux_srcs:
            if aux_last is not None:
                SY.wait_ge(s_st.h, aux_last)
            dst = x2_sb[:, 0:1024].bitcast(BF16) if isbf else x2_sb[:, 0:1024]
            m = s_ld.inc(SY.dma_start(out=dst, in_=src), 16)
            SY.wait_ge(s_ld.h, m)
            aux_last = s_st.inc(SY.dma_start(out=aux_d[row:row + 128, 0:1024], in_=x2_sb[:, 0:1024]), 16)
        SY.wait_ge(s_st.h, aux_last)
        m = s_ld.inc(SY.dma_start(out=x2_sb[0:16, 0:4096], in_=combT_d[0:16, 0:4096]), 16)
        SY.wait_ge(s_ld.h, m)
        s_st.inc(SY.dma_start(out=aux_d[512:528, 0:4096], in_=x2_sb[0:16, 0:4096]), 16)
        s_st.inc(SY.dma_start(out=aux_d[1152:1280, 0:4096], in_=Wo_sb[:, :].bitcast(F32)), 16)
        s_st.inc(SY.dma_start(out=aux_d[1408:1536, 0:256], in_=Gres_sb[:, :]), 16)

    if stop_after in ("probe_ol", "probe_sg", "probe_qt", "probe_gres", "probe_sinit"):
        SY.wait_ge(s_v.h, v_sinit)
        src_t = {"probe_ol": olocal_d, "probe_sg": sgate_d}.get(stop_after)
        if src_t is not None:
            for u in range(4):
                m = s_ld.inc(SY.dma_start(out=x2_sb[:, 0:E], in_=src_t[u * 128:(u + 1) * 128, :]), 16)
                SY.wait_ge(s_ld.h, m)
                s_st.inc(SY.dma_start(out=out_d[u * 128:(u + 1) * 128, :], in_=x2_sb[:, 0:E]), 16)
        elif stop_after == "probe_sinit":
            VE.wait_ge(s_ld.h, s_ld.n)
            mv = s_v.inc(VE.tensor_copy(x2_sb[:, 0:512], Sinit_sb[:, :]))
            VE.tensor_copy(x2_sb[:, 512:516], emask_sb[:, :])
            mv = mark_last("vector", s_v)
            SY.wait_ge(s_v.h, mv)
            stj = s_st.inc(SY.dma_start(out=out_d[0:128, 0:516], in_=x2_sb[:, 0:516]), 16)
            for j in range(4):
                VE.wait_ge(s_st.h, stj)
                mv = s_v.inc(VE.tensor_copy(x2_sb[:, 0:520], agland[:, j * 520:(j + 1) * 520]))
                SY.wait_ge(s_v.h, mv)
                stj = s_st.inc(SY.dma_start(out=out_d[128 * (1 + j):128 * (2 + j), 0:520],
                                            in_=x2_sb[:, 0:520]), 16)
            m = s_ld.inc(SY.dma_start(out=x2_sb[:, 0:520], in_=cc_in_d[:, :]), 16)
            SY.wait_ge(s_ld.h, m)
            s_st.inc(SY.dma_start(out=out_d[128 * 5:128 * 6, 0:520], in_=x2_sb[:, 0:520]), 16)
        elif stop_after == "probe_qt":
            m = s_ld.inc(SY.dma_start(out=qt2_sb[0][:, :], in_=qte_d[:, 0:8 * TILE]), 16)
            SY.wait_ge(s_ld.h, m)
            VE.wait_ge(s_ld.h, m)
            mv = s_v.inc(VE.tensor_copy(x2_sb[:, 0:E], qt2_sb[0][:, 0:1024]))
            SY.wait_ge(s_v.h, mv)
            s_st.inc(SY.dma_start(out=out_d[0:128, :], in_=x2_sb[:, 0:E]), 16)
        else:
            mv = s_v.inc(VE.tensor_copy(x2_sb[:, 0:8 * nchunk], Gres_sb[:, :]))
            SY.wait_ge(s_v.h, mv)
            s_st.inc(SY.dma_start(out=out_d[0:128, 0:8 * nchunk], in_=x2_sb[:, 0:8 * nchunk]), 16)
        SY.wait_ge(s_st.h, s_st.n)
        return finish([P2])
    if stop_after == "p2a":
        sentinel(1, 11.0, (s_v, s_v.n))
        VE.wait_ge(s_s.h, s_s.n)
        VE.wait_ge(s_t.h, s_t.n)
        sentinel(2, 11.0)
        return finish([P2])

    barrier()
    P2.close()

    P3 = Scope(nc)
    ln2T_ft = P3.sbuf("ln2T_ft", [128, 8 * FT], BF16)
    moe_acc = P3.sbuf("moe_acc", [128, nsub * E], F32)
    w1e_sb = [P3.sbuf(f"w1e{i}", [128, 8 * HID], BF16) for i in range(2)]
    w3e_sb = [P3.sbuf(f"w3e{i}", [128, 8 * HID], BF16) for i in range(2)]
    w2e_sb = [P3.sbuf(f"w2e{i}", [128, 4 * E], BF16) for i in range(4)]
    w2s_sb = P3.sbuf("w2s_sb", [128, 4 * E], BF16)
    cb_sb = [P3.sbuf(f"cb{i}", [128, FT], F32) for i in range(2)]
    act_sb = [P3.sbuf(f"act{i}", [128, 4 * FT], BF16) for i in range(5)]
    silu_scr = [P3.sbuf(f"silu{i}", [128, 512], BF16) for i in range(2)]
    silu2_scr = [P3.sbuf(f"silu2{i}", [128, 512], BF16) for i in range(2)]

    GROUPS = [[0, 1, 2, 3], [4, 5, 6, 7], [8, 9, 10, 11], [12, 13, 14, 15, None]]

    moe_free = s_v.n
    ln2ft_free = 0
    w13_free_wb = [0, 0]
    cb_free_wb = [0, 0]
    w2_free = 0
    act_free = 0
    st_out = [None] * nftile

    for FTi in range(nftile):
        T0 = FTi * (FT // TILE)
        if FTi >= 1:
            SY.wait_ge(s_t.h, ln2ft_free)
        for th2 in range(FT // TILE):
            s_ld.inc(SY.dma_start(
                out=ln2T_ft[:, :].rearrange("p (j s) -> p j s", j=8)[:, :, th2 * TILE:(th2 + 1) * TILE],
                in_=ln2T_d[:, (T0 + th2) * 8 * TILE:(T0 + th2 + 1) * 8 * TILE]
                .rearrange("p (j s) -> p j s", j=8)), 16)
        if FTi >= 1:
            SY.wait_ge(s_st.h, st_out[FTi - 1])
        ld_ft = s_ld.inc(SY.dma_start(
            out=moe_acc.rearrange("p (u e) -> p u e", e=E),
            in_=xp_d[FTi * FT:(FTi + 1) * FT, :].rearrange("(u p) e -> p u e", p=128)),
            16)

        if stop_after == "f0" and FTi == 0:
            VE.wait_ge(s_ld.h, ld_ft)
            sentinel(1, 13.0)
            return finish([P3])
        if stop_after == "f0x" and FTi == 0:
            SY.wait_ge(s_ld.h, ld_ft)
            for u in range(4):
                s_st.inc(SY.dma_start(out=out_d[u * 128:(u + 1) * 128, :],
                                      in_=moe_acc[:, u * E:(u + 1) * E]), 16)
            VE.wait_ge(s_st.h, s_st.n)
            return finish([P3])
        hstep = 0
        bank_act_mark = {0: 0, 1: 0}
        for gi, grp in enumerate(GROUPS):
            act_marks = {}
            ldw13 = {}
            ldw2 = {}
            ldcb = {}
            for ei, e in enumerate(grp):
                shared = e is None
                wb = (e % 2) if not shared else 0
                if shared:
                    GP.wait_ge(s_t.h, w13_free_wb[wb])
                    s_gp.inc(GP.dma_start(out=w1e_sb[wb].rearrange("p (a n) -> p a n", a=8), in_=sw1T_in.rearrange("(a p) n -> p a n", p=128)), 16)
                    ldw13[ei] = s_gp.inc(
                        GP.dma_start(out=w3e_sb[wb].rearrange("p (a n) -> p a n", a=8), in_=sw3T_in.rearrange("(a p) n -> p a n", p=128)), 16)
                    GP.wait_ge(s_t.h, w2_free)
                    ldw2[ei] = s_gp.inc(
                        GP.dma_start(out=w2s_sb.rearrange("p (a n) -> p a n", a=4), in_=sw2T_in.rearrange("(a p) n -> p a n", p=128)), 16)
                else:
                    SY.wait_ge(s_t.h, w13_free_wb[wb])
                    s_ld.inc(SY.dma_start(out=w1e_sb[wb].rearrange("p (a n) -> p a n", a=8), in_=w1b_d[e].rearrange("(a p) n -> p a n", p=128)), 16)
                    ldw13[ei] = s_ld.inc(
                        SY.dma_start(out=w3e_sb[wb].rearrange("p (a n) -> p a n", a=8), in_=w3b_d[e].rearrange("(a p) n -> p a n", p=128)), 16)
                    SY.wait_ge(s_t.h, w2_free)
                    ldw2[ei] = s_ld.inc(
                        SY.dma_start(out=w2e_sb[ei].rearrange("p (a n) -> p a n", a=4), in_=w2b_d[e].rearrange("(a p) n -> p a n", p=128)), 16)
                    GP.wait_ge(s_v.h, cb_free_wb[wb])
                    ldcb[ei] = s_gp.inc(GP.dma_start(out=cb_sb[wb][:, :],
                                                     in_=combT_d[e:e + 1, FTi * FT:(FTi + 1) * FT]
                                                     .broadcast_to([128, FT])), 16)
                w1 = w1e_sb[wb]
                w3 = w3e_sb[wb]
                act = act_sb[ei]
                TE.wait_ge(s_gp.h if shared else s_ld.h, ldw13[ei])
                TE.wait_ge(s_ld.h, ld_ft)
                first_of_expert = True
                for jj in range(4):
                    for th in range(nth):
                        bk = hstep % 2
                        hstep += 1
                        ph1 = pA if bk == 0 else pB
                        ph3 = pP[0] if bk == 0 else pP[1]
                        TE.wait_ge(s_v.h, bank_act_mark[bk])
                        for dj in range(8):
                            TE.matmul(ph1[:, :],
                                      w1[:, dj * HID + jj * 128: dj * HID + (jj + 1) * 128],
                                      ln2T_ft[:, dj * FT + th * 512: dj * FT + (th + 1) * 512],
                                      start=(dj == 0), stop=(dj == 7))
                        for dj in range(8):
                            TE.matmul(ph3[:, :],
                                      w3[:, dj * HID + jj * 128: dj * HID + (jj + 1) * 128],
                                      ln2T_ft[:, dj * FT + th * 512: dj * FT + (th + 1) * 512],
                                      start=(dj == 0), stop=(dj == 7))
                        m_h = mark_last("tensor", s_t)
                        SC.wait_ge(s_t.h, m_h)
                        SC.wait_ge(s_v.h, bank_act_mark[bk])
                        m_si = s_s.inc(SC.activation(silu_scr[bk][:, :], ph1[:, :], AF.Silu))
                        VE.wait_ge(s_s.h, m_si)
                        if shared:
                            VE.tensor_copy(silu2_scr[bk][:, :], ph3[:, :])
                        else:
                            VE.wait_ge(s_gp.h, ldcb[ei])
                            VE.tensor_mul(silu2_scr[bk][:, :], ph3[:, :],
                                          cb_sb[wb][:, th * 512:(th + 1) * 512])
                        if first_of_expert and act_free > 0:
                            VE.wait_ge(s_t.h, act_free)
                        first_of_expert = False
                        m_act = s_v.inc(VE.tensor_mul(act[:, jj * FT + th * 512: jj * FT + (th + 1) * 512],
                                                      silu_scr[bk][:, :], silu2_scr[bk][:, :]))
                        bank_act_mark[bk] = m_act
                        act_marks[(ei, jj, th)] = m_act
                w13_free_wb[wb] = s_t.n
                if not shared:
                    cb_free_wb[wb] = s_v.n
            if stop_after == "f1" and FTi == 0 and gi == 0:
                VE.wait_ge(s_s.h, s_s.n)
                VE.wait_ge(s_t.h, s_t.n)
                sentinel(1, 13.0)
                return finish([P3])
            for ei, e in enumerate(grp):
                TE.wait_ge(s_v.h, act_marks[(ei, 3, nth - 1)])
                TE.wait_ge(s_gp.h if e is None else s_ld.h, ldw2[ei])
            for u in range(nsub):
                TE.wait_ge(s_v.h, moe_free)
                first_half = [True, True]
                for ei, e in enumerate(grp):
                    w2 = w2s_sb if e is None else w2e_sb[ei]
                    act = act_sb[ei]
                    for jj in range(4):
                        last = (ei == len(grp) - 1) and (jj == 3)
                        for nh2 in range(2):
                            TE.matmul(pO[:, nh2 * 512:(nh2 + 1) * 512],
                                      act[:, jj * FT + u * 128: jj * FT + (u + 1) * 128],
                                      w2[:, jj * E + nh2 * 512: jj * E + (nh2 + 1) * 512],
                                      start=first_half[nh2], stop=last)
                            first_half[nh2] = False
                m_eo = mark_last("tensor", s_t)
                VE.wait_ge(s_t.h, m_eo)
                VE.wait_ge(s_ld.h, ld_ft)
                moe_free = s_v.inc(VE.tensor_add(moe_acc[:, u * E:(u + 1) * E],
                                                 moe_acc[:, u * E:(u + 1) * E], pO[:, :]))
            if stop_after in ("f2", "f2g1", "f2g2", "f2g3") and FTi == 0 and gi == {"f2": 0, "f2g1": 1, "f2g2": 2, "f2g3": 3}[stop_after]:
                VE.wait_ge(s_s.h, s_s.n)
                VE.wait_ge(s_t.h, s_t.n)
                sentinel(1, 13.0)
                return finish([P3])
            act_free = s_t.n
            w2_free = s_t.n
        if stop_after == "f3" and FTi == 0:
            VE.wait_ge(s_s.h, s_s.n)
            VE.wait_ge(s_t.h, s_t.n)
            sentinel(1, 13.0)
            return finish([P3])
        if stop_after == "f3s" and FTi == 0:
            SY.wait_ge(s_v.h, moe_free)
            s_st.inc(SY.dma_start(out=out_d[0:128, :], in_=moe_acc[:, 0:E]), 16)
            VE.wait_ge(s_st.h, s_st.n)
            sentinel(1, 13.0)
            return finish([P3])
        SY.wait_ge(s_v.h, moe_free)
        for u in range(nsub):
            st_out[FTi] = s_st.inc(SY.dma_start(
                out=out_d[FTi * FT + u * 128: FTi * FT + (u + 1) * 128, :],
                in_=moe_acc[:, u * E:(u + 1) * E]), 16)
        ln2ft_free = s_t.n

    return finish([P3])


def _host_inputs(inputs, core, tpc, use_ag):
    x = np.asarray(inputs["L0_attn_x"])
    if use_ag:
        b, q = core // 4, core % 4
        xs = np.ascontiguousarray(x[b].reshape(-1, E)[q * tpc:(q + 1) * tpc], dtype=np.float32)
    else:
        q = 0
        xs = np.ascontiguousarray(np.asarray(x, np.float32).reshape(-1, E)[:tpc])
    return {
        "x": xs,
        "Wq": np.ascontiguousarray(inputs["L0_attn_Wq"], np.float32),
        "Wk": np.ascontiguousarray(inputs["L0_attn_Wk"], np.float32),
        "Wv": np.ascontiguousarray(inputs["L0_attn_Wv"], np.float32),
        "Wo": np.ascontiguousarray(inputs["L0_attn_Wo"], np.float32),
        "G1": np.ascontiguousarray(inputs["L0_attn_G1"], np.float32),
        "G2": np.ascontiguousarray(inputs["L0_attn_G2"], np.float32),
        "delta_c": np.ascontiguousarray(np.asarray(inputs["L0_attn_delta"], np.float32).reshape(8, 128).T),
        "normw_c": np.ascontiguousarray(np.asarray(inputs["L0_attn_norm_w"], np.float32).reshape(8, 128).T),
        "normw_row": np.ascontiguousarray(np.asarray(inputs["L0_attn_norm_w"], np.float32).reshape(1, E)),
        "gateT": np.ascontiguousarray(np.asarray(inputs["L0_ffn_gate_w"], np.float32).T),
        "w1T": np.ascontiguousarray(np.asarray(inputs["L0_ffn_w1"], np.float32).transpose(0, 2, 1)),
        "w3T": np.ascontiguousarray(np.asarray(inputs["L0_ffn_w3"], np.float32).transpose(0, 2, 1)),
        "w2T": np.ascontiguousarray(np.asarray(inputs["L0_ffn_w2"], np.float32).transpose(0, 2, 1)),
        "sw1T": np.ascontiguousarray(np.asarray(inputs["L0_ffn_sw1"], np.float32).T),
        "sw3T": np.ascontiguousarray(np.asarray(inputs["L0_ffn_sw3"], np.float32).T),
        "sw2T": np.ascontiguousarray(np.asarray(inputs["L0_ffn_sw2"], np.float32).T),
        "ident_bf": np.eye(128, dtype=ml_dtypes.bfloat16),
        "ident_f32": np.eye(128, dtype=np.float32),
        "mask_st": np.ascontiguousarray(np.triu(np.ones((128, 128), np.float32))),
        "emask": np.ascontiguousarray(np.repeat((np.arange(4) == q).astype(np.float32)[None, :], 128, axis=0)),
    }


def kernel(**inputs):
    tpc = 4096
    nc = build_program(tpc, use_ag=True)
    in_maps = [_host_inputs(inputs, c, tpc, True) for c in range(8)]
    res = run_bass_kernel_spmd(nc, in_maps, core_ids=list(range(8)))
    out = np.empty((2, 16384, E), np.float32)
    for c in range(8):
        b, q = c // 4, c % 4
        out[b, q * tpc:(q + 1) * tpc, :] = res.results[c]["out"]
    return out


# version 09
# speedup vs baseline: 1.0106x; 1.0106x over previous
import numpy as np
import ml_dtypes
import concourse.bass as bass
import concourse.mybir as mybir
from concourse.bass_utils import run_bass_kernel_spmd

F32 = mybir.dt.float32
BF16 = mybir.dt.bfloat16
AF = mybir.ActivationFunctionType
ALU = mybir.AluOpType
AX = mybir.AxisListType

E = 1024
D = 64
H = 16
HID = 512
NEXP = 16
C = 128
TILE = 512
NCT = TILE // C
LN_EPS = 1e-5
RMS_EPS = 1e-5


class _RecInstr:
    __slots__ = ("incs",)

    def __init__(self):
        self.incs = []

    def then_inc(self, sem_handle, k):
        self.incs.append((sem_handle, k))
        return self


_SEQ_OPS = ("wait_ge", "nop", "drain")


def _ap_tensor_names(x):
    t = getattr(x, "tensor", None)
    if t is not None:
        return [t.name]
    return []


class Rec:

    def __init__(self, queue, hazard=False):
        self._q = queue
        self._hazard = hazard
        self._prev_outs = set()

    def __getattr__(self, op):
        q = self._q
        rec = self

        def call(*a, **kw):
            if rec._hazard and op not in _SEQ_OPS:
                aps = []
                outs = set()
                ins = set()
                pos_aps = [x for x in a if getattr(x, "tensor", None) is not None]
                if "out" in kw and getattr(kw["out"], "tensor", None) is not None:
                    outs.add(kw["out"].tensor.name)
                    pos_in_start = 0
                elif pos_aps:
                    outs.add(pos_aps[0].tensor.name)
                    pos_aps = pos_aps[1:]
                for x in pos_aps:
                    ins.add(x.tensor.name)
                for kname, x in kw.items():
                    if getattr(x, "tensor", None) is not None:
                        if kname in ("out", "accum_out", "out_max", "out_indices"):
                            outs.add(x.tensor.name)
                        else:
                            ins.add(x.tensor.name)
                if rec._prev_outs & (ins | outs):
                    q.append(("drain", (), {}, _RecInstr()))
                rec._prev_outs = outs
            ri = _RecInstr()
            q.append((op, a, kw, ri))
            return ri

        return call


class Sem:
    def __init__(self, handle):
        self.h = handle
        self.n = 0

    def inc(self, instr, k=1):
        instr.then_inc(self.h, k)
        self.n += k
        return self.n


def _replay(engine, queue):
    for op, a, kw, ri in queue:
        instr = getattr(engine, op)(*a, **kw)
        for sem_handle, k in ri.incs:
            instr.then_inc(sem_handle, k)


class Scope:
    def __init__(self, nc):
        self.nc = nc
        self.items = []

    def sbuf(self, name, shape, dtype):
        t = self.nc.sbuf_tensor(name, shape, dtype)
        h = t.__enter__()
        self.items.append(t)
        return h

    def close(self):
        for t in reversed(self.items):
            t.__exit__(None, None, None)
        self.items = []


def build_program(tpc, use_ag, stop_after=None, debug_tap=None, paranoid=0, aux=False):
    ntile = tpc // TILE
    nchunk = tpc // C
    FT = min(1024, tpc)
    nftile = tpc // FT
    nsub = FT // 128
    nth = FT // 512

    nc = bass.Bass("TRN2", target_bir_lowering=False, detect_race_conditions=False)

    def din(name, shape, dtype=F32):
        return nc.dram_tensor(name, shape, dtype, kind="ExternalInput").ap()

    def dint(name, shape, dtype=F32):
        return nc.dram_tensor(name, shape, dtype, kind="Internal").ap()

    x_in = din("x", [tpc, E])
    Wq_in = din("Wq", [E, E])
    Wk_in = din("Wk", [E, E])
    Wv_in = din("Wv", [E, E])
    Wo_in = din("Wo", [E, E])
    G1_in = din("G1", [E, D])
    G2_in = din("G2", [D, E])
    delta_in = din("delta_c", [128, 8])
    normw_in = din("normw_c", [128, 8])
    normw_row_in = din("normw_row", [1, E])
    gateT_in = din("gateT", [E, NEXP])
    w1T_in = din("w1T", [NEXP, E, HID])
    w3T_in = din("w3T", [NEXP, E, HID])
    w2T_in = din("w2T", [NEXP, HID, E])
    sw1T_in = din("sw1T", [E, HID])
    sw3T_in = din("sw3T", [E, HID])
    sw2T_in = din("sw2T", [HID, E])
    ident_bf_in = din("ident_bf", [128, 128], BF16)
    ident_f32_in = din("ident_f32", [128, 128])
    mask_st_in = din("mask_st", [128, 128])
    emask_in = din("emask", [128, 4])

    out_d = nc.dram_tensor("out", [tpc, E], F32, kind="ExternalOutput").ap()
    aux_d = nc.dram_tensor("aux", [1536, 4096], F32, kind="ExternalOutput").ap() if aux else None

    olocal_d = dint("olocal_d", [tpc, E])
    sgate_d = dint("sgate_d", [tpc, E])
    qte_d = dint("qte_d", [128, 8 * tpc], BF16)
    qto_d = dint("qto_d", [128, 8 * tpc], BF16)
    xp_d = dint("xp_d", [tpc, E])
    ln2T_d = dint("ln2T_d", [128, 8 * tpc], BF16)
    combT_d = dint("combT_d", [NEXP, tpc])
    w1b_d = dint("w1b_d", [NEXP, E, HID], BF16)
    w3b_d = dint("w3b_d", [NEXP, E, HID], BF16)
    w2b_d = dint("w2b_d", [NEXP, HID, E], BF16)
    if use_ag:
        cc_in_d = dint("cc_in", [128, 520])
        cc_out_d = dint("cc_out", [512, 520])

    P0 = Scope(nc)
    delta_neg = P0.sbuf("delta_neg", [128, 8], F32)
    normw_sb = P0.sbuf("normw_sb", [128, 8], F32)
    ident_bf = P0.sbuf("ident_bf_sb", [128, 128], BF16)
    ident_f32 = P0.sbuf("ident_f32_sb", [128, 128], F32)
    mask_st = P0.sbuf("mask_st_sb", [128, 128], F32)
    emask_sb = P0.sbuf("emask_sb", [128, 4], F32)
    ones_col = P0.sbuf("ones_col", [128, 1], F32)
    eps_col = P0.sbuf("eps_col", [128, 1], F32)
    S_sb = P0.sbuf("S_sb", [128, 512], F32)
    S_bf = P0.sbuf("S_bf_sb", [128, 512], BF16)
    Gamma_sb = P0.sbuf("Gamma_sb", [128, 8], F32)
    Gres_sb = P0.sbuf("Gres_sb", [128, 8 * nchunk], F32)
    lnS = P0.sbuf("lnS", [128, 32], F32)
    sq_scr = P0.sbuf("sq_scr", [128, E], F32)
    nwbc_sb = P0.sbuf("nwbc_sb", [128, E], F32)
    Wo_sb = P0.sbuf("Wo_sb", [128, 8 * E], BF16)
    gateT_sb = P0.sbuf("gateT_sb", [128, 8 * NEXP], BF16)

    psum_items = []

    def psum(name, shape, dtype):
        t = nc.psum_tensor(name, shape, dtype)
        h = t.__enter__()
        psum_items.append(t)
        return h

    pA = psum("pA", [128, 512], F32)
    pB = psum("pB", [128, 512], F32)
    pT = psum("pT", [128, 1024], BF16)
    pP = [psum("pP0", [128, 512], F32), psum("pP1", [128, 512], F32)]
    pO = psum("pO", [128, 1024], F32)
    pS = psum("pS", [128, 512], F32)

    sem_items = []

    def sem(name):
        t = nc.semaphore(name)
        h = t.__enter__()
        sem_items.append(t)
        return Sem(h)

    s_ld = sem("s_ld")
    s_gp = sem("s_gp")
    s_t = sem("s_t")
    s_v = sem("s_v")
    s_s = sem("s_s")
    s_cc = sem("s_cc")
    s_st = sem("s_st")

    s_bar = sem("s_bar")

    Q = {k: [] for k in ("tensor", "vector", "scalar", "sync", "gpsimd")}
    TE, VE, SC, SY, GP = (Rec(Q["tensor"]), Rec(Q["vector"], hazard=True),
                          Rec(Q["scalar"], hazard=True),
                          Rec(Q["sync"]), Rec(Q["gpsimd"]))
    ENGS = {"tensor": (TE, s_t), "vector": (VE, s_v), "scalar": (SC, s_s),
            "sync": (SY, s_ld), "gpsimd": (GP, s_gp)}

    def mark_last(qname, sem_obj, k=1):
        for op, a, kw, ri in reversed(Q[qname]):
            if op not in ("wait_ge", "nop", "drain", "engine_nop"):
                assert not ri.incs, f"double sync update on {qname}:{op}"
                ri.incs.append((sem_obj.h, k))
                sem_obj.n += k
                return sem_obj.n
        raise RuntimeError(f"no real instruction to mark in {qname}")

    def barrier():
        for name, s in (("tensor", s_t), ("vector", s_v), ("scalar", s_s)):
            for op, a, kw, ri in reversed(Q[name]):
                if op not in ("wait_ge", "nop", "drain", "engine_nop"):
                    if not any(h is s.h for h, _ in ri.incs):
                        assert not ri.incs, f"barrier collision on {name}:{op}"
                        ri.incs.append((s.h, 1))
                        s.n += 1
                    break
        waits = [(s_t, s_t.n), (s_v, s_v.n), (s_s, s_s.n),
                 (s_ld, s_ld.n), (s_st, s_st.n), (s_gp, s_gp.n), (s_cc, s_cc.n)]
        for name, (R_, _) in ENGS.items():
            for s, v in waits:
                R_.wait_ge(s.h, v)

    def finish(scopes):
        SY.wait_ge(s_st.h, s_st.n)
        block_cm = nc.Block()
        block = block_cm.__enter__()

        @block.tensor
        def _(t):
            _replay(t, Q["tensor"])

        @block.vector
        def _(v):
            _replay(v, Q["vector"])

        @block.scalar
        def _(sc):
            _replay(sc, Q["scalar"])

        @block.sync
        def _(sy):
            _replay(sy, Q["sync"])

        @block.gpsimd
        def _(g):
            _replay(g, Q["gpsimd"])

        block_cm.__exit__(None, None, None)
        for sc_ in scopes:
            sc_.close()
        for t in reversed(psum_items):
            t.__exit__(None, None, None)
        P0.close()
        for t in reversed(sem_items):
            t.__exit__(None, None, None)
        return nc

    P1 = Scope(nc)
    Wq_sb = P1.sbuf("Wq_sb", [128, 8 * E], BF16)
    Wk_sb = P1.sbuf("Wk_sb", [128, 8 * E], BF16)
    Wv_sb = P1.sbuf("Wv_sb", [128, 8 * E], BF16)
    G1_sb = P1.sbuf("G1_sb", [128, 8 * D], BF16)
    G2_sb = P1.sbuf("G2_sb", [64, E], BF16)
    xs_sb = P1.sbuf("xs", [128, 4 * E], F32)
    ln1_bf = [P1.sbuf(f"ln1_bf{i}", [128, E], BF16) for i in range(2)]
    ln1T_sb = P1.sbuf("ln1T_sb", [128, 8 * TILE], BF16)
    kT_j = [P1.sbuf(f"kTj{i}", [128, TILE], F32) for i in range(2)]
    qs_j = [P1.sbuf(f"qsj{i}", [128, TILE], F32) for i in range(2)]
    f_j = P1.sbuf("fj", [128, TILE], F32)
    Lraw_j = [P1.sbuf(f"Lrawj{i}", [128, TILE], F32) for i in range(2)]
    rrev_j = [P1.sbuf(f"rrevj{i}", [128, TILE + 4], F32) for i in range(2)]
    tmpq_j = [P1.sbuf(f"tmpqj{i}", [128, TILE], F32) for i in range(2)]
    colst = P1.sbuf("colst", [128, 64], F32)
    colst2_dbg = P1.sbuf("colst2_dbg", [128, 48], F32)
    Dcols = P1.sbuf("Dcols", [128, 32], F32)
    kscale = P1.sbuf("kscale", [128, 32], F32)
    qt_pf = [P1.sbuf("qt_pe", [128, 8 * TILE], BF16), P1.sbuf("qt_po", [128, 8 * TILE], BF16)]
    kt_pf = [P1.sbuf("kt_pe", [128, 8 * TILE], BF16), P1.sbuf("kt_po", [128, 8 * TILE], BF16)]
    khat_bf = P1.sbuf("khat_bf", [128, 8 * TILE], BF16)
    khat_tok = P1.sbuf("khat_tok", [128, E], BF16)
    V_tok = [P1.sbuf(f"V_tok{i}", [128, E], BF16) for i in range(2)]
    P_bf = P1.sbuf("P_bf", [128, 16 * C], BF16)
    o_scr = [P1.sbuf(f"o_scr{i}", [128, E], F32) for i in range(2)]
    sg_scr = [P1.sbuf(f"sg_scr{i}", [128, E], F32) for i in range(2)]
    ghT_sb = P1.sbuf("ghT_sb", [64, TILE], BF16)
    wcb_sb = [P1.sbuf(f"wcb{i}", [128, 2048], BF16) for i in range(2)]

    s_ld.inc(SY.dma_start(out=ident_bf[:, :], in_=ident_bf_in), 16)
    s_ld.inc(SY.dma_start(out=ident_f32[:, :], in_=ident_f32_in), 16)
    s_ld.inc(SY.dma_start(out=mask_st[:, :], in_=mask_st_in), 16)
    s_ld.inc(SY.dma_start(out=emask_sb[:, :], in_=emask_in), 16)
    s_ld.inc(SY.dma_start(out=delta_neg[:, :], in_=delta_in), 16)
    s_ld.inc(SY.dma_start(out=normw_sb[:, :], in_=normw_in), 16)
    ld_const = s_ld.inc(SY.dma_start(out=nwbc_sb[:, :], in_=normw_row_in.broadcast_to([128, E])), 16)

    s_gp.inc(GP.dma_start(out=Wq_sb.rearrange("p (a n) -> p a n", a=8), in_=Wq_in.rearrange("(a p) n -> p a n", p=128)), 16)
    s_gp.inc(GP.dma_start(out=Wk_sb.rearrange("p (a n) -> p a n", a=8), in_=Wk_in.rearrange("(a p) n -> p a n", p=128)), 16)
    s_gp.inc(GP.dma_start(out=Wv_sb.rearrange("p (a n) -> p a n", a=8), in_=Wv_in.rearrange("(a p) n -> p a n", p=128)), 16)
    s_gp.inc(GP.dma_start(out=G1_sb.rearrange("p (a n) -> p a n", a=8), in_=G1_in.rearrange("(a p) n -> p a n", p=128)), 16)
    gp_attnw = s_gp.inc(GP.dma_start(out=G2_sb[:, :], in_=G2_in), 16)
    gp_wo = s_gp.inc(GP.dma_start(out=Wo_sb.rearrange("p (a n) -> p a n", a=8),
                                  in_=Wo_in.rearrange("(a p) n -> p a n", p=128)), 16)
    gp_gateT = s_gp.inc(GP.dma_start(out=gateT_sb.rearrange("p (a n) -> p a n", a=8),
                                     in_=gateT_in.rearrange("(a p) n -> p a n", p=128)), 16)
    if stop_after == "p0b":
        GP.wait_ge(s_gp.h, s_gp.n)
        return finish([P1])

    VE.wait_ge(s_ld.h, ld_const)
    VE.tensor_scalar_mul(delta_neg[:, :], delta_neg[:, :], -1.0)
    VE.memset(ones_col[:, :], 1.0)
    VE.memset(eps_col[:, :], RMS_EPS)
    VE.memset(S_sb[:, :], 0.0)
    VE.memset(S_bf[:, :], 0.0)
    VE.memset(Gamma_sb[:, :], 1.0)
    VE.memset(rrev_j[0][:, TILE:], 1.0)
    VE.memset(rrev_j[1][:, TILE:], 1.0)
    VE.memset(qt_pf[0][:, :], 0.0)
    VE.memset(qt_pf[1][:, :], 0.0)
    VE.memset(kt_pf[0][:, :], 0.0)
    VE.memset(kt_pf[1][:, :], 0.0)
    v_const = mark_last("vector", s_v)
    if stop_after == "p0a":
        SY.wait_ge(s_ld.h, s_ld.n)
        return finish([P1])

    if stop_after == "sent1":
        VE.wait_ge(s_gp.h, gp_attnw)
        sentinel(1, 7.0)
        return finish([P1])
    cast_jobs = []
    for src, dst in [(w1T_in, w1b_d), (w3T_in, w3b_d), (w2T_in, w2b_d)]:
        fs = src.rearrange("e a b -> (e a b)")
        fd = dst.rearrange("e a b -> (e a b)")
        total = NEXP * E * HID
        step = 128 * 2048
        for off in range(0, total, step):
            cast_jobs.append((fs[off:off + step].rearrange("(p n) -> p n", p=128),
                              fd[off:off + step].rearrange("(p n) -> p n", p=128)))
    cast_marks = []
    for i, (src, dst) in enumerate(cast_jobs):
        buf = wcb_sb[i % 2]
        if i >= 2:
            GP.wait_ge(s_gp.h, cast_marks[i - 2])
        m = s_gp.inc(GP.dma_start(out=buf[:, :], in_=src), 16)
        GP.wait_ge(s_gp.h, m)
        cast_marks.append(s_gp.inc(GP.dma_start(out=dst, in_=buf[:, :]), 16))
    def sentinel(blk, val, eng_wait=None):
        if eng_wait is not None:
            VE.wait_ge(eng_wait[0].h, eng_wait[1])
        m = s_v.inc(VE.memset(sq_scr[:, :], float(val)))
        SY.wait_ge(s_v.h, m)
        s_st.inc(SY.dma_start(out=out_d[blk * 128:(blk + 1) * 128, :], in_=sq_scr[:, :]), 16)

    if stop_after in ("sent", "sent0", "sent1", "sent2"):
        sentinel(0, 7.0)
    if stop_after == "sent0":
        return finish([P1])
    gp_cast_done = s_gp.n
    if stop_after == "sent2":
        VE.wait_ge(s_gp.h, gp_cast_done)
        sentinel(1, 7.0)
        return finish([P1])
    if stop_after == "p0":
        GP.wait_ge(s_gp.h, s_gp.n)
        SY.wait_ge(s_ld.h, s_ld.n)
        return finish([P1])

    x_free = v_const
    pT_free = v_const
    pS_free = v_const
    pO_free = v_const
    pAB_free = None
    last_P_evacs = None
    st_p1 = [None] * ntile
    ln1bf_free = [0, 0]
    Vtok_free = [0, 0]
    ost_free = [0, 0]
    sgst_free = [0, 0]
    s_kT_m = [None, None]
    s_qs_m = [None, None]
    v_j_m = [None, None]
    scale_free = [None, None]
    p1_te_prev = 0

    for T in range(ntile):
        SY.wait_ge(s_v.h, x_free)
        ld_x = s_ld.inc(SY.dma_start(
            out=xs_sb.rearrange("p (c e) -> p c e", c=4),
            in_=x_in[T * TILE:(T + 1) * TILE, :].rearrange("(c p) e -> p c e", p=128)), 16)
        if T == 0:
            TE.wait_ge(s_gp.h, gp_attnw)
            TE.wait_ge(s_ld.h, ld_const)
        if stop_after == "probe_stats" and T == 0:
            VE.wait_ge(s_ld.h, ld_x)
            SC.wait_ge(s_ld.h, ld_x)
            for c in range(4):
                VE.reduce_sum(colst[:, c:c + 1], xs_sb[:, c * E:(c + 1) * E], axis=AX.X)
                SC.activation(sq_scr[:, :], xs_sb[:, c * E:(c + 1) * E], AF.Square,
                              accum_out=colst2_dbg[:, c:c + 1])
            mv = mark_last("vector", s_v)
            ms = mark_last("scalar", s_s)
            VE.wait_ge(s_s.h, ms)
            mv2 = s_v.inc(VE.tensor_copy(xs_sb[:, 0:4], colst[:, 0:4]))
            mv3 = s_v.inc(VE.tensor_copy(xs_sb[:, 4:8], colst2_dbg[:, 0:4]))
            SY.wait_ge(s_v.h, mv3)
            s_st.inc(SY.dma_start(out=out_d[0:128, 0:8], in_=xs_sb[:, 0:8]), 16)
            SY.wait_ge(s_st.h, s_st.n)
            return finish([P1])

        if stop_after == "sent" and T == 0:
            sentinel(1, 7.0)
        v_ln = [None] * NCT
        for c in range(NCT):
            xsl = xs_sb[:, c * E:(c + 1) * E]
            SC.wait_ge(s_ld.h, ld_x)
            SC.activation(sq_scr[:, :], xsl, AF.Square, accum_out=lnS[:, 2:3])
            m_ssq = mark_last("scalar", s_s)
            VE.wait_ge(s_ld.h, ld_x)
            VE.reduce_sum(lnS[:, 0:1], xsl, axis=AX.X)
            VE.tensor_scalar_mul(lnS[:, 1:2], lnS[:, 0:1], 1.0 / E)
            VE.wait_ge(s_s.h, m_ssq)
            VE.tensor_mul(lnS[:, 5:6], lnS[:, 1:2], lnS[:, 1:2])
            VE.tensor_scalar(lnS[:, 6:7], lnS[:, 2:3], 1.0 / E, None, ALU.mult)
            VE.tensor_sub(lnS[:, 7:8], lnS[:, 6:7], lnS[:, 5:6])
            m_var = s_v.inc(VE.tensor_scalar(lnS[:, 3:4], lnS[:, 7:8], LN_EPS, None, ALU.add))
            SC.wait_ge(s_v.h, m_var)
            m_sqrt = s_s.inc(SC.activation(lnS[:, 4:5], lnS[:, 3:4], AF.Sqrt))
            VE.wait_ge(s_s.h, m_sqrt)
            VE.reciprocal(lnS[:, 8:9], lnS[:, 4:5])
            VE.tensor_mul(lnS[:, 9:10], lnS[:, 1:2], lnS[:, 8:9])
            if ln1bf_free[c % 2] > 0:
                VE.wait_ge(s_t.h, ln1bf_free[c % 2])
            if stop_after == "probe_lnx" and T == 0:
                VE.tensor_copy(colst2_dbg[:, c * 10:(c + 1) * 10], lnS[:, 0:10])
            v_ln[c] = s_v.inc(VE.tensor_scalar(ln1_bf[c % 2][:, :], xsl,
                                               lnS[:, 8:9], lnS[:, 9:10], ALU.mult, ALU.subtract))
            TE.wait_ge(s_v.h, v_ln[c])
            TE.wait_ge(s_v.h, pT_free)
            for j in range(8):
                TE.transpose(pT[:, j * 128:(j + 1) * 128],
                             ln1_bf[c % 2][:, j * 128:(j + 1) * 128], ident_bf[:, :])
            m_t = mark_last("tensor", s_t)
            VE.wait_ge(s_t.h, m_t)
            if stop_after == "probe_pT" and T == 0 and c == 0:
                mv = s_v.inc(VE.tensor_copy(xs_sb[:, 0:E], pT[:, :]))
                SY.wait_ge(s_v.h, mv)
                m2 = s_st.inc(SY.dma_start(out=out_d[0:128, :], in_=xs_sb[:, 0:E]), 16)
                VE.wait_ge(s_st.h, m2)
                mv = s_v.inc(VE.tensor_copy(xs_sb[:, 0:128], ident_bf[:, :]))
                SY.wait_ge(s_v.h, mv)
                m2 = s_st.inc(SY.dma_start(out=out_d[128:256, 0:128], in_=xs_sb[:, 0:128]), 16)
                VE.wait_ge(s_st.h, m2)
                mv = s_v.inc(VE.tensor_copy(xs_sb[:, 0:E], ln1_bf[0][:, :]))
                SY.wait_ge(s_v.h, mv)
                s_st.inc(SY.dma_start(out=out_d[256:384, :], in_=xs_sb[:, 0:E]), 16)
                SY.wait_ge(s_st.h, s_st.n)
                return finish([P1])
            VE.tensor_copy(
                ln1T_sb[:, :].rearrange("p (j s) -> p j s", j=8)[:, :, c * 128:(c + 1) * 128],
                pT[:, :].rearrange("p (j s) -> p j s", j=8))
            pT_free = mark_last("vector", s_v)
            ln1bf_free[c % 2] = m_t
            barrier() if paranoid else None
        v_lnT_done = s_v.n
        if stop_after == "probe_lnx" and T == 0:
            VE.wait_ge(s_v.h, s_v.n)
            mv = s_v.inc(VE.tensor_copy(xs_sb[:, 0:40], colst2_dbg[:, 0:40]))
            SY.wait_ge(s_v.h, mv)
            s_st.inc(SY.dma_start(out=out_d[0:128, 0:40], in_=xs_sb[:, 0:40]), 16)
            VE.wait_ge(s_st.h, s_st.n)
            mv = s_v.inc(VE.tensor_copy(xs_sb[:, 0:E], ln1_bf[1][:, :]))
            SY.wait_ge(s_v.h, mv)
            s_st.inc(SY.dma_start(out=out_d[128:256, :], in_=xs_sb[:, 0:E]), 16)
            SY.wait_ge(s_st.h, s_st.n)
            return finish([P1])
        if stop_after == "probe_V2":
            TE.wait_ge(s_v.h, v_lnT_done)
            for nh in range(2):
                for j_in in range(8):
                    TE.matmul(pO[:, nh * 512:(nh + 1) * 512],
                              ln1T_sb[:, j_in * TILE + 0 * 128: j_in * TILE + 1 * 128],
                              Wv_sb[:, j_in * E + nh * 512: j_in * E + (nh + 1) * 512],
                              start=(j_in == 0), stop=(j_in == 7))
            mV = mark_last("tensor", s_t)
            VE.wait_ge(s_t.h, mV)
            mv = s_v.inc(VE.tensor_copy(xs_sb[:, 0:E], pO[:, :]))
            SY.wait_ge(s_v.h, mv)
            s_st.inc(SY.dma_start(out=out_d[0:128, :], in_=xs_sb[:, 0:E]), 16)
            SY.wait_ge(s_st.h, s_st.n)
            return finish([P1])
        if stop_after == "p1a":
            sentinel(1, 9.0, (s_v, v_lnT_done))
            return finish([P1])
        if stop_after == "probe_ln1T":
            SY.wait_ge(s_v.h, v_lnT_done)
            mv = s_v.inc(VE.tensor_copy(xs_sb[:, 0:E], ln1T_sb[:, 0:E]))
            SY.wait_ge(s_v.h, mv)
            s_st.inc(SY.dma_start(out=out_d[0:128, :], in_=xs_sb[:, 0:E]), 16)
            mv2 = s_v.inc(VE.tensor_copy(xs_sb[:, 0:E], ln1_bf[1][:, :]))
            SY.wait_ge(s_v.h, mv2)
            s_st.inc(SY.dma_start(out=out_d[128:256, :], in_=xs_sb[:, 0:E]), 16)
            SY.wait_ge(s_st.h, s_st.n)
            return finish([P1])

        jlvl = {"j_mm": 0, "j_ve1": 1, "j_ve2": 2, "j_sc": 3}.get(stop_after, 99)
        s_scale = [None] * 8
        TE.wait_ge(s_v.h, v_lnT_done)
        if p1_te_prev > 0:
            VE.wait_ge(s_t.h, p1_te_prev)
        for j in range(8):
            jb = j % 2
            if j >= 1:
                TE.wait_ge(s_s.h, s_qs_m[(j - 1) % 2])
            elif pAB_free is not None:
                TE.wait_ge(s_s.h, pAB_free)
            for j_in in range(8):
                TE.matmul(pA[:, :], Wk_sb[:, j_in * E + j * 128: j_in * E + j * 128 + 128],
                          ln1T_sb[:, j_in * TILE:(j_in + 1) * TILE],
                          start=(j_in == 0), stop=(j_in == 7))
            m_xk = mark_last("tensor", s_t)
            for j_in in range(8):
                TE.matmul(pB[:, :], Wq_sb[:, j_in * E + j * 128: j_in * E + j * 128 + 128],
                          ln1T_sb[:, j_in * TILE:(j_in + 1) * TILE],
                          start=(j_in == 0), stop=(j_in == 7))
            m_q = mark_last("tensor", s_t)
            SC.wait_ge(s_t.h, m_xk)
            if v_j_m[jb] is not None:
                SC.wait_ge(s_v.h, v_j_m[jb])
            SC.activation(kT_j[jb][:, :], pA[:, :], AF.Sigmoid,
                          bias=delta_neg[:, j:j + 1], scale=-1.0)
            s_kT_m[jb] = mark_last("scalar", s_s)
            SC.wait_ge(s_t.h, m_q)
            SC.activation(qs_j[jb][:, :], pB[:, :], AF.Silu)
            s_qs_m[jb] = mark_last("scalar", s_s)
            if jlvl < 1:
                v_j_m[jb] = s_v.n if v_j_m[jb] is None else v_j_m[jb]
                s_scale[j] = s_v.n
                scale_free[jb] = s_scale[j]
                continue
            VE.wait_ge(s_s.h, s_kT_m[jb])
            VE.tensor_scalar(f_j[:, :], kT_j[jb][:, :], -1.0, 1.0, ALU.mult, ALU.add)
            if scale_free[jb] is not None:
                VE.wait_ge(s_s.h, scale_free[jb])
            VE.tensor_tensor_scan(Lraw_j[jb][:, :], f_j[:, :], f_j[:, :], 1.0, ALU.mult, ALU.bypass)
            VE.tensor_tensor_scan(rrev_j[jb][:, 0:TILE][:, ::-1], f_j[:, ::-1], f_j[:, ::-1],
                                  1.0, ALU.mult, ALU.bypass)
            if jlvl < 2:
                v_j_m[jb] = mark_last("vector", s_v)
                s_scale[j] = s_v.n
                scale_free[jb] = None
                continue
            ecols = Lraw_j[jb][:, 127::128]
            VE.reciprocal(colst[:, j * 8: j * 8 + 4], ecols)
            VE.reciprocal(colst[:, j * 8 + 4: j * 8 + 7], rrev_j[jb][:, 128::128][:, 0:3])
            VE.wait_ge(s_s.h, s_qs_m[jb])
            for c in range(NCT):
                rr = colst[:, j * 8 + 4 + c: j * 8 + 5 + c] if c < 3 else ones_col[:, 0:1]
                VE.scalar_tensor_tensor(khat_bf[:, j * TILE + c * 128: j * TILE + (c + 1) * 128],
                                        rrev_j[jb][:, c * 128 + 1:(c + 1) * 128 + 1],
                                        rr, kT_j[jb][:, c * 128:(c + 1) * 128],
                                        ALU.mult, ALU.mult)
                bl = colst[:, j * 8 + c - 1: j * 8 + c] if c > 0 else ones_col[:, 0:1]
                VE.scalar_tensor_tensor(Dcols[:, c * 8 + j: c * 8 + j + 1],
                                        Lraw_j[jb][:, (c + 1) * 128 - 1:(c + 1) * 128],
                                        1.0, bl, ALU.mult, ALU.mult)
                blv = Lraw_j[jb][:, c * 128 - 1: c * 128] if c > 0 else ones_col[:, 0:1]
                VE.scalar_tensor_tensor(kscale[:, c * 8 + j: c * 8 + j + 1],
                                        blv, 1.0, colst[:, j * 8 + c: j * 8 + c + 1],
                                        ALU.mult, ALU.mult)
            if jlvl < 3:
                v_j_m[jb] = mark_last("vector", s_v)
                s_scale[j] = s_v.n
                scale_free[jb] = None
                continue
            if T >= 1 and j == 0:
                VE.wait_ge(s_st.h, st_p1[T - 1])
            for c in range(NCT):
                sl = slice(j * TILE + c * 128, j * TILE + (c + 1) * 128)
                lsl = slice(c * 128, (c + 1) * 128)
                bl = colst[:, j * 8 + c - 1: j * 8 + c] if c > 0 else ones_col[:, 0:1]
                el = kscale[:, c * 8 + j: c * 8 + j + 1]
                VE.scalar_tensor_tensor(qt_pf[0][0:64, sl], qs_j[jb][0:64, lsl],
                                        bl[0:64, :], Lraw_j[jb][0:64, lsl], ALU.mult, ALU.mult)
                VE.scalar_tensor_tensor(qt_pf[1][64:128, sl], qs_j[jb][64:128, lsl],
                                        bl[64:128, :], Lraw_j[jb][64:128, lsl], ALU.mult, ALU.mult)
                VE.tensor_scalar_mul(kt_pf[0][0:64, sl], khat_bf[0:64, sl], el[0:64, :])
                VE.tensor_scalar_mul(kt_pf[1][64:128, sl], khat_bf[64:128, sl], el[64:128, :])
            v_j_m[jb] = mark_last("vector", s_v)
            s_scale[j] = v_j_m[jb]
            scale_free[jb] = None
            barrier() if paranoid else None
        pAB_free = s_qs_m[1]
        if stop_after == "probe_wv":
            SY.wait_ge(s_s.h, s_s.n)
            for half in range(4):
                VE.wait_ge(s_st.h, s_st.n)
                mv = s_v.inc(VE.tensor_copy(xs_sb[:, 0:E], Wv_sb[:, half * E:(half + 1) * E]))
                SY.wait_ge(s_v.h, mv)
                s_st.inc(SY.dma_start(out=out_d[half * 128:(half + 1) * 128, :], in_=xs_sb[:, 0:E]), 16)
            SY.wait_ge(s_st.h, s_st.n)
            return finish([P1])
        if stop_after in ("probe_j", "j_mm", "j_ve1", "j_ve2", "j_sc"):
            SY.wait_ge(s_s.h, s_s.n)
            SY.wait_ge(s_v.h, s_v.n)
            dumps = [qs_j[0][:, :], kT_j[0][:, :], Lraw_j[0][:, :], tmpq_j[0][:, :]]
            for i, dsrc in enumerate(dumps):
                VE.wait_ge(s_st.h, s_st.n)
                mv = s_v.inc(VE.tensor_copy(xs_sb[:, i * TILE:(i + 1) * TILE], dsrc))
                SY.wait_ge(s_v.h, mv)
                s_st.inc(SY.dma_start(out=out_d[i * 128:(i + 1) * 128, 0:TILE],
                                      in_=xs_sb[:, i * TILE:(i + 1) * TILE]), 16)
            VE.wait_ge(s_st.h, s_st.n)
            mv = s_v.inc(VE.tensor_copy(xs_sb[:, 0:TILE], qt_pf[0][:, 7 * TILE:8 * TILE]))
            SY.wait_ge(s_v.h, mv)
            s_st.inc(SY.dma_start(out=out_d[0:128, TILE:2 * TILE], in_=xs_sb[:, 0:TILE]), 16)
            VE.wait_ge(s_st.h, s_st.n)
            mv = s_v.inc(VE.tensor_copy(xs_sb[:, 0:64], colst[:, :]))
            SY.wait_ge(s_v.h, mv)
            s_st.inc(SY.dma_start(out=out_d[128:256, TILE:TILE + 64], in_=xs_sb[:, 0:64]), 16)
            SY.wait_ge(s_st.h, s_st.n)
            return finish([P1])
        if stop_after == "p1b":
            VE.wait_ge(s_v.h, s_scale[7])
            sentinel(1, 9.0)
            return finish([P1])

        if stop_after == "sent" and T == 0:
            sentinel(2, 7.0)
        TE.wait_ge(s_s.h, s_kT_m[0])
        TE.wait_ge(s_s.h, s_kT_m[1])
        for j_in in range(8):
            TE.matmul(pA[0:64, :], G1_sb[:, j_in * D:(j_in + 1) * D],
                      ln1T_sb[:, j_in * TILE:(j_in + 1) * TILE],
                      start=(j_in == 0), stop=(j_in == 7))
        m_g1 = mark_last("tensor", s_t)
        VE.wait_ge(s_t.h, m_g1)
        v_ghT = s_v.inc(VE.tensor_copy(ghT_sb[:, :], pA[0:64, :]))
        barrier() if paranoid else None
        if stop_after == "p1c":
            VE.wait_ge(s_v.h, v_ghT)
            sentinel(1, 9.0)
            return finish([P1])

        if stop_after == "sent" and T == 0:
            sentinel(3, 7.0)
        glvl = {"g0": 0, "g1": 1, "g2": 2, "g3": 3}.get(stop_after, 99)
        v_O_evac = [None] * NCT
        s_gate = [None] * NCT
        for c in range(NCT):
            g = T * NCT + c
            cb2 = c % 2
            TE.wait_ge(s_v.h, pO_free)
            TE.wait_ge(s_v.h, s_scale[7])
            for nh in range(2):
                for j_in in range(8):
                    TE.matmul(pO[:, nh * 512:(nh + 1) * 512],
                              ln1T_sb[:, j_in * TILE + c * 128: j_in * TILE + (c + 1) * 128],
                              Wv_sb[:, j_in * E + nh * 512: j_in * E + (nh + 1) * 512],
                              start=(j_in == 0), stop=(j_in == 7))
            m_V = mark_last("tensor", s_t)
            VE.wait_ge(s_t.h, m_V)
            if Vtok_free[cb2] > 0:
                VE.wait_ge(s_t.h, Vtok_free[cb2])
            VE.tensor_copy(V_tok[cb2][:, :], pO[:, :])
            pO_free = mark_last("vector", s_v)
            v_V = pO_free
            barrier() if paranoid else None
            TE.wait_ge(s_v.h, v_ghT)
            if c > 0:
                TE.wait_ge(s_s.h, s_gate[c - 1])
            for nh in range(2):
                TE.matmul(pB[:, :] if nh == 0 else pA[:, :],
                          ghT_sb[:, c * 128:(c + 1) * 128],
                          G2_sb[:, nh * 512:(nh + 1) * 512], start=True, stop=True)
            m_gt = mark_last("tensor", s_t)
            SC.wait_ge(s_t.h, m_gt)
            if sgst_free[cb2] > 0:
                SC.wait_ge(s_st.h, sgst_free[cb2])
            SC.activation(sg_scr[cb2][:, 0:512], pB[:, :], AF.Sigmoid)
            SC.activation(sg_scr[cb2][:, 512:1024], pA[:, :], AF.Sigmoid)
            s_gate[c] = mark_last("scalar", s_s)
            SY.wait_ge(s_s.h, s_gate[c])
            sgst_free[cb2] = s_st.inc(SY.dma_start(
                out=sgate_d[(T * NCT + c) * C:(T * NCT + c + 1) * C, :],
                in_=sg_scr[cb2][:, :]), 16)

            if glvl < 1:
                continue
            VE.tensor_copy(Gres_sb[:, g * 8:(g + 1) * 8], Gamma_sb[:, :])
            v_Sbf = s_v.inc(VE.tensor_copy(S_bf[:, :], S_sb[:, :]))

            TE.wait_ge(s_v.h, pT_free)
            for j in range(8):
                TE.transpose(pT[:, j * 128:(j + 1) * 128],
                             khat_bf[:, j * TILE + c * 128: j * TILE + (c + 1) * 128],
                             ident_bf[:, :])
            m_kh = mark_last("tensor", s_t)
            VE.wait_ge(s_t.h, m_kh)
            VE.tensor_copy(khat_tok[:, :], pT[:, :])
            pT_free = mark_last("vector", s_v)
            v_khat = pT_free
            barrier() if paranoid else None

            if glvl < 2:
                continue
            v_P = [None] * 4
            for wv in range(4):
                pw = pP[wv % 2]
                if wv >= 2:
                    TE.wait_ge(s_v.h, v_P[wv - 2])
                elif last_P_evacs is not None:
                    TE.wait_ge(s_v.h, last_P_evacs[wv % 2])
                for hh in range(4):
                    h = wv * 4 + hh
                    j, par = h // 2, h % 2
                    TE.matmul(pw[:, hh * 128:(hh + 1) * 128],
                              kt_pf[par][:, j * TILE + c * 128: j * TILE + (c + 1) * 128],
                              qt_pf[par][:, j * TILE + c * 128: j * TILE + (c + 1) * 128],
                              start=True, stop=True)
                m_P = mark_last("tensor", s_t)
                VE.wait_ge(s_t.h, m_P)
                VE.tensor_tensor(P_bf[:, wv * 512:(wv + 1) * 512], pw[:, :],
                                 mask_st[:, :].unsqueeze(1).broadcast_to([128, 4, 128]),
                                 ALU.mult)
                v_P[wv] = mark_last("vector", s_v)
            last_P_evacs = [v_P[2], v_P[3]]
            barrier() if paranoid else None
            if stop_after == "probe_P" and T == 0 and c == 0:
                VE.wait_ge(s_v.h, v_P[3])
                for half in range(2):
                    mv = s_v.inc(VE.tensor_copy(xs_sb[:, 0:E], P_bf[:, half * 1024:(half + 1) * 1024]))
                    SY.wait_ge(s_v.h, mv)
                    m2 = s_st.inc(SY.dma_start(out=out_d[half * 128:(half + 1) * 128, :],
                                               in_=xs_sb[:, 0:E]), 16)
                    VE.wait_ge(s_st.h, m2)
                mv = s_v.inc(VE.tensor_copy(xs_sb[:, 0:E], V_tok[cb2][:, :]))
                SY.wait_ge(s_v.h, mv)
                s_st.inc(SY.dma_start(out=out_d[256:384, :], in_=xs_sb[:, 0:E]), 16)
                mv = s_v.inc(VE.tensor_copy(xs_sb[:, 0:E], qt_pf[0][:, 0:E].bitcast(F32).bitcast(BF16)[:, 0:1024] if False else qt_pf[0][:, 0:1024]))
                SY.wait_ge(s_v.h, mv)
                s_st.inc(SY.dma_start(out=out_d[384:512, :], in_=xs_sb[:, 0:E]), 16)
                SY.wait_ge(s_st.h, s_st.n)
                return finish([P1])

            if glvl < 3:
                continue
            TE.wait_ge(s_v.h, pO_free)
            TE.wait_ge(s_v.h, v_Sbf)
            TE.wait_ge(s_v.h, v_P[3])
            for h in range(H):
                j, par = h // 2, h % 2
                TE.matmul(pO[:, h * D:(h + 1) * D],
                          P_bf[:, h * 128:(h + 1) * 128],
                          V_tok[cb2][:, h * D:(h + 1) * D],
                          start=True, stop=False)
                TE.matmul(pO[:, h * D:(h + 1) * D],
                          qt_pf[par][:, j * TILE + c * 128: j * TILE + (c + 1) * 128],
                          S_bf[:, j * D:(j + 1) * D],
                          start=False, stop=True)
            m_O = mark_last("tensor", s_t)
            VE.wait_ge(s_t.h, m_O)
            if ost_free[cb2] > 0:
                VE.wait_ge(s_st.h, ost_free[cb2])
            VE.tensor_copy(o_scr[cb2][:, :], pO[:, :])
            pO_free = mark_last("vector", s_v)
            v_O_evac[c] = pO_free
            barrier() if paranoid else None
            SY.wait_ge(s_v.h, v_O_evac[c])
            ost_free[cb2] = s_st.inc(SY.dma_start(
                out=olocal_d[(T * NCT + c) * C:(T * NCT + c + 1) * C, :],
                in_=o_scr[cb2][:, :]), 16)

            if glvl < 4:
                continue
            TE.wait_ge(s_v.h, v_khat)
            for r in range(2):
                TE.wait_ge(s_v.h, pS_free)
                for jb2 in range(4):
                    j = 4 * r + jb2
                    for par in range(2):
                        h = 2 * j + par
                        po = par * 64
                        TE.matmul(pS[po:po + 64, jb2 * 128: jb2 * 128 + D],
                                  khat_tok[:, h * D:(h + 1) * D],
                                  V_tok[cb2][:, h * D:(h + 1) * D],
                                  start=True, stop=True,
                                  tile_position=(0, po))
                m_Sg = mark_last("tensor", s_t)
                VE.wait_ge(s_t.h, m_Sg)
                for jb2 in range(4):
                    j = 4 * r + jb2
                    VE.scalar_tensor_tensor(S_sb[:, j * D:(j + 1) * D], S_sb[:, j * D:(j + 1) * D],
                                            Dcols[:, c * 8 + j: c * 8 + j + 1],
                                            pS[:, jb2 * 128: jb2 * 128 + D],
                                            ALU.mult, ALU.add)
                    VE.tensor_scalar_mul(Gamma_sb[:, j:j + 1], Gamma_sb[:, j:j + 1],
                                         Dcols[:, c * 8 + j: c * 8 + j + 1])
                pS_free = mark_last("vector", s_v)
            Vtok_free[cb2] = m_Sg
            barrier() if paranoid else None

        x_free = v_ln[NCT - 1]
        pAB_free = s_gate[NCT - 1]
        p1_te_prev = s_t.n
        SY.wait_ge(s_v.h, s_scale[7])
        s_st.inc(SY.dma_start(out=qte_d[:, T * 8 * TILE:(T + 1) * 8 * TILE], in_=qt_pf[0][:, :]), 16)
        if glvl < 99 and T == ntile - 1:
            sentinel(1, 9.0, (s_v, s_v.n))
            VE.wait_ge(s_s.h, s_s.n)
            VE.wait_ge(s_t.h, s_t.n)
            sentinel(2, 9.0)
            return finish([P1])
        elif glvl < 99:
            pass
        st_p1[T] = s_st.inc(SY.dma_start(out=qto_d[:, T * 8 * TILE:(T + 1) * 8 * TILE], in_=qt_pf[1][:, :]), 16)

    if stop_after == "p1":
        sentinel(1, 9.0, (s_v, s_v.n))
        VE.wait_ge(s_s.h, s_s.n)
        VE.wait_ge(s_t.h, s_t.n)
        sentinel(2, 9.0)
        return finish([P1])

    if stop_after == "sent":
        sentinel(0, 17.0)
    if use_ag:
        SY.wait_ge(s_v.h, s_v.n)
        s_ld.inc(SY.dma_start(out=cc_in_d[:, 0:512], in_=S_sb[:, :]), 16)
        ag_in = s_ld.inc(SY.dma_start(out=cc_in_d[:, 512:520], in_=Gamma_sb[:, :]), 16)
        GP.wait_ge(s_ld.h, ag_in)
        cc_mark = s_cc.inc(GP.collective_compute(
            "AllGather", ALU.bypass,
            replica_groups=[[0, 1, 2, 3], [4, 5, 6, 7]],
            ins=[cc_in_d[:, :].opt()],
            outs=[cc_out_d[:, :].opt()],
        ))

    barrier()
    P1.close()

    P2 = Scope(nc)
    agland = P2.sbuf("agland", [128, 4 * 520], F32)
    Sinit_sb = P2.sbuf("Sinit_sb", [128, 512], F32)
    Srun_sb = P2.sbuf("Srun_sb", [128, 512], F32)
    St_sb = [P2.sbuf(f"St_sb{i}", [128, 512], F32) for i in range(2)]
    St_bf = [P2.sbuf(f"St_bf{i}", [128, 512], BF16) for i in range(2)]
    x2_sb = P2.sbuf("x2", [128, 4 * E], F32)
    ot_sb = P2.sbuf("ot", [128, 4 * E], F32)
    sg2_sb = P2.sbuf("sg2", [128, 4 * E], F32)
    qt2_sb = [P2.sbuf("qt2e", [128, 8 * TILE], BF16), P2.sbuf("qt2o", [128, 8 * TILE], BF16)]
    xp_sb = P2.sbuf("xp", [128, 4 * E], F32)
    on_bf = [P2.sbuf(f"on_bf{i}", [128, E], BF16) for i in range(2)]
    onT_sb = [P2.sbuf(f"onT_sb{i}", [128, 8 * C], BF16) for i in range(2)]
    o2_sb = [P2.sbuf(f"o2_sb{i}", [128, E], F32) for i in range(2)]
    osq_sb = P2.sbuf("osq_sb", [128, E], F32)
    ln2_bf = [P2.sbuf(f"ln2_bf{i}", [128, E], BF16) for i in range(2)]
    ln2T_sb = P2.sbuf("ln2T_sb", [128, 8 * TILE], BF16)
    probs_sb = P2.sbuf("probs_sb", [128, 4 * 16], F32)
    mx8_sb = P2.sbuf("mx8_sb", [128, 8], F32)
    rt_sc = P2.sbuf("rt_sc", [128, 8], F32)
    comb_sb = P2.sbuf("comb_sb", [128, 16], F32)
    combT_tile = P2.sbuf("combT_tile", [16, 4 * C], F32)
    rms_sc = P2.sbuf("rms_sc", [128, 96], F32)

    if aux and use_ag:
        SY.wait_ge(s_gp.h, gp_wo)
        s_st.inc(SY.dma_start(out=aux_d[1024:1152, 0:4096], in_=Wo_sb[:, :].bitcast(F32)), 16)
        SY.wait_ge(s_gp.h, gp_gateT)
        s_st.inc(SY.dma_start(out=aux_d[1280:1408, 0:64], in_=gateT_sb[:, :].bitcast(F32)), 16)

    def tile_loads(T):
        s_ld.inc(SY.dma_start(out=x2_sb.rearrange("p (c e) -> p c e", c=4),
                     in_=x_in[T * TILE:(T + 1) * TILE, :].rearrange("(c p) e -> p c e", p=128)), 16)
        s_ld.inc(SY.dma_start(out=ot_sb.rearrange("p (c e) -> p c e", c=4),
                     in_=olocal_d[T * TILE:(T + 1) * TILE, :].rearrange("(c p) e -> p c e", p=128)), 16)
        s_ld.inc(SY.dma_start(out=sg2_sb.rearrange("p (c e) -> p c e", c=4),
                     in_=sgate_d[T * TILE:(T + 1) * TILE, :].rearrange("(c p) e -> p c e", p=128)), 16)
        s_ld.inc(SY.dma_start(out=qt2_sb[0][:, :], in_=qte_d[:, T * 8 * TILE:(T + 1) * 8 * TILE]), 16)
        return s_ld.inc(SY.dma_start(out=qt2_sb[1][:, :],
                                     in_=qto_d[:, T * 8 * TILE:(T + 1) * 8 * TILE]), 16)

    ld2_t0 = tile_loads(0)

    if use_ag:
        SY.wait_ge(s_cc.h, cc_mark)
        ag_out = None
        for j in range(4):
            ag_out = s_ld.inc(SY.dma_start(out=agland[:, j * 520:(j + 1) * 520],
                                           in_=cc_out_d[j * 128:(j + 1) * 128, :]), 16)
        VE.wait_ge(s_ld.h, ag_out)
        VE.memset(Srun_sb[:, :], 0.0)
        VE.memset(Sinit_sb[:, :], 0.0)
        for j in range(4):
            Lj = agland[:, j * 520: j * 520 + 512]
            Gj = agland[:, j * 520 + 512: j * 520 + 520]
            VE.scalar_tensor_tensor(Sinit_sb[:, :], Srun_sb[:, :], emask_sb[:, j:j + 1],
                                    Sinit_sb[:, :], ALU.mult, ALU.add)
            for b in range(8):
                VE.scalar_tensor_tensor(Srun_sb[:, b * D:(b + 1) * D], Srun_sb[:, b * D:(b + 1) * D],
                                        Gj[:, b:b + 1], Lj[:, b * D:(b + 1) * D], ALU.mult, ALU.add)
        v_sinit = mark_last("vector", s_v)
    else:
        VE.memset(Sinit_sb[:, :], 0.0)
        v_sinit = mark_last("vector", s_v)

    if aux and use_ag:
        SY.wait_ge(s_v.h, v_sinit)
        s_st.inc(SY.dma_start(out=aux_d[0:128, 0:512], in_=Sinit_sb[:, :]), 16)
        s_st.inc(SY.dma_start(out=aux_d[0:128, 512:516], in_=emask_sb[:, :]), 16)
        s_st.inc(SY.dma_start(out=aux_d[0:128, 516:2596], in_=agland[:, :]), 16)

    if stop_after == "sent":
        sentinel(1, 17.0)
    st_p2 = [None] * ntile
    buf2_free = v_sinit
    buf2_te = 0
    pT_free2 = v_const
    pS_free2 = v_const
    pAB_free2 = 0
    corr_par_te = [0, 0]
    corr_par_ve = [0, 0]
    onbf_free = [0, 0]
    ln2bf_free = [0, 0]

    TE.wait_ge(s_gp.h, gp_wo)
    TE.wait_ge(s_gp.h, gp_gateT)
    for T in range(ntile):
        if T == 0:
            ld2 = ld2_t0
        else:
            SY.wait_ge(s_v.h, buf2_free)
            if buf2_te > 0:
                SY.wait_ge(s_t.h, buf2_te)
            ld2 = tile_loads(T)

        v2_ln2T = [None] * NCT
        v2_xp = [None] * NCT
        m_lg = [None] * NCT
        m_ssq16_arr = [None] * NCT
        m_rms_arr = [None] * NCT
        v_on_arr = [None] * NCT
        m_ssq_arr = [None] * NCT
        m_var_arr = [None] * NCT
        m_sqrt_arr = [None] * NCT
        v_ln2_arr = [None] * NCT

        def front(c):
            nonlocal corr_par_te, corr_par_ve
            p2 = c % 2
            g = T * NCT + c
            if c == 0:
                VE.wait_ge(s_ld.h, ld2)
            if corr_par_te[p2] > 0:
                VE.wait_ge(s_t.h, corr_par_te[p2])
            if c >= 2 and m_rms_arr[c - 2] is not None:
                VE.wait_ge(s_s.h, m_rms_arr[c - 2])
            for blk in range(8):
                VE.tensor_scalar_mul(St_sb[p2][:, blk * D:(blk + 1) * D],
                                     Sinit_sb[:, blk * D:(blk + 1) * D],
                                     Gres_sb[:, g * 8 + blk: g * 8 + blk + 1])
            v_St = s_v.inc(VE.tensor_copy(St_bf[p2][:, :], St_sb[p2][:, :]))
            TE.wait_ge(s_v.h, v_St)
            if c == 0:
                TE.wait_ge(s_ld.h, ld2)
            free_m = corr_par_ve[p2]
            for r in range(2):
                if free_m > 0:
                    TE.wait_ge(s_v.h, free_m)
                for i in range(8):
                    h = 8 * r + i
                    j, par = h // 2, h % 2
                    if p2 == 0:
                        dst = pO[:, i * 128: i * 128 + D]
                    else:
                        dst = pP[i // 4][:, (i % 4) * 128: (i % 4) * 128 + D]
                    TE.matmul(dst,
                              qt2_sb[par][:, j * TILE + c * 128: j * TILE + (c + 1) * 128],
                              St_bf[p2][:, j * D:(j + 1) * D],
                              start=True, stop=True)
                m_corr = mark_last("tensor", s_t)
                VE.wait_ge(s_t.h, m_corr)
                if p2 == 0:
                    VE.tensor_add(o2_sb[p2][:, :].rearrange("p (h d) -> p h d", h=16)[:, 8 * r:8 * r + 8, :],
                                  ot_sb[:, c * E:(c + 1) * E].rearrange("p (h d) -> p h d", h=16)[:, 8 * r:8 * r + 8, :],
                                  pO[:, :].rearrange("p (i s) -> p i s", i=8)[:, :, 0:D])
                else:
                    for half in range(2):
                        hsl = slice(8 * r + 4 * half, 8 * r + 4 * half + 4)
                        VE.tensor_add(o2_sb[p2][:, :].rearrange("p (h d) -> p h d", h=16)[:, hsl, :],
                                      ot_sb[:, c * E:(c + 1) * E].rearrange("p (h d) -> p h d", h=16)[:, hsl, :],
                                      pP[half][:, :].rearrange("p (i s) -> p i s", i=4)[:, :, 0:D])
                free_m = mark_last("vector", s_v)
            corr_par_te[p2] = m_corr
            corr_par_ve[p2] = free_m
            VE.tensor_mul(o2_sb[p2][:, :], o2_sb[p2][:, :], sg2_sb[:, c * E:(c + 1) * E])
            VE.tensor_mul(osq_sb[:, :], o2_sb[p2][:, :], o2_sb[p2][:, :])
            m_ssq16_arr[c] = s_v.inc(VE.reduce_sum(rms_sc[:, 48 * p2: 48 * p2 + 16],
                                                   osq_sb[:, :].rearrange("p (h d) -> p h d", h=16),
                                                   axis=AX.X))

        def b_rms(c):
            SC.wait_ge(s_v.h, m_ssq16_arr[c])
            rb = 48 * (c % 2)
            m_rms_arr[c] = s_s.inc(SC.activation(rms_sc[:, rb + 16: rb + 32], rms_sc[:, rb: rb + 16],
                                                 AF.Sqrt, scale=1.0 / D, bias=eps_col[:, 0:1]))

        def b_on(c):
            p2 = c % 2
            rb = 48 * p2
            VE.wait_ge(s_s.h, m_rms_arr[c])
            VE.reciprocal(rms_sc[:, rb + 32: rb + 48], rms_sc[:, rb + 16: rb + 32])
            if isinstance(onbf_free[p2], int) and onbf_free[p2] > 0:
                VE.wait_ge(s_t.h, onbf_free[p2])
            for h in range(H):
                VE.scalar_tensor_tensor(on_bf[p2][:, h * D:(h + 1) * D],
                                        o2_sb[p2][:, h * D:(h + 1) * D], rms_sc[:, rb + 32 + h: rb + 33 + h],
                                        nwbc_sb[:, h * D:(h + 1) * D], ALU.mult, ALU.mult)
            v_on_arr[c] = mark_last("vector", s_v)

        def b_onT(c):
            nonlocal pT_free2
            p2 = c % 2
            TE.wait_ge(s_v.h, v_on_arr[c])
            TE.wait_ge(s_v.h, pT_free2)
            for j in range(8):
                TE.transpose(pT[:, j * 128:(j + 1) * 128],
                             on_bf[p2][:, j * 128:(j + 1) * 128], ident_bf[:, :])
            m_onT = mark_last("tensor", s_t)
            VE.wait_ge(s_t.h, m_onT)
            VE.tensor_copy(onT_sb[p2][:, :], pT[:, :])
            pT_free2 = mark_last("vector", s_v)
            onbf_free[p2] = m_onT

        def b_wo(c):
            nonlocal pAB_free2
            p2 = c % 2
            TE.wait_ge(s_v.h, pT_free2)
            if pAB_free2 > 0:
                TE.wait_ge(s_v.h, pAB_free2)
            for nh, bank in ((0, pA), (1, pB)):
                for j in range(8):
                    TE.matmul(bank[:, :],
                              onT_sb[p2][:, j * 128:(j + 1) * 128],
                              Wo_sb[:, j * E + nh * 512: j * E + (nh + 1) * 512],
                              start=(j == 0), stop=(j == 7))
            m_wo = mark_last("tensor", s_t)
            VE.wait_ge(s_t.h, m_wo)
            if T >= 1 and c == 0:
                VE.wait_ge(s_st.h, st_p2[T - 1])
            VE.tensor_add(xp_sb[:, c * E: c * E + 512], x2_sb[:, c * E: c * E + 512], pA[:, :])
            VE.tensor_add(xp_sb[:, c * E + 512:(c + 1) * E], x2_sb[:, c * E + 512:(c + 1) * E], pB[:, :])
            pAB_free2 = mark_last("vector", s_v)
            v2_xp[c] = pAB_free2

        def b_ln2stats(c):
            lb = 16 * (c % 2)
            xsl = xp_sb[:, c * E:(c + 1) * E]
            SC.wait_ge(s_v.h, v2_xp[c])
            SC.activation(sq_scr[:, :], xsl, AF.Square, accum_out=lnS[:, lb + 2: lb + 3])
            m_ssq_arr[c] = mark_last("scalar", s_s)
            VE.reduce_sum(lnS[:, lb + 0: lb + 1], xsl, axis=AX.X)
            VE.tensor_scalar_mul(lnS[:, lb + 1: lb + 2], lnS[:, lb + 0: lb + 1], 1.0 / E)

        def b_ln2var(c):
            lb = 16 * (c % 2)
            VE.wait_ge(s_s.h, m_ssq_arr[c])
            VE.tensor_mul(lnS[:, lb + 5: lb + 6], lnS[:, lb + 1: lb + 2], lnS[:, lb + 1: lb + 2])
            VE.tensor_scalar(lnS[:, lb + 6: lb + 7], lnS[:, lb + 2: lb + 3], 1.0 / E, None, ALU.mult)
            VE.tensor_sub(lnS[:, lb + 7: lb + 8], lnS[:, lb + 6: lb + 7], lnS[:, lb + 5: lb + 6])
            m_var_arr[c] = s_v.inc(VE.tensor_scalar(lnS[:, lb + 3: lb + 4], lnS[:, lb + 7: lb + 8],
                                                    LN_EPS, None, ALU.add))

        def b_ln2sqrt(c):
            lb = 16 * (c % 2)
            SC.wait_ge(s_v.h, m_var_arr[c])
            m_sqrt_arr[c] = s_s.inc(SC.activation(lnS[:, lb + 4: lb + 5], lnS[:, lb + 3: lb + 4], AF.Sqrt))

        def b_ln2scale(c):
            p2 = c % 2
            lb = 16 * p2
            xsl = xp_sb[:, c * E:(c + 1) * E]
            VE.wait_ge(s_s.h, m_sqrt_arr[c])
            VE.reciprocal(lnS[:, lb + 8: lb + 9], lnS[:, lb + 4: lb + 5])
            VE.tensor_mul(lnS[:, lb + 9: lb + 10], lnS[:, lb + 1: lb + 2], lnS[:, lb + 8: lb + 9])
            if isinstance(ln2bf_free[p2], int) and ln2bf_free[p2] > 0:
                VE.wait_ge(s_t.h, ln2bf_free[p2])
            v_ln2_arr[c] = s_v.inc(VE.tensor_scalar(ln2_bf[p2][:, :], xsl,
                                                    lnS[:, lb + 8: lb + 9], lnS[:, lb + 9: lb + 10],
                                                    ALU.mult, ALU.subtract))

        def b_ln2T(c):
            nonlocal pT_free2
            p2 = c % 2
            TE.wait_ge(s_v.h, v_ln2_arr[c])
            TE.wait_ge(s_v.h, pT_free2)
            for j in range(8):
                TE.transpose(pT[:, j * 128:(j + 1) * 128],
                             ln2_bf[p2][:, j * 128:(j + 1) * 128], ident_bf[:, :])
            m_l2T = mark_last("tensor", s_t)
            VE.wait_ge(s_t.h, m_l2T)
            if T >= 1 and c == 0:
                VE.wait_ge(s_st.h, st_p2[T - 1])
            VE.tensor_copy(
                ln2T_sb[:, :].rearrange("p (j s) -> p j s", j=8)[:, :, c * 128:(c + 1) * 128],
                pT[:, :].rearrange("p (j s) -> p j s", j=8))
            pT_free2 = mark_last("vector", s_v)
            ln2bf_free[p2] = m_l2T
            v2_ln2T[c] = pT_free2

        def b_probs(c):
            nonlocal pS_free2
            TE.wait_ge(s_v.h, v2_ln2T[c])
            if c == 0:
                TE.wait_ge(s_v.h, pS_free2)
            for j in range(8):
                TE.matmul(pS[:, c * 128: c * 128 + 16],
                          ln2T_sb[:, j * TILE + c * 128: j * TILE + (c + 1) * 128],
                          gateT_sb[:, j * NEXP:(j + 1) * NEXP],
                          start=(j == 0), stop=(j == 7))
            m_lg[c] = mark_last("tensor", s_t)

        for pair in range(NCT // 2):
            c0, c1 = 2 * pair, 2 * pair + 1
            front(c0)
            front(c1)
            b_rms(c0); b_rms(c1)
            b_on(c0); b_on(c1)
            b_onT(c0); b_onT(c1)
            b_wo(c0); b_wo(c1)
            b_ln2stats(c0); b_ln2stats(c1)
            b_ln2var(c0); b_ln2var(c1)
            b_ln2sqrt(c0); b_ln2sqrt(c1)
            b_ln2scale(c0); b_ln2scale(c1)
            b_ln2T(c0); b_ln2T(c1)
            b_probs(c0); b_probs(c1)

        VE.wait_ge(s_t.h, m_lg[NCT - 1])
        VE.reduce_max(rt_sc[:, 0:4], pS[:, 0:512].rearrange("p (c k) -> p c k", c=4)[:, :, 0:16],
                      axis=AX.X, negate=True)
        for c in range(NCT):
            VE.tensor_scalar(probs_sb[:, c * 16:(c + 1) * 16], pS[:, c * 128: c * 128 + 16],
                             rt_sc[:, c:c + 1], None, ALU.add)
        m_pre = mark_last("vector", s_v)
        SC.wait_ge(s_v.h, m_pre)
        m_exp = s_s.inc(SC.activation(probs_sb[:, :], probs_sb[:, :], AF.Exp))
        VE.wait_ge(s_s.h, m_exp)
        if T >= 1:
            VE.wait_ge(s_st.h, st_p2[T - 1])
        for c in range(NCT):
            psl = probs_sb[:, c * 16:(c + 1) * 16]
            VE.max(out=mx8_sb[:, :], in_=psl)
            VE.drain()
            VE.tensor_scalar(comb_sb[:, :], psl, mx8_sb[:, 3:4], None, ALU.is_ge)
            VE.tensor_mul(comb_sb[:, :], comb_sb[:, :], psl)
            VE.reduce_sum(rt_sc[:, 4:5], comb_sb[:, :], axis=AX.X)
            VE.reciprocal(rt_sc[:, 5:6], rt_sc[:, 4:5])
            m_cb = s_v.inc(VE.tensor_scalar_mul(comb_sb[:, :], comb_sb[:, :], rt_sc[:, 5:6]))
            TE.wait_ge(s_v.h, m_cb)
            if c > 0:
                TE.wait_ge(s_v.h, pS_free2)
            m_cT = s_t.inc(TE.transpose(pS[0:16, 0:128], comb_sb[:, :], ident_f32[:, :]))
            VE.wait_ge(s_t.h, m_cT)
            VE.tensor_copy(combT_tile[:, c * 128:(c + 1) * 128], pS[0:16, 0:128])
            pS_free2 = mark_last("vector", s_v)

        SY.wait_ge(s_v.h, v2_xp[NCT - 1])
        s_st.inc(SY.dma_start(out=xp_d[T * TILE:(T + 1) * TILE, :].rearrange("(c p) e -> p c e", p=128),
                              in_=xp_sb.rearrange("p (c e) -> p c e", c=4)), 16)
        SY.wait_ge(s_v.h, v2_ln2T[NCT - 1])
        s_st.inc(SY.dma_start(out=ln2T_d[:, T * 8 * TILE:(T + 1) * 8 * TILE], in_=ln2T_sb[:, :]), 16)
        SY.wait_ge(s_v.h, pS_free2)
        st_p2[T] = s_st.inc(SY.dma_start(out=combT_d[:, T * TILE:(T + 1) * TILE], in_=combT_tile[:, :]), 16)
        buf2_free = pAB_free2
        buf2_te = corr_par_te[1]

    if aux and use_ag:
        SY.wait_ge(s_st.h, s_st.n)
        aux_srcs = [
            (128, xp_d[0:128, 0:1024], False),
            (256, xp_d[2048:2176, 0:1024], False),
            (384, ln2T_d[:, 0:2048], True),
            (640, olocal_d[0:128, 0:1024], False),
            (768, sgate_d[0:128, 0:1024], False),
            (896, qte_d[:, 0:2048], True),
        ]
        aux_last = None
        for row, src, isbf in aux_srcs:
            if aux_last is not None:
                SY.wait_ge(s_st.h, aux_last)
            dst = x2_sb[:, 0:1024].bitcast(BF16) if isbf else x2_sb[:, 0:1024]
            m = s_ld.inc(SY.dma_start(out=dst, in_=src), 16)
            SY.wait_ge(s_ld.h, m)
            aux_last = s_st.inc(SY.dma_start(out=aux_d[row:row + 128, 0:1024], in_=x2_sb[:, 0:1024]), 16)
        SY.wait_ge(s_st.h, aux_last)
        m = s_ld.inc(SY.dma_start(out=x2_sb[0:16, 0:4096], in_=combT_d[0:16, 0:4096]), 16)
        SY.wait_ge(s_ld.h, m)
        s_st.inc(SY.dma_start(out=aux_d[512:528, 0:4096], in_=x2_sb[0:16, 0:4096]), 16)
        s_st.inc(SY.dma_start(out=aux_d[1152:1280, 0:4096], in_=Wo_sb[:, :].bitcast(F32)), 16)
        s_st.inc(SY.dma_start(out=aux_d[1408:1536, 0:256], in_=Gres_sb[:, :]), 16)

    if stop_after in ("probe_ol", "probe_sg", "probe_qt", "probe_gres", "probe_sinit"):
        SY.wait_ge(s_v.h, v_sinit)
        src_t = {"probe_ol": olocal_d, "probe_sg": sgate_d}.get(stop_after)
        if src_t is not None:
            for u in range(4):
                m = s_ld.inc(SY.dma_start(out=x2_sb[:, 0:E], in_=src_t[u * 128:(u + 1) * 128, :]), 16)
                SY.wait_ge(s_ld.h, m)
                s_st.inc(SY.dma_start(out=out_d[u * 128:(u + 1) * 128, :], in_=x2_sb[:, 0:E]), 16)
        elif stop_after == "probe_sinit":
            VE.wait_ge(s_ld.h, s_ld.n)
            mv = s_v.inc(VE.tensor_copy(x2_sb[:, 0:512], Sinit_sb[:, :]))
            VE.tensor_copy(x2_sb[:, 512:516], emask_sb[:, :])
            mv = mark_last("vector", s_v)
            SY.wait_ge(s_v.h, mv)
            stj = s_st.inc(SY.dma_start(out=out_d[0:128, 0:516], in_=x2_sb[:, 0:516]), 16)
            for j in range(4):
                VE.wait_ge(s_st.h, stj)
                mv = s_v.inc(VE.tensor_copy(x2_sb[:, 0:520], agland[:, j * 520:(j + 1) * 520]))
                SY.wait_ge(s_v.h, mv)
                stj = s_st.inc(SY.dma_start(out=out_d[128 * (1 + j):128 * (2 + j), 0:520],
                                            in_=x2_sb[:, 0:520]), 16)
            m = s_ld.inc(SY.dma_start(out=x2_sb[:, 0:520], in_=cc_in_d[:, :]), 16)
            SY.wait_ge(s_ld.h, m)
            s_st.inc(SY.dma_start(out=out_d[128 * 5:128 * 6, 0:520], in_=x2_sb[:, 0:520]), 16)
        elif stop_after == "probe_qt":
            m = s_ld.inc(SY.dma_start(out=qt2_sb[0][:, :], in_=qte_d[:, 0:8 * TILE]), 16)
            SY.wait_ge(s_ld.h, m)
            VE.wait_ge(s_ld.h, m)
            mv = s_v.inc(VE.tensor_copy(x2_sb[:, 0:E], qt2_sb[0][:, 0:1024]))
            SY.wait_ge(s_v.h, mv)
            s_st.inc(SY.dma_start(out=out_d[0:128, :], in_=x2_sb[:, 0:E]), 16)
        else:
            mv = s_v.inc(VE.tensor_copy(x2_sb[:, 0:8 * nchunk], Gres_sb[:, :]))
            SY.wait_ge(s_v.h, mv)
            s_st.inc(SY.dma_start(out=out_d[0:128, 0:8 * nchunk], in_=x2_sb[:, 0:8 * nchunk]), 16)
        SY.wait_ge(s_st.h, s_st.n)
        return finish([P2])
    if stop_after == "p2a":
        sentinel(1, 11.0, (s_v, s_v.n))
        VE.wait_ge(s_s.h, s_s.n)
        VE.wait_ge(s_t.h, s_t.n)
        sentinel(2, 11.0)
        return finish([P2])

    barrier()
    P2.close()

    P3 = Scope(nc)
    ln2T_ft = P3.sbuf("ln2T_ft", [128, 8 * FT], BF16)
    moe_acc = P3.sbuf("moe_acc", [128, nsub * E], F32)
    w1e_sb = [P3.sbuf(f"w1e{i}", [128, 8 * HID], BF16) for i in range(2)]
    w3e_sb = [P3.sbuf(f"w3e{i}", [128, 8 * HID], BF16) for i in range(2)]
    w2e_sb = [P3.sbuf(f"w2e{i}", [128, 4 * E], BF16) for i in range(4)]
    w2s_sb = P3.sbuf("w2s_sb", [128, 4 * E], BF16)
    cb_sb = [P3.sbuf(f"cb{i}", [128, FT], F32) for i in range(2)]
    act_sb = [P3.sbuf(f"act{i}", [128, 4 * FT], BF16) for i in range(5)]
    silu_scr = [P3.sbuf(f"silu{i}", [128, 512], BF16) for i in range(2)]
    silu2_scr = [P3.sbuf(f"silu2{i}", [128, 512], BF16) for i in range(2)]

    GROUPS = [[0, 1, 2, 3], [4, 5, 6, 7], [8, 9, 10, 11], [12, 13, 14, 15, None]]

    moe_free = s_v.n
    ln2ft_free = 0
    w13_free_wb = [0, 0]
    cb_free_wb = [0, 0]
    w2_free = 0
    act_free = 0
    st_out = [None] * nftile

    for FTi in range(nftile):
        T0 = FTi * (FT // TILE)
        if FTi >= 1:
            SY.wait_ge(s_t.h, ln2ft_free)
        for th2 in range(FT // TILE):
            s_ld.inc(SY.dma_start(
                out=ln2T_ft[:, :].rearrange("p (j s) -> p j s", j=8)[:, :, th2 * TILE:(th2 + 1) * TILE],
                in_=ln2T_d[:, (T0 + th2) * 8 * TILE:(T0 + th2 + 1) * 8 * TILE]
                .rearrange("p (j s) -> p j s", j=8)), 16)
        if FTi >= 1:
            SY.wait_ge(s_st.h, st_out[FTi - 1])
        ld_ft = s_ld.inc(SY.dma_start(
            out=moe_acc.rearrange("p (u e) -> p u e", e=E),
            in_=xp_d[FTi * FT:(FTi + 1) * FT, :].rearrange("(u p) e -> p u e", p=128)),
            16)

        if stop_after == "f0" and FTi == 0:
            VE.wait_ge(s_ld.h, ld_ft)
            sentinel(1, 13.0)
            return finish([P3])
        if stop_after == "f0x" and FTi == 0:
            SY.wait_ge(s_ld.h, ld_ft)
            for u in range(4):
                s_st.inc(SY.dma_start(out=out_d[u * 128:(u + 1) * 128, :],
                                      in_=moe_acc[:, u * E:(u + 1) * E]), 16)
            VE.wait_ge(s_st.h, s_st.n)
            return finish([P3])
        hstep = 0
        bank_act_mark = {0: 0, 1: 0}
        for gi, grp in enumerate(GROUPS):
            act_marks = {}
            ldw13 = {}
            ldw2 = {}
            ldcb = {}
            for ei, e in enumerate(grp):
                shared = e is None
                wb = (e % 2) if not shared else 0
                if shared:
                    GP.wait_ge(s_t.h, w13_free_wb[wb])
                    s_gp.inc(GP.dma_start(out=w1e_sb[wb].rearrange("p (a n) -> p a n", a=8), in_=sw1T_in.rearrange("(a p) n -> p a n", p=128)), 16)
                    ldw13[ei] = s_gp.inc(
                        GP.dma_start(out=w3e_sb[wb].rearrange("p (a n) -> p a n", a=8), in_=sw3T_in.rearrange("(a p) n -> p a n", p=128)), 16)
                    GP.wait_ge(s_t.h, w2_free)
                    ldw2[ei] = s_gp.inc(
                        GP.dma_start(out=w2s_sb.rearrange("p (a n) -> p a n", a=4), in_=sw2T_in.rearrange("(a p) n -> p a n", p=128)), 16)
                else:
                    SY.wait_ge(s_t.h, w13_free_wb[wb])
                    s_ld.inc(SY.dma_start(out=w1e_sb[wb].rearrange("p (a n) -> p a n", a=8), in_=w1b_d[e].rearrange("(a p) n -> p a n", p=128)), 16)
                    ldw13[ei] = s_ld.inc(
                        SY.dma_start(out=w3e_sb[wb].rearrange("p (a n) -> p a n", a=8), in_=w3b_d[e].rearrange("(a p) n -> p a n", p=128)), 16)
                    SY.wait_ge(s_t.h, w2_free)
                    ldw2[ei] = s_ld.inc(
                        SY.dma_start(out=w2e_sb[ei].rearrange("p (a n) -> p a n", a=4), in_=w2b_d[e].rearrange("(a p) n -> p a n", p=128)), 16)
                    GP.wait_ge(s_v.h, cb_free_wb[wb])
                    ldcb[ei] = s_gp.inc(GP.dma_start(out=cb_sb[wb][:, :],
                                                     in_=combT_d[e:e + 1, FTi * FT:(FTi + 1) * FT]
                                                     .broadcast_to([128, FT])), 16)
                w1 = w1e_sb[wb]
                w3 = w3e_sb[wb]
                act = act_sb[ei]
                TE.wait_ge(s_gp.h if shared else s_ld.h, ldw13[ei])
                TE.wait_ge(s_ld.h, ld_ft)
                first_of_expert = True
                for jj in range(4):
                    for th in range(nth):
                        bk = hstep % 2
                        hstep += 1
                        ph1 = pA if bk == 0 else pB
                        ph3 = pP[0] if bk == 0 else pP[1]
                        TE.wait_ge(s_v.h, bank_act_mark[bk])
                        for dj in range(8):
                            TE.matmul(ph1[:, :],
                                      w1[:, dj * HID + jj * 128: dj * HID + (jj + 1) * 128],
                                      ln2T_ft[:, dj * FT + th * 512: dj * FT + (th + 1) * 512],
                                      start=(dj == 0), stop=(dj == 7))
                        for dj in range(8):
                            TE.matmul(ph3[:, :],
                                      w3[:, dj * HID + jj * 128: dj * HID + (jj + 1) * 128],
                                      ln2T_ft[:, dj * FT + th * 512: dj * FT + (th + 1) * 512],
                                      start=(dj == 0), stop=(dj == 7))
                        m_h = mark_last("tensor", s_t)
                        SC.wait_ge(s_t.h, m_h)
                        SC.wait_ge(s_v.h, bank_act_mark[bk])
                        m_si = s_s.inc(SC.activation(silu_scr[bk][:, :], ph1[:, :], AF.Silu))
                        VE.wait_ge(s_s.h, m_si)
                        if shared:
                            VE.tensor_copy(silu2_scr[bk][:, :], ph3[:, :])
                        else:
                            VE.wait_ge(s_gp.h, ldcb[ei])
                            VE.tensor_mul(silu2_scr[bk][:, :], ph3[:, :],
                                          cb_sb[wb][:, th * 512:(th + 1) * 512])
                        if first_of_expert and act_free > 0:
                            VE.wait_ge(s_t.h, act_free)
                        first_of_expert = False
                        m_act = s_v.inc(VE.tensor_mul(act[:, jj * FT + th * 512: jj * FT + (th + 1) * 512],
                                                      silu_scr[bk][:, :], silu2_scr[bk][:, :]))
                        bank_act_mark[bk] = m_act
                        act_marks[(ei, jj, th)] = m_act
                w13_free_wb[wb] = s_t.n
                if not shared:
                    cb_free_wb[wb] = s_v.n
            if stop_after == "f1" and FTi == 0 and gi == 0:
                VE.wait_ge(s_s.h, s_s.n)
                VE.wait_ge(s_t.h, s_t.n)
                sentinel(1, 13.0)
                return finish([P3])
            for ei, e in enumerate(grp):
                TE.wait_ge(s_v.h, act_marks[(ei, 3, nth - 1)])
                TE.wait_ge(s_gp.h if e is None else s_ld.h, ldw2[ei])
            for u in range(nsub):
                TE.wait_ge(s_v.h, moe_free)
                first_half = [True, True]
                for ei, e in enumerate(grp):
                    w2 = w2s_sb if e is None else w2e_sb[ei]
                    act = act_sb[ei]
                    for jj in range(4):
                        last = (ei == len(grp) - 1) and (jj == 3)
                        for nh2 in range(2):
                            TE.matmul(pO[:, nh2 * 512:(nh2 + 1) * 512],
                                      act[:, jj * FT + u * 128: jj * FT + (u + 1) * 128],
                                      w2[:, jj * E + nh2 * 512: jj * E + (nh2 + 1) * 512],
                                      start=first_half[nh2], stop=last)
                            first_half[nh2] = False
                m_eo = mark_last("tensor", s_t)
                VE.wait_ge(s_t.h, m_eo)
                VE.wait_ge(s_ld.h, ld_ft)
                moe_free = s_v.inc(VE.tensor_add(moe_acc[:, u * E:(u + 1) * E],
                                                 moe_acc[:, u * E:(u + 1) * E], pO[:, :]))
            if stop_after in ("f2", "f2g1", "f2g2", "f2g3") and FTi == 0 and gi == {"f2": 0, "f2g1": 1, "f2g2": 2, "f2g3": 3}[stop_after]:
                VE.wait_ge(s_s.h, s_s.n)
                VE.wait_ge(s_t.h, s_t.n)
                sentinel(1, 13.0)
                return finish([P3])
            act_free = s_t.n
            w2_free = s_t.n
        if stop_after == "f3" and FTi == 0:
            VE.wait_ge(s_s.h, s_s.n)
            VE.wait_ge(s_t.h, s_t.n)
            sentinel(1, 13.0)
            return finish([P3])
        if stop_after == "f3s" and FTi == 0:
            SY.wait_ge(s_v.h, moe_free)
            s_st.inc(SY.dma_start(out=out_d[0:128, :], in_=moe_acc[:, 0:E]), 16)
            VE.wait_ge(s_st.h, s_st.n)
            sentinel(1, 13.0)
            return finish([P3])
        SY.wait_ge(s_v.h, moe_free)
        for u in range(nsub):
            st_out[FTi] = s_st.inc(SY.dma_start(
                out=out_d[FTi * FT + u * 128: FTi * FT + (u + 1) * 128, :],
                in_=moe_acc[:, u * E:(u + 1) * E]), 16)
        ln2ft_free = s_t.n

    return finish([P3])


def _host_inputs(inputs, core, tpc, use_ag):
    x = np.asarray(inputs["L0_attn_x"])
    if use_ag:
        b, q = core // 4, core % 4
        xs = np.ascontiguousarray(x[b].reshape(-1, E)[q * tpc:(q + 1) * tpc], dtype=np.float32)
    else:
        q = 0
        xs = np.ascontiguousarray(np.asarray(x, np.float32).reshape(-1, E)[:tpc])
    return {
        "x": xs,
        "Wq": np.ascontiguousarray(inputs["L0_attn_Wq"], np.float32),
        "Wk": np.ascontiguousarray(inputs["L0_attn_Wk"], np.float32),
        "Wv": np.ascontiguousarray(inputs["L0_attn_Wv"], np.float32),
        "Wo": np.ascontiguousarray(inputs["L0_attn_Wo"], np.float32),
        "G1": np.ascontiguousarray(inputs["L0_attn_G1"], np.float32),
        "G2": np.ascontiguousarray(inputs["L0_attn_G2"], np.float32),
        "delta_c": np.ascontiguousarray(np.asarray(inputs["L0_attn_delta"], np.float32).reshape(8, 128).T),
        "normw_c": np.ascontiguousarray(np.asarray(inputs["L0_attn_norm_w"], np.float32).reshape(8, 128).T),
        "normw_row": np.ascontiguousarray(np.asarray(inputs["L0_attn_norm_w"], np.float32).reshape(1, E)),
        "gateT": np.ascontiguousarray(np.asarray(inputs["L0_ffn_gate_w"], np.float32).T),
        "w1T": np.ascontiguousarray(np.asarray(inputs["L0_ffn_w1"], np.float32).transpose(0, 2, 1)),
        "w3T": np.ascontiguousarray(np.asarray(inputs["L0_ffn_w3"], np.float32).transpose(0, 2, 1)),
        "w2T": np.ascontiguousarray(np.asarray(inputs["L0_ffn_w2"], np.float32).transpose(0, 2, 1)),
        "sw1T": np.ascontiguousarray(np.asarray(inputs["L0_ffn_sw1"], np.float32).T),
        "sw3T": np.ascontiguousarray(np.asarray(inputs["L0_ffn_sw3"], np.float32).T),
        "sw2T": np.ascontiguousarray(np.asarray(inputs["L0_ffn_sw2"], np.float32).T),
        "ident_bf": np.eye(128, dtype=ml_dtypes.bfloat16),
        "ident_f32": np.eye(128, dtype=np.float32),
        "mask_st": np.ascontiguousarray(np.triu(np.ones((128, 128), np.float32))),
        "emask": np.ascontiguousarray(np.repeat((np.arange(4) == q).astype(np.float32)[None, :], 128, axis=0)),
    }


def kernel(**inputs):
    tpc = 4096
    nc = build_program(tpc, use_ag=True)
    in_maps = [_host_inputs(inputs, c, tpc, True) for c in range(8)]
    res = run_bass_kernel_spmd(nc, in_maps, core_ids=list(range(8)))
    out = np.empty((2, 16384, E), np.float32)
    for c in range(8):
        b, q = c // 4, c % 4
        out[b, q * tpc:(q + 1) * tpc, :] = res.results[c]["out"]
    return out


# version 10 (final)
# speedup vs baseline: 1.0272x; 1.0164x over previous
import numpy as np
import ml_dtypes
import concourse.bass as bass
import concourse.mybir as mybir
from concourse.bass_utils import run_bass_kernel_spmd

F32 = mybir.dt.float32
BF16 = mybir.dt.bfloat16
AF = mybir.ActivationFunctionType
ALU = mybir.AluOpType
AX = mybir.AxisListType

E = 1024
D = 64
H = 16
HID = 512
NEXP = 16
C = 128
TILE = 512
NCT = TILE // C
LN_EPS = 1e-5
RMS_EPS = 1e-5


class _RecInstr:
    __slots__ = ("incs",)

    def __init__(self):
        self.incs = []

    def then_inc(self, sem_handle, k):
        self.incs.append((sem_handle, k))
        return self


_SEQ_OPS = ("wait_ge", "nop", "drain")


def _ap_tensor_names(x):
    t = getattr(x, "tensor", None)
    if t is not None:
        return [t.name]
    return []


class Rec:

    def __init__(self, queue, hazard=False):
        self._q = queue
        self._hazard = hazard
        self._prev_outs = set()

    def __getattr__(self, op):
        q = self._q
        rec = self

        def call(*a, **kw):
            if rec._hazard and op not in _SEQ_OPS:
                aps = []
                outs = set()
                ins = set()
                pos_aps = [x for x in a if getattr(x, "tensor", None) is not None]
                if "out" in kw and getattr(kw["out"], "tensor", None) is not None:
                    outs.add(kw["out"].tensor.name)
                    pos_in_start = 0
                elif pos_aps:
                    outs.add(pos_aps[0].tensor.name)
                    pos_aps = pos_aps[1:]
                for x in pos_aps:
                    ins.add(x.tensor.name)
                for kname, x in kw.items():
                    if getattr(x, "tensor", None) is not None:
                        if kname in ("out", "accum_out", "out_max", "out_indices"):
                            outs.add(x.tensor.name)
                        else:
                            ins.add(x.tensor.name)
                if rec._prev_outs & (ins | outs):
                    q.append(("drain", (), {}, _RecInstr()))
                rec._prev_outs = outs
            ri = _RecInstr()
            q.append((op, a, kw, ri))
            return ri

        return call


class Sem:
    def __init__(self, handle):
        self.h = handle
        self.n = 0

    def inc(self, instr, k=1):
        instr.then_inc(self.h, k)
        self.n += k
        return self.n


def _replay(engine, queue):
    for op, a, kw, ri in queue:
        instr = getattr(engine, op)(*a, **kw)
        for sem_handle, k in ri.incs:
            instr.then_inc(sem_handle, k)


class Scope:
    def __init__(self, nc):
        self.nc = nc
        self.items = []

    def sbuf(self, name, shape, dtype):
        t = self.nc.sbuf_tensor(name, shape, dtype)
        h = t.__enter__()
        self.items.append(t)
        return h

    def close(self):
        for t in reversed(self.items):
            t.__exit__(None, None, None)
        self.items = []


def build_program(tpc, use_ag, stop_after=None, debug_tap=None, paranoid=0, aux=False):
    ntile = tpc // TILE
    nchunk = tpc // C
    FT = min(1024, tpc)
    nftile = tpc // FT
    nsub = FT // 128
    nth = FT // 512

    nc = bass.Bass("TRN2", target_bir_lowering=False, detect_race_conditions=False)

    def din(name, shape, dtype=F32):
        return nc.dram_tensor(name, shape, dtype, kind="ExternalInput").ap()

    def dint(name, shape, dtype=F32):
        return nc.dram_tensor(name, shape, dtype, kind="Internal").ap()

    x_in = din("x", [tpc, E])
    Wq_in = din("Wq", [E, E])
    Wk_in = din("Wk", [E, E])
    Wv_in = din("Wv", [E, E])
    Wo_in = din("Wo", [E, E])
    G1_in = din("G1", [E, D])
    G2_in = din("G2", [D, E])
    delta_in = din("delta_c", [128, 8])
    normw_in = din("normw_c", [128, 8])
    normw_row_in = din("normw_row", [1, E])
    gateT_in = din("gateT", [E, NEXP])
    w1T_in = din("w1T", [NEXP, E, HID])
    w3T_in = din("w3T", [NEXP, E, HID])
    w2T_in = din("w2T", [NEXP, HID, E])
    sw1T_in = din("sw1T", [E, HID])
    sw3T_in = din("sw3T", [E, HID])
    sw2T_in = din("sw2T", [HID, E])
    ident_bf_in = din("ident_bf", [128, 128], BF16)
    ident_f32_in = din("ident_f32", [128, 128])
    mask_st_in = din("mask_st", [128, 128])
    emask_in = din("emask", [128, 4])

    out_d = nc.dram_tensor("out", [tpc, E], F32, kind="ExternalOutput").ap()
    aux_d = nc.dram_tensor("aux", [1536, 4096], F32, kind="ExternalOutput").ap() if aux else None

    olocal_d = dint("olocal_d", [tpc, E])
    sgate_d = dint("sgate_d", [tpc, E])
    qte_d = dint("qte_d", [128, 8 * tpc], BF16)
    qto_d = dint("qto_d", [128, 8 * tpc], BF16)
    xp_d = dint("xp_d", [tpc, E])
    ln2T_d = dint("ln2T_d", [128, 8 * tpc], BF16)
    combT_d = dint("combT_d", [NEXP, tpc])
    w1b_d = dint("w1b_d", [NEXP, E, HID], BF16)
    w3b_d = dint("w3b_d", [NEXP, E, HID], BF16)
    w2b_d = dint("w2b_d", [NEXP, HID, E], BF16)
    if use_ag:
        cc_in_d = dint("cc_in", [128, 520])
        cc_out_d = dint("cc_out", [512, 520])

    P0 = Scope(nc)
    delta_neg = P0.sbuf("delta_neg", [128, 8], F32)
    normw_sb = P0.sbuf("normw_sb", [128, 8], F32)
    ident_bf = P0.sbuf("ident_bf_sb", [128, 128], BF16)
    ident_f32 = P0.sbuf("ident_f32_sb", [128, 128], F32)
    mask_st = P0.sbuf("mask_st_sb", [128, 128], F32)
    emask_sb = P0.sbuf("emask_sb", [128, 4], F32)
    ones_col = P0.sbuf("ones_col", [128, 1], F32)
    eps_col = P0.sbuf("eps_col", [128, 1], F32)
    S_sb = P0.sbuf("S_sb", [128, 512], F32)
    S_bf = P0.sbuf("S_bf_sb", [128, 512], BF16)
    Gamma_sb = P0.sbuf("Gamma_sb", [128, 8], F32)
    Gres_sb = P0.sbuf("Gres_sb", [128, 8 * nchunk], F32)
    lnS = P0.sbuf("lnS", [128, 32], F32)
    sq_scr = P0.sbuf("sq_scr", [128, E], F32)
    nwbc_sb = P0.sbuf("nwbc_sb", [128, E], F32)
    Wo_sb = P0.sbuf("Wo_sb", [128, 8 * E], BF16)
    gateT_sb = P0.sbuf("gateT_sb", [128, 8 * NEXP], BF16)

    psum_items = []

    def psum(name, shape, dtype):
        t = nc.psum_tensor(name, shape, dtype)
        h = t.__enter__()
        psum_items.append(t)
        return h

    pA = psum("pA", [128, 512], F32)
    pB = psum("pB", [128, 512], F32)
    pT = psum("pT", [128, 1024], BF16)
    pP = [psum("pP0", [128, 512], F32), psum("pP1", [128, 512], F32)]
    pO = psum("pO", [128, 1024], F32)
    pS = psum("pS", [128, 512], F32)

    sem_items = []

    def sem(name):
        t = nc.semaphore(name)
        h = t.__enter__()
        sem_items.append(t)
        return Sem(h)

    s_ld = sem("s_ld")
    s_gp = sem("s_gp")
    s_t = sem("s_t")
    s_v = sem("s_v")
    s_s = sem("s_s")
    s_cc = sem("s_cc")
    s_st = sem("s_st")

    s_bar = sem("s_bar")

    Q = {k: [] for k in ("tensor", "vector", "scalar", "sync", "gpsimd")}
    TE, VE, SC, SY, GP = (Rec(Q["tensor"]), Rec(Q["vector"], hazard=True),
                          Rec(Q["scalar"], hazard=True),
                          Rec(Q["sync"]), Rec(Q["gpsimd"]))
    ENGS = {"tensor": (TE, s_t), "vector": (VE, s_v), "scalar": (SC, s_s),
            "sync": (SY, s_ld), "gpsimd": (GP, s_gp)}

    def mark_last(qname, sem_obj, k=1):
        for op, a, kw, ri in reversed(Q[qname]):
            if op not in ("wait_ge", "nop", "drain", "engine_nop"):
                assert not ri.incs, f"double sync update on {qname}:{op}"
                ri.incs.append((sem_obj.h, k))
                sem_obj.n += k
                return sem_obj.n
        raise RuntimeError(f"no real instruction to mark in {qname}")

    def barrier():
        for name, s in (("tensor", s_t), ("vector", s_v), ("scalar", s_s)):
            for op, a, kw, ri in reversed(Q[name]):
                if op not in ("wait_ge", "nop", "drain", "engine_nop"):
                    if not any(h is s.h for h, _ in ri.incs):
                        assert not ri.incs, f"barrier collision on {name}:{op}"
                        ri.incs.append((s.h, 1))
                        s.n += 1
                    break
        waits = [(s_t, s_t.n), (s_v, s_v.n), (s_s, s_s.n),
                 (s_ld, s_ld.n), (s_st, s_st.n), (s_gp, s_gp.n), (s_cc, s_cc.n)]
        for name, (R_, _) in ENGS.items():
            for s, v in waits:
                R_.wait_ge(s.h, v)

    def finish(scopes):
        SY.wait_ge(s_st.h, s_st.n)
        block_cm = nc.Block()
        block = block_cm.__enter__()

        @block.tensor
        def _(t):
            _replay(t, Q["tensor"])

        @block.vector
        def _(v):
            _replay(v, Q["vector"])

        @block.scalar
        def _(sc):
            _replay(sc, Q["scalar"])

        @block.sync
        def _(sy):
            _replay(sy, Q["sync"])

        @block.gpsimd
        def _(g):
            _replay(g, Q["gpsimd"])

        block_cm.__exit__(None, None, None)
        for sc_ in scopes:
            sc_.close()
        for t in reversed(psum_items):
            t.__exit__(None, None, None)
        P0.close()
        for t in reversed(sem_items):
            t.__exit__(None, None, None)
        return nc

    P1 = Scope(nc)
    Wq_sb = P1.sbuf("Wq_sb", [128, 8 * E], BF16)
    Wk_sb = P1.sbuf("Wk_sb", [128, 8 * E], BF16)
    Wv_sb = P1.sbuf("Wv_sb", [128, 8 * E], BF16)
    G1_sb = P1.sbuf("G1_sb", [128, 8 * D], BF16)
    G2_sb = P1.sbuf("G2_sb", [64, E], BF16)
    xs_sb = P1.sbuf("xs", [128, 4 * E], F32)
    ln1_bf = [P1.sbuf(f"ln1_bf{i}", [128, E], BF16) for i in range(2)]
    ln1T_sb = P1.sbuf("ln1T_sb", [128, 8 * TILE], BF16)
    kT_j = [P1.sbuf(f"kTj{i}", [128, TILE], F32) for i in range(2)]
    qs_j = [P1.sbuf(f"qsj{i}", [128, TILE], F32) for i in range(2)]
    f_j = P1.sbuf("fj", [128, TILE], F32)
    Lraw_j = [P1.sbuf(f"Lrawj{i}", [128, TILE], F32) for i in range(2)]
    rrev_j = [P1.sbuf(f"rrevj{i}", [128, TILE + 4], F32) for i in range(2)]
    tmpq_j = [P1.sbuf(f"tmpqj{i}", [128, TILE], F32) for i in range(2)]
    colst = P1.sbuf("colst", [128, 64], F32)
    colst2_dbg = P1.sbuf("colst2_dbg", [128, 48], F32)
    Dcols = P1.sbuf("Dcols", [128, 32], F32)
    kscale = P1.sbuf("kscale", [128, 32], F32)
    qt_pf = [P1.sbuf("qt_pe", [128, 8 * TILE], BF16), P1.sbuf("qt_po", [128, 8 * TILE], BF16)]
    kt_pf = [P1.sbuf("kt_pe", [128, 8 * TILE], BF16), P1.sbuf("kt_po", [128, 8 * TILE], BF16)]
    khat_bf = P1.sbuf("khat_bf", [128, 8 * TILE], BF16)
    khat_tok = P1.sbuf("khat_tok", [128, E], BF16)
    V_tok = [P1.sbuf(f"V_tok{i}", [128, E], BF16) for i in range(2)]
    P_bf = P1.sbuf("P_bf", [128, 16 * C], BF16)
    o_scr = [P1.sbuf(f"o_scr{i}", [128, E], F32) for i in range(2)]
    sg_scr = [P1.sbuf(f"sg_scr{i}", [128, E], F32) for i in range(2)]
    ghT_sb = P1.sbuf("ghT_sb", [64, TILE], BF16)
    wcb_sb = [P1.sbuf(f"wcb{i}", [128, 2048], BF16) for i in range(2)]

    s_ld.inc(SY.dma_start(out=ident_bf[:, :], in_=ident_bf_in), 16)
    s_ld.inc(SY.dma_start(out=ident_f32[:, :], in_=ident_f32_in), 16)
    s_ld.inc(SY.dma_start(out=mask_st[:, :], in_=mask_st_in), 16)
    s_ld.inc(SY.dma_start(out=emask_sb[:, :], in_=emask_in), 16)
    s_ld.inc(SY.dma_start(out=delta_neg[:, :], in_=delta_in), 16)
    s_ld.inc(SY.dma_start(out=normw_sb[:, :], in_=normw_in), 16)
    ld_const = s_ld.inc(SY.dma_start(out=nwbc_sb[:, :], in_=normw_row_in.broadcast_to([128, E])), 16)

    s_gp.inc(GP.dma_start(out=Wq_sb.rearrange("p (a n) -> p a n", a=8), in_=Wq_in.rearrange("(a p) n -> p a n", p=128)), 16)
    s_gp.inc(GP.dma_start(out=Wk_sb.rearrange("p (a n) -> p a n", a=8), in_=Wk_in.rearrange("(a p) n -> p a n", p=128)), 16)
    s_gp.inc(GP.dma_start(out=Wv_sb.rearrange("p (a n) -> p a n", a=8), in_=Wv_in.rearrange("(a p) n -> p a n", p=128)), 16)
    s_gp.inc(GP.dma_start(out=G1_sb.rearrange("p (a n) -> p a n", a=8), in_=G1_in.rearrange("(a p) n -> p a n", p=128)), 16)
    gp_attnw = s_gp.inc(GP.dma_start(out=G2_sb[:, :], in_=G2_in), 16)
    gp_wo = s_gp.inc(GP.dma_start(out=Wo_sb.rearrange("p (a n) -> p a n", a=8),
                                  in_=Wo_in.rearrange("(a p) n -> p a n", p=128)), 16)
    gp_gateT = s_gp.inc(GP.dma_start(out=gateT_sb.rearrange("p (a n) -> p a n", a=8),
                                     in_=gateT_in.rearrange("(a p) n -> p a n", p=128)), 16)
    if stop_after == "p0b":
        GP.wait_ge(s_gp.h, s_gp.n)
        return finish([P1])

    VE.wait_ge(s_ld.h, ld_const)
    VE.tensor_scalar_mul(delta_neg[:, :], delta_neg[:, :], -1.0)
    VE.memset(ones_col[:, :], 1.0)
    VE.memset(eps_col[:, :], RMS_EPS)
    VE.memset(S_sb[:, :], 0.0)
    VE.memset(S_bf[:, :], 0.0)
    VE.memset(Gamma_sb[:, :], 1.0)
    VE.memset(rrev_j[0][:, TILE:], 1.0)
    VE.memset(rrev_j[1][:, TILE:], 1.0)
    VE.memset(qt_pf[0][:, :], 0.0)
    VE.memset(qt_pf[1][:, :], 0.0)
    VE.memset(kt_pf[0][:, :], 0.0)
    VE.memset(kt_pf[1][:, :], 0.0)
    v_const = mark_last("vector", s_v)
    if stop_after == "p0a":
        SY.wait_ge(s_ld.h, s_ld.n)
        return finish([P1])

    if stop_after == "sent1":
        VE.wait_ge(s_gp.h, gp_attnw)
        sentinel(1, 7.0)
        return finish([P1])
    cast_jobs = []
    for src, dst in [(w1T_in, w1b_d), (w3T_in, w3b_d), (w2T_in, w2b_d)]:
        fs = src.rearrange("e a b -> (e a b)")
        fd = dst.rearrange("e a b -> (e a b)")
        total = NEXP * E * HID
        step = 128 * 2048
        for off in range(0, total, step):
            cast_jobs.append((fs[off:off + step].rearrange("(p n) -> p n", p=128),
                              fd[off:off + step].rearrange("(p n) -> p n", p=128)))
    cast_marks = []
    for i, (src, dst) in enumerate(cast_jobs):
        buf = wcb_sb[i % 2]
        if i >= 2:
            GP.wait_ge(s_gp.h, cast_marks[i - 2])
        m = s_gp.inc(GP.dma_start(out=buf[:, :], in_=src), 16)
        GP.wait_ge(s_gp.h, m)
        cast_marks.append(s_gp.inc(GP.dma_start(out=dst, in_=buf[:, :]), 16))
    def sentinel(blk, val, eng_wait=None):
        if eng_wait is not None:
            VE.wait_ge(eng_wait[0].h, eng_wait[1])
        m = s_v.inc(VE.memset(sq_scr[:, :], float(val)))
        SY.wait_ge(s_v.h, m)
        s_st.inc(SY.dma_start(out=out_d[blk * 128:(blk + 1) * 128, :], in_=sq_scr[:, :]), 16)

    if stop_after in ("sent", "sent0", "sent1", "sent2"):
        sentinel(0, 7.0)
    if stop_after == "sent0":
        return finish([P1])
    gp_cast_done = s_gp.n
    if stop_after == "sent2":
        VE.wait_ge(s_gp.h, gp_cast_done)
        sentinel(1, 7.0)
        return finish([P1])
    if stop_after == "p0":
        GP.wait_ge(s_gp.h, s_gp.n)
        SY.wait_ge(s_ld.h, s_ld.n)
        return finish([P1])

    x_free = v_const
    pT_free = v_const
    pS_free = v_const
    pO_free = v_const
    pAB_free = None
    last_P_evacs = None
    st_p1 = [None] * ntile
    ln1bf_free = [0, 0]
    Vtok_free = [0, 0]
    ost_free = [0, 0]
    sgst_free = [0, 0]
    s_kT_m = [None, None]
    s_qs_m = [None, None]
    v_j_m = [None, None]
    scale_free = [None, None]
    p1_te_prev = 0

    for T in range(ntile):
        SY.wait_ge(s_v.h, x_free)
        ld_x = s_ld.inc(SY.dma_start(
            out=xs_sb.rearrange("p (c e) -> p c e", c=4),
            in_=x_in[T * TILE:(T + 1) * TILE, :].rearrange("(c p) e -> p c e", p=128)), 16)
        if T == 0:
            TE.wait_ge(s_gp.h, gp_attnw)
            TE.wait_ge(s_ld.h, ld_const)
        if stop_after == "probe_stats" and T == 0:
            VE.wait_ge(s_ld.h, ld_x)
            SC.wait_ge(s_ld.h, ld_x)
            for c in range(4):
                VE.reduce_sum(colst[:, c:c + 1], xs_sb[:, c * E:(c + 1) * E], axis=AX.X)
                SC.activation(sq_scr[:, :], xs_sb[:, c * E:(c + 1) * E], AF.Square,
                              accum_out=colst2_dbg[:, c:c + 1])
            mv = mark_last("vector", s_v)
            ms = mark_last("scalar", s_s)
            VE.wait_ge(s_s.h, ms)
            mv2 = s_v.inc(VE.tensor_copy(xs_sb[:, 0:4], colst[:, 0:4]))
            mv3 = s_v.inc(VE.tensor_copy(xs_sb[:, 4:8], colst2_dbg[:, 0:4]))
            SY.wait_ge(s_v.h, mv3)
            s_st.inc(SY.dma_start(out=out_d[0:128, 0:8], in_=xs_sb[:, 0:8]), 16)
            SY.wait_ge(s_st.h, s_st.n)
            return finish([P1])

        if stop_after == "sent" and T == 0:
            sentinel(1, 7.0)
        v_ln = [None] * NCT
        for c in range(NCT):
            xsl = xs_sb[:, c * E:(c + 1) * E]
            SC.wait_ge(s_ld.h, ld_x)
            SC.activation(sq_scr[:, :], xsl, AF.Square, accum_out=lnS[:, 2:3])
            m_ssq = mark_last("scalar", s_s)
            VE.wait_ge(s_ld.h, ld_x)
            VE.reduce_sum(lnS[:, 0:1], xsl, axis=AX.X)
            VE.tensor_scalar_mul(lnS[:, 1:2], lnS[:, 0:1], 1.0 / E)
            VE.wait_ge(s_s.h, m_ssq)
            VE.tensor_mul(lnS[:, 5:6], lnS[:, 1:2], lnS[:, 1:2])
            VE.tensor_scalar(lnS[:, 6:7], lnS[:, 2:3], 1.0 / E, None, ALU.mult)
            VE.tensor_sub(lnS[:, 7:8], lnS[:, 6:7], lnS[:, 5:6])
            m_var = s_v.inc(VE.tensor_scalar(lnS[:, 3:4], lnS[:, 7:8], LN_EPS, None, ALU.add))
            SC.wait_ge(s_v.h, m_var)
            m_sqrt = s_s.inc(SC.activation(lnS[:, 4:5], lnS[:, 3:4], AF.Sqrt))
            VE.wait_ge(s_s.h, m_sqrt)
            VE.reciprocal(lnS[:, 8:9], lnS[:, 4:5])
            VE.tensor_mul(lnS[:, 9:10], lnS[:, 1:2], lnS[:, 8:9])
            if ln1bf_free[c % 2] > 0:
                VE.wait_ge(s_t.h, ln1bf_free[c % 2])
            if stop_after == "probe_lnx" and T == 0:
                VE.tensor_copy(colst2_dbg[:, c * 10:(c + 1) * 10], lnS[:, 0:10])
            v_ln[c] = s_v.inc(VE.tensor_scalar(ln1_bf[c % 2][:, :], xsl,
                                               lnS[:, 8:9], lnS[:, 9:10], ALU.mult, ALU.subtract))
            TE.wait_ge(s_v.h, v_ln[c])
            TE.wait_ge(s_v.h, pT_free)
            for j in range(8):
                TE.transpose(pT[:, j * 128:(j + 1) * 128],
                             ln1_bf[c % 2][:, j * 128:(j + 1) * 128], ident_bf[:, :])
            m_t = mark_last("tensor", s_t)
            VE.wait_ge(s_t.h, m_t)
            if stop_after == "probe_pT" and T == 0 and c == 0:
                mv = s_v.inc(VE.tensor_copy(xs_sb[:, 0:E], pT[:, :]))
                SY.wait_ge(s_v.h, mv)
                m2 = s_st.inc(SY.dma_start(out=out_d[0:128, :], in_=xs_sb[:, 0:E]), 16)
                VE.wait_ge(s_st.h, m2)
                mv = s_v.inc(VE.tensor_copy(xs_sb[:, 0:128], ident_bf[:, :]))
                SY.wait_ge(s_v.h, mv)
                m2 = s_st.inc(SY.dma_start(out=out_d[128:256, 0:128], in_=xs_sb[:, 0:128]), 16)
                VE.wait_ge(s_st.h, m2)
                mv = s_v.inc(VE.tensor_copy(xs_sb[:, 0:E], ln1_bf[0][:, :]))
                SY.wait_ge(s_v.h, mv)
                s_st.inc(SY.dma_start(out=out_d[256:384, :], in_=xs_sb[:, 0:E]), 16)
                SY.wait_ge(s_st.h, s_st.n)
                return finish([P1])
            VE.tensor_copy(
                ln1T_sb[:, :].rearrange("p (j s) -> p j s", j=8)[:, :, c * 128:(c + 1) * 128],
                pT[:, :].rearrange("p (j s) -> p j s", j=8))
            pT_free = mark_last("vector", s_v)
            ln1bf_free[c % 2] = m_t
            barrier() if paranoid else None
        v_lnT_done = s_v.n
        if stop_after == "probe_lnx" and T == 0:
            VE.wait_ge(s_v.h, s_v.n)
            mv = s_v.inc(VE.tensor_copy(xs_sb[:, 0:40], colst2_dbg[:, 0:40]))
            SY.wait_ge(s_v.h, mv)
            s_st.inc(SY.dma_start(out=out_d[0:128, 0:40], in_=xs_sb[:, 0:40]), 16)
            VE.wait_ge(s_st.h, s_st.n)
            mv = s_v.inc(VE.tensor_copy(xs_sb[:, 0:E], ln1_bf[1][:, :]))
            SY.wait_ge(s_v.h, mv)
            s_st.inc(SY.dma_start(out=out_d[128:256, :], in_=xs_sb[:, 0:E]), 16)
            SY.wait_ge(s_st.h, s_st.n)
            return finish([P1])
        if stop_after == "probe_V2":
            TE.wait_ge(s_v.h, v_lnT_done)
            for nh in range(2):
                for j_in in range(8):
                    TE.matmul(pO[:, nh * 512:(nh + 1) * 512],
                              ln1T_sb[:, j_in * TILE + 0 * 128: j_in * TILE + 1 * 128],
                              Wv_sb[:, j_in * E + nh * 512: j_in * E + (nh + 1) * 512],
                              start=(j_in == 0), stop=(j_in == 7))
            mV = mark_last("tensor", s_t)
            VE.wait_ge(s_t.h, mV)
            mv = s_v.inc(VE.tensor_copy(xs_sb[:, 0:E], pO[:, :]))
            SY.wait_ge(s_v.h, mv)
            s_st.inc(SY.dma_start(out=out_d[0:128, :], in_=xs_sb[:, 0:E]), 16)
            SY.wait_ge(s_st.h, s_st.n)
            return finish([P1])
        if stop_after == "p1a":
            sentinel(1, 9.0, (s_v, v_lnT_done))
            return finish([P1])
        if stop_after == "probe_ln1T":
            SY.wait_ge(s_v.h, v_lnT_done)
            mv = s_v.inc(VE.tensor_copy(xs_sb[:, 0:E], ln1T_sb[:, 0:E]))
            SY.wait_ge(s_v.h, mv)
            s_st.inc(SY.dma_start(out=out_d[0:128, :], in_=xs_sb[:, 0:E]), 16)
            mv2 = s_v.inc(VE.tensor_copy(xs_sb[:, 0:E], ln1_bf[1][:, :]))
            SY.wait_ge(s_v.h, mv2)
            s_st.inc(SY.dma_start(out=out_d[128:256, :], in_=xs_sb[:, 0:E]), 16)
            SY.wait_ge(s_st.h, s_st.n)
            return finish([P1])

        jlvl = {"j_mm": 0, "j_ve1": 1, "j_ve2": 2, "j_sc": 3}.get(stop_after, 99)
        s_scale = [None] * 8
        TE.wait_ge(s_v.h, v_lnT_done)
        if p1_te_prev > 0:
            VE.wait_ge(s_t.h, p1_te_prev)
        for j in range(8):
            jb = j % 2
            if j >= 1:
                TE.wait_ge(s_s.h, s_qs_m[(j - 1) % 2])
            elif pAB_free is not None:
                TE.wait_ge(s_s.h, pAB_free)
            for j_in in range(8):
                TE.matmul(pA[:, :], Wk_sb[:, j_in * E + j * 128: j_in * E + j * 128 + 128],
                          ln1T_sb[:, j_in * TILE:(j_in + 1) * TILE],
                          start=(j_in == 0), stop=(j_in == 7))
            m_xk = mark_last("tensor", s_t)
            for j_in in range(8):
                TE.matmul(pB[:, :], Wq_sb[:, j_in * E + j * 128: j_in * E + j * 128 + 128],
                          ln1T_sb[:, j_in * TILE:(j_in + 1) * TILE],
                          start=(j_in == 0), stop=(j_in == 7))
            m_q = mark_last("tensor", s_t)
            SC.wait_ge(s_t.h, m_xk)
            if v_j_m[jb] is not None:
                SC.wait_ge(s_v.h, v_j_m[jb])
            SC.activation(kT_j[jb][:, :], pA[:, :], AF.Sigmoid,
                          bias=delta_neg[:, j:j + 1], scale=-1.0)
            s_kT_m[jb] = mark_last("scalar", s_s)
            SC.wait_ge(s_t.h, m_q)
            SC.activation(qs_j[jb][:, :], pB[:, :], AF.Silu)
            s_qs_m[jb] = mark_last("scalar", s_s)
            if jlvl < 1:
                v_j_m[jb] = s_v.n if v_j_m[jb] is None else v_j_m[jb]
                s_scale[j] = s_v.n
                scale_free[jb] = s_scale[j]
                continue
            VE.wait_ge(s_s.h, s_kT_m[jb])
            VE.tensor_scalar(f_j[:, :], kT_j[jb][:, :], -1.0, 1.0, ALU.mult, ALU.add)
            if scale_free[jb] is not None:
                VE.wait_ge(s_s.h, scale_free[jb])
            VE.tensor_tensor_scan(Lraw_j[jb][:, :], f_j[:, :], f_j[:, :], 1.0, ALU.mult, ALU.bypass)
            VE.tensor_tensor_scan(rrev_j[jb][:, 0:TILE][:, ::-1], f_j[:, ::-1], f_j[:, ::-1],
                                  1.0, ALU.mult, ALU.bypass)
            if jlvl < 2:
                v_j_m[jb] = mark_last("vector", s_v)
                s_scale[j] = s_v.n
                scale_free[jb] = None
                continue
            ecols = Lraw_j[jb][:, 127::128]
            VE.reciprocal(colst[:, j * 8: j * 8 + 4], ecols)
            VE.reciprocal(colst[:, j * 8 + 4: j * 8 + 7], rrev_j[jb][:, 128::128][:, 0:3])
            VE.wait_ge(s_s.h, s_qs_m[jb])
            for c in range(NCT):
                rr = colst[:, j * 8 + 4 + c: j * 8 + 5 + c] if c < 3 else ones_col[:, 0:1]
                VE.scalar_tensor_tensor(khat_bf[:, j * TILE + c * 128: j * TILE + (c + 1) * 128],
                                        rrev_j[jb][:, c * 128 + 1:(c + 1) * 128 + 1],
                                        rr, kT_j[jb][:, c * 128:(c + 1) * 128],
                                        ALU.mult, ALU.mult)
                bl = colst[:, j * 8 + c - 1: j * 8 + c] if c > 0 else ones_col[:, 0:1]
                VE.scalar_tensor_tensor(Dcols[:, c * 8 + j: c * 8 + j + 1],
                                        Lraw_j[jb][:, (c + 1) * 128 - 1:(c + 1) * 128],
                                        1.0, bl, ALU.mult, ALU.mult)
                blv = Lraw_j[jb][:, c * 128 - 1: c * 128] if c > 0 else ones_col[:, 0:1]
                VE.scalar_tensor_tensor(kscale[:, c * 8 + j: c * 8 + j + 1],
                                        blv, 1.0, colst[:, j * 8 + c: j * 8 + c + 1],
                                        ALU.mult, ALU.mult)
            if jlvl < 3:
                v_j_m[jb] = mark_last("vector", s_v)
                s_scale[j] = s_v.n
                scale_free[jb] = None
                continue
            if T >= 1 and j == 0:
                VE.wait_ge(s_st.h, st_p1[T - 1])
            for c in range(NCT):
                sl = slice(j * TILE + c * 128, j * TILE + (c + 1) * 128)
                lsl = slice(c * 128, (c + 1) * 128)
                bl = colst[:, j * 8 + c - 1: j * 8 + c] if c > 0 else ones_col[:, 0:1]
                el = kscale[:, c * 8 + j: c * 8 + j + 1]
                VE.scalar_tensor_tensor(qt_pf[0][0:64, sl], qs_j[jb][0:64, lsl],
                                        bl[0:64, :], Lraw_j[jb][0:64, lsl], ALU.mult, ALU.mult)
                VE.scalar_tensor_tensor(qt_pf[1][64:128, sl], qs_j[jb][64:128, lsl],
                                        bl[64:128, :], Lraw_j[jb][64:128, lsl], ALU.mult, ALU.mult)
                VE.tensor_scalar_mul(kt_pf[0][0:64, sl], khat_bf[0:64, sl], el[0:64, :])
                VE.tensor_scalar_mul(kt_pf[1][64:128, sl], khat_bf[64:128, sl], el[64:128, :])
            v_j_m[jb] = mark_last("vector", s_v)
            s_scale[j] = v_j_m[jb]
            scale_free[jb] = None
            barrier() if paranoid else None
        pAB_free = s_qs_m[1]
        if stop_after == "probe_wv":
            SY.wait_ge(s_s.h, s_s.n)
            for half in range(4):
                VE.wait_ge(s_st.h, s_st.n)
                mv = s_v.inc(VE.tensor_copy(xs_sb[:, 0:E], Wv_sb[:, half * E:(half + 1) * E]))
                SY.wait_ge(s_v.h, mv)
                s_st.inc(SY.dma_start(out=out_d[half * 128:(half + 1) * 128, :], in_=xs_sb[:, 0:E]), 16)
            SY.wait_ge(s_st.h, s_st.n)
            return finish([P1])
        if stop_after in ("probe_j", "j_mm", "j_ve1", "j_ve2", "j_sc"):
            SY.wait_ge(s_s.h, s_s.n)
            SY.wait_ge(s_v.h, s_v.n)
            dumps = [qs_j[0][:, :], kT_j[0][:, :], Lraw_j[0][:, :], tmpq_j[0][:, :]]
            for i, dsrc in enumerate(dumps):
                VE.wait_ge(s_st.h, s_st.n)
                mv = s_v.inc(VE.tensor_copy(xs_sb[:, i * TILE:(i + 1) * TILE], dsrc))
                SY.wait_ge(s_v.h, mv)
                s_st.inc(SY.dma_start(out=out_d[i * 128:(i + 1) * 128, 0:TILE],
                                      in_=xs_sb[:, i * TILE:(i + 1) * TILE]), 16)
            VE.wait_ge(s_st.h, s_st.n)
            mv = s_v.inc(VE.tensor_copy(xs_sb[:, 0:TILE], qt_pf[0][:, 7 * TILE:8 * TILE]))
            SY.wait_ge(s_v.h, mv)
            s_st.inc(SY.dma_start(out=out_d[0:128, TILE:2 * TILE], in_=xs_sb[:, 0:TILE]), 16)
            VE.wait_ge(s_st.h, s_st.n)
            mv = s_v.inc(VE.tensor_copy(xs_sb[:, 0:64], colst[:, :]))
            SY.wait_ge(s_v.h, mv)
            s_st.inc(SY.dma_start(out=out_d[128:256, TILE:TILE + 64], in_=xs_sb[:, 0:64]), 16)
            SY.wait_ge(s_st.h, s_st.n)
            return finish([P1])
        if stop_after == "p1b":
            VE.wait_ge(s_v.h, s_scale[7])
            sentinel(1, 9.0)
            return finish([P1])

        if stop_after == "sent" and T == 0:
            sentinel(2, 7.0)
        TE.wait_ge(s_s.h, s_kT_m[0])
        TE.wait_ge(s_s.h, s_kT_m[1])
        for j_in in range(8):
            TE.matmul(pA[0:64, :], G1_sb[:, j_in * D:(j_in + 1) * D],
                      ln1T_sb[:, j_in * TILE:(j_in + 1) * TILE],
                      start=(j_in == 0), stop=(j_in == 7))
        m_g1 = mark_last("tensor", s_t)
        VE.wait_ge(s_t.h, m_g1)
        v_ghT = s_v.inc(VE.tensor_copy(ghT_sb[:, :], pA[0:64, :]))
        barrier() if paranoid else None
        if stop_after == "p1c":
            VE.wait_ge(s_v.h, v_ghT)
            sentinel(1, 9.0)
            return finish([P1])

        if stop_after == "sent" and T == 0:
            sentinel(3, 7.0)
        glvl = {"g0": 0, "g1": 1, "g2": 2, "g3": 3}.get(stop_after, 99)
        v_O_evac = [None] * NCT
        s_gate = [None] * NCT
        for c in range(NCT):
            g = T * NCT + c
            cb2 = c % 2
            TE.wait_ge(s_v.h, pO_free)
            TE.wait_ge(s_v.h, s_scale[7])
            for nh in range(2):
                for j_in in range(8):
                    TE.matmul(pO[:, nh * 512:(nh + 1) * 512],
                              ln1T_sb[:, j_in * TILE + c * 128: j_in * TILE + (c + 1) * 128],
                              Wv_sb[:, j_in * E + nh * 512: j_in * E + (nh + 1) * 512],
                              start=(j_in == 0), stop=(j_in == 7))
            m_V = mark_last("tensor", s_t)
            VE.wait_ge(s_t.h, m_V)
            if Vtok_free[cb2] > 0:
                VE.wait_ge(s_t.h, Vtok_free[cb2])
            VE.tensor_copy(V_tok[cb2][:, :], pO[:, :])
            pO_free = mark_last("vector", s_v)
            v_V = pO_free
            barrier() if paranoid else None
            TE.wait_ge(s_v.h, v_ghT)
            if c > 0:
                TE.wait_ge(s_s.h, s_gate[c - 1])
            for nh in range(2):
                TE.matmul(pB[:, :] if nh == 0 else pA[:, :],
                          ghT_sb[:, c * 128:(c + 1) * 128],
                          G2_sb[:, nh * 512:(nh + 1) * 512], start=True, stop=True)
            m_gt = mark_last("tensor", s_t)
            SC.wait_ge(s_t.h, m_gt)
            if sgst_free[cb2] > 0:
                SC.wait_ge(s_st.h, sgst_free[cb2])
            SC.activation(sg_scr[cb2][:, 0:512], pB[:, :], AF.Sigmoid)
            SC.activation(sg_scr[cb2][:, 512:1024], pA[:, :], AF.Sigmoid)
            s_gate[c] = mark_last("scalar", s_s)
            SY.wait_ge(s_s.h, s_gate[c])
            sgst_free[cb2] = s_st.inc(SY.dma_start(
                out=sgate_d[(T * NCT + c) * C:(T * NCT + c + 1) * C, :],
                in_=sg_scr[cb2][:, :]), 16)

            if glvl < 1:
                continue
            VE.tensor_copy(Gres_sb[:, g * 8:(g + 1) * 8], Gamma_sb[:, :])
            v_Sbf = s_v.inc(VE.tensor_copy(S_bf[:, :], S_sb[:, :]))

            TE.wait_ge(s_v.h, pT_free)
            for j in range(8):
                TE.transpose(pT[:, j * 128:(j + 1) * 128],
                             khat_bf[:, j * TILE + c * 128: j * TILE + (c + 1) * 128],
                             ident_bf[:, :])
            m_kh = mark_last("tensor", s_t)
            VE.wait_ge(s_t.h, m_kh)
            VE.tensor_copy(khat_tok[:, :], pT[:, :])
            pT_free = mark_last("vector", s_v)
            v_khat = pT_free
            barrier() if paranoid else None

            if glvl < 2:
                continue
            v_P = [None] * 4
            for wv in range(4):
                pw = pP[wv % 2]
                if wv >= 2:
                    TE.wait_ge(s_v.h, v_P[wv - 2])
                elif last_P_evacs is not None:
                    TE.wait_ge(s_v.h, last_P_evacs[wv % 2])
                for hh in range(4):
                    h = wv * 4 + hh
                    j, par = h // 2, h % 2
                    TE.matmul(pw[:, hh * 128:(hh + 1) * 128],
                              kt_pf[par][:, j * TILE + c * 128: j * TILE + (c + 1) * 128],
                              qt_pf[par][:, j * TILE + c * 128: j * TILE + (c + 1) * 128],
                              start=True, stop=True)
                m_P = mark_last("tensor", s_t)
                VE.wait_ge(s_t.h, m_P)
                VE.tensor_tensor(P_bf[:, wv * 512:(wv + 1) * 512], pw[:, :],
                                 mask_st[:, :].unsqueeze(1).broadcast_to([128, 4, 128]),
                                 ALU.mult)
                v_P[wv] = mark_last("vector", s_v)
            last_P_evacs = [v_P[2], v_P[3]]
            barrier() if paranoid else None
            if stop_after == "probe_P" and T == 0 and c == 0:
                VE.wait_ge(s_v.h, v_P[3])
                for half in range(2):
                    mv = s_v.inc(VE.tensor_copy(xs_sb[:, 0:E], P_bf[:, half * 1024:(half + 1) * 1024]))
                    SY.wait_ge(s_v.h, mv)
                    m2 = s_st.inc(SY.dma_start(out=out_d[half * 128:(half + 1) * 128, :],
                                               in_=xs_sb[:, 0:E]), 16)
                    VE.wait_ge(s_st.h, m2)
                mv = s_v.inc(VE.tensor_copy(xs_sb[:, 0:E], V_tok[cb2][:, :]))
                SY.wait_ge(s_v.h, mv)
                s_st.inc(SY.dma_start(out=out_d[256:384, :], in_=xs_sb[:, 0:E]), 16)
                mv = s_v.inc(VE.tensor_copy(xs_sb[:, 0:E], qt_pf[0][:, 0:E].bitcast(F32).bitcast(BF16)[:, 0:1024] if False else qt_pf[0][:, 0:1024]))
                SY.wait_ge(s_v.h, mv)
                s_st.inc(SY.dma_start(out=out_d[384:512, :], in_=xs_sb[:, 0:E]), 16)
                SY.wait_ge(s_st.h, s_st.n)
                return finish([P1])

            if glvl < 3:
                continue
            TE.wait_ge(s_v.h, pO_free)
            TE.wait_ge(s_v.h, v_Sbf)
            TE.wait_ge(s_v.h, v_P[3])
            for h in range(H):
                j, par = h // 2, h % 2
                TE.matmul(pO[:, h * D:(h + 1) * D],
                          P_bf[:, h * 128:(h + 1) * 128],
                          V_tok[cb2][:, h * D:(h + 1) * D],
                          start=True, stop=False)
                TE.matmul(pO[:, h * D:(h + 1) * D],
                          qt_pf[par][:, j * TILE + c * 128: j * TILE + (c + 1) * 128],
                          S_bf[:, j * D:(j + 1) * D],
                          start=False, stop=True)
            m_O = mark_last("tensor", s_t)
            VE.wait_ge(s_t.h, m_O)
            if ost_free[cb2] > 0:
                VE.wait_ge(s_st.h, ost_free[cb2])
            VE.tensor_copy(o_scr[cb2][:, :], pO[:, :])
            pO_free = mark_last("vector", s_v)
            v_O_evac[c] = pO_free
            barrier() if paranoid else None
            SY.wait_ge(s_v.h, v_O_evac[c])
            ost_free[cb2] = s_st.inc(SY.dma_start(
                out=olocal_d[(T * NCT + c) * C:(T * NCT + c + 1) * C, :],
                in_=o_scr[cb2][:, :]), 16)

            if glvl < 4:
                continue
            TE.wait_ge(s_v.h, v_khat)
            for r in range(2):
                TE.wait_ge(s_v.h, pS_free)
                for jb2 in range(4):
                    j = 4 * r + jb2
                    for par in range(2):
                        h = 2 * j + par
                        po = par * 64
                        TE.matmul(pS[po:po + 64, jb2 * 128: jb2 * 128 + D],
                                  khat_tok[:, h * D:(h + 1) * D],
                                  V_tok[cb2][:, h * D:(h + 1) * D],
                                  start=True, stop=True,
                                  tile_position=(0, po))
                m_Sg = mark_last("tensor", s_t)
                VE.wait_ge(s_t.h, m_Sg)
                for jb2 in range(4):
                    j = 4 * r + jb2
                    VE.scalar_tensor_tensor(S_sb[:, j * D:(j + 1) * D], S_sb[:, j * D:(j + 1) * D],
                                            Dcols[:, c * 8 + j: c * 8 + j + 1],
                                            pS[:, jb2 * 128: jb2 * 128 + D],
                                            ALU.mult, ALU.add)
                    VE.tensor_scalar_mul(Gamma_sb[:, j:j + 1], Gamma_sb[:, j:j + 1],
                                         Dcols[:, c * 8 + j: c * 8 + j + 1])
                pS_free = mark_last("vector", s_v)
            Vtok_free[cb2] = m_Sg
            barrier() if paranoid else None

        x_free = v_ln[NCT - 1]
        pAB_free = s_gate[NCT - 1]
        p1_te_prev = s_t.n
        SY.wait_ge(s_v.h, s_scale[7])
        s_st.inc(SY.dma_start(out=qte_d[:, T * 8 * TILE:(T + 1) * 8 * TILE], in_=qt_pf[0][:, :]), 16)
        if glvl < 99 and T == ntile - 1:
            sentinel(1, 9.0, (s_v, s_v.n))
            VE.wait_ge(s_s.h, s_s.n)
            VE.wait_ge(s_t.h, s_t.n)
            sentinel(2, 9.0)
            return finish([P1])
        elif glvl < 99:
            pass
        st_p1[T] = s_st.inc(SY.dma_start(out=qto_d[:, T * 8 * TILE:(T + 1) * 8 * TILE], in_=qt_pf[1][:, :]), 16)

    if stop_after == "p1":
        sentinel(1, 9.0, (s_v, s_v.n))
        VE.wait_ge(s_s.h, s_s.n)
        VE.wait_ge(s_t.h, s_t.n)
        sentinel(2, 9.0)
        return finish([P1])

    if stop_after == "sent":
        sentinel(0, 17.0)
    if use_ag:
        SY.wait_ge(s_v.h, s_v.n)
        s_ld.inc(SY.dma_start(out=cc_in_d[:, 0:512], in_=S_sb[:, :]), 16)
        ag_in = s_ld.inc(SY.dma_start(out=cc_in_d[:, 512:520], in_=Gamma_sb[:, :]), 16)
        GP.wait_ge(s_ld.h, ag_in)
        cc_mark = s_cc.inc(GP.collective_compute(
            "AllGather", ALU.bypass,
            replica_groups=[[0, 1, 2, 3], [4, 5, 6, 7]],
            ins=[cc_in_d[:, :].opt()],
            outs=[cc_out_d[:, :].opt()],
        ))

    barrier()
    P1.close()

    P2 = Scope(nc)
    agland = P2.sbuf("agland", [128, 4 * 520], F32)
    Sinit_sb = P2.sbuf("Sinit_sb", [128, 512], F32)
    Srun_sb = P2.sbuf("Srun_sb", [128, 512], F32)
    St_sb = [P2.sbuf(f"St_sb{i}", [128, 512], F32) for i in range(2)]
    St_bf = [P2.sbuf(f"St_bf{i}", [128, 512], BF16) for i in range(2)]
    x2_sb = P2.sbuf("x2", [128, 4 * E], F32)
    ot_sb = P2.sbuf("ot", [128, 4 * E], F32)
    sg2_sb = P2.sbuf("sg2", [128, 4 * E], F32)
    qt2_sb = [P2.sbuf("qt2e", [128, 8 * TILE], BF16), P2.sbuf("qt2o", [128, 8 * TILE], BF16)]
    xp_sb = P2.sbuf("xp", [128, 4 * E], F32)
    on_bf = [P2.sbuf(f"on_bf{i}", [128, E], BF16) for i in range(2)]
    onT_sb = [P2.sbuf(f"onT_sb{i}", [128, 8 * C], BF16) for i in range(2)]
    o2_sb = [P2.sbuf(f"o2_sb{i}", [128, E], F32) for i in range(2)]
    osq_sb = P2.sbuf("osq_sb", [128, E], F32)
    ln2_bf = [P2.sbuf(f"ln2_bf{i}", [128, E], BF16) for i in range(2)]
    ln2T_sb = P2.sbuf("ln2T_sb", [128, 8 * TILE], BF16)
    probs_sb = P2.sbuf("probs_sb", [128, 4 * 16], F32)
    mx8_sb = P2.sbuf("mx8_sb", [128, 8], F32)
    rt_sc = P2.sbuf("rt_sc", [128, 8], F32)
    comb_sb = P2.sbuf("comb_sb", [128, 16], F32)
    combT_tile = P2.sbuf("combT_tile", [16, 4 * C], F32)
    rms_sc = P2.sbuf("rms_sc", [128, 96], F32)

    if aux and use_ag:
        SY.wait_ge(s_gp.h, gp_wo)
        s_st.inc(SY.dma_start(out=aux_d[1024:1152, 0:4096], in_=Wo_sb[:, :].bitcast(F32)), 16)
        SY.wait_ge(s_gp.h, gp_gateT)
        s_st.inc(SY.dma_start(out=aux_d[1280:1408, 0:64], in_=gateT_sb[:, :].bitcast(F32)), 16)

    def tile_loads(T):
        s_ld.inc(SY.dma_start(out=x2_sb.rearrange("p (c e) -> p c e", c=4),
                     in_=x_in[T * TILE:(T + 1) * TILE, :].rearrange("(c p) e -> p c e", p=128)), 16)
        s_ld.inc(SY.dma_start(out=ot_sb.rearrange("p (c e) -> p c e", c=4),
                     in_=olocal_d[T * TILE:(T + 1) * TILE, :].rearrange("(c p) e -> p c e", p=128)), 16)
        s_ld.inc(SY.dma_start(out=sg2_sb.rearrange("p (c e) -> p c e", c=4),
                     in_=sgate_d[T * TILE:(T + 1) * TILE, :].rearrange("(c p) e -> p c e", p=128)), 16)
        s_ld.inc(SY.dma_start(out=qt2_sb[0][:, :], in_=qte_d[:, T * 8 * TILE:(T + 1) * 8 * TILE]), 16)
        return s_ld.inc(SY.dma_start(out=qt2_sb[1][:, :],
                                     in_=qto_d[:, T * 8 * TILE:(T + 1) * 8 * TILE]), 16)

    ld2_t0 = tile_loads(0)

    if use_ag:
        SY.wait_ge(s_cc.h, cc_mark)
        ag_out = None
        for j in range(4):
            ag_out = s_ld.inc(SY.dma_start(out=agland[:, j * 520:(j + 1) * 520],
                                           in_=cc_out_d[j * 128:(j + 1) * 128, :]), 16)
        VE.wait_ge(s_ld.h, ag_out)
        VE.memset(Srun_sb[:, :], 0.0)
        VE.memset(Sinit_sb[:, :], 0.0)
        for j in range(4):
            Lj = agland[:, j * 520: j * 520 + 512]
            Gj = agland[:, j * 520 + 512: j * 520 + 520]
            VE.scalar_tensor_tensor(Sinit_sb[:, :], Srun_sb[:, :], emask_sb[:, j:j + 1],
                                    Sinit_sb[:, :], ALU.mult, ALU.add)
            for b in range(8):
                VE.scalar_tensor_tensor(Srun_sb[:, b * D:(b + 1) * D], Srun_sb[:, b * D:(b + 1) * D],
                                        Gj[:, b:b + 1], Lj[:, b * D:(b + 1) * D], ALU.mult, ALU.add)
        v_sinit = mark_last("vector", s_v)
    else:
        VE.memset(Sinit_sb[:, :], 0.0)
        v_sinit = mark_last("vector", s_v)

    if aux and use_ag:
        SY.wait_ge(s_v.h, v_sinit)
        s_st.inc(SY.dma_start(out=aux_d[0:128, 0:512], in_=Sinit_sb[:, :]), 16)
        s_st.inc(SY.dma_start(out=aux_d[0:128, 512:516], in_=emask_sb[:, :]), 16)
        s_st.inc(SY.dma_start(out=aux_d[0:128, 516:2596], in_=agland[:, :]), 16)

    if stop_after == "sent":
        sentinel(1, 17.0)
    st_p2 = [None] * ntile
    buf2_free = v_sinit
    buf2_te = 0
    pT_free2 = v_const
    pS_free2 = v_const
    pAB_free2 = 0
    corr_par_te = [0, 0]
    corr_par_ve = [0, 0]
    onbf_free = [0, 0]
    ln2bf_free = [0, 0]

    TE.wait_ge(s_gp.h, gp_wo)
    TE.wait_ge(s_gp.h, gp_gateT)
    for T in range(ntile):
        if T == 0:
            ld2 = ld2_t0
        else:
            SY.wait_ge(s_v.h, buf2_free)
            if buf2_te > 0:
                SY.wait_ge(s_t.h, buf2_te)
            ld2 = tile_loads(T)

        v2_ln2T = [None] * NCT
        v2_xp = [None] * NCT
        m_lg = [None] * NCT
        m_ssq16_arr = [None] * NCT
        m_rms_arr = [None] * NCT
        v_on_arr = [None] * NCT
        m_ssq_arr = [None] * NCT
        m_var_arr = [None] * NCT
        m_sqrt_arr = [None] * NCT
        v_ln2_arr = [None] * NCT

        def front(c):
            nonlocal corr_par_te, corr_par_ve
            p2 = c % 2
            g = T * NCT + c
            if c == 0:
                VE.wait_ge(s_ld.h, ld2)
            if corr_par_te[p2] > 0:
                VE.wait_ge(s_t.h, corr_par_te[p2])
            if c >= 2 and m_rms_arr[c - 2] is not None:
                VE.wait_ge(s_s.h, m_rms_arr[c - 2])
            for blk in range(8):
                VE.tensor_scalar_mul(St_sb[p2][:, blk * D:(blk + 1) * D],
                                     Sinit_sb[:, blk * D:(blk + 1) * D],
                                     Gres_sb[:, g * 8 + blk: g * 8 + blk + 1])
            v_St = s_v.inc(VE.tensor_copy(St_bf[p2][:, :], St_sb[p2][:, :]))
            TE.wait_ge(s_v.h, v_St)
            if c == 0:
                TE.wait_ge(s_ld.h, ld2)
            free_m = corr_par_ve[p2]
            for r in range(2):
                if free_m > 0:
                    TE.wait_ge(s_v.h, free_m)
                for i in range(8):
                    h = 8 * r + i
                    j, par = h // 2, h % 2
                    if p2 == 0:
                        dst = pO[:, i * 128: i * 128 + D]
                    else:
                        dst = pP[i // 4][:, (i % 4) * 128: (i % 4) * 128 + D]
                    TE.matmul(dst,
                              qt2_sb[par][:, j * TILE + c * 128: j * TILE + (c + 1) * 128],
                              St_bf[p2][:, j * D:(j + 1) * D],
                              start=True, stop=True)
                m_corr = mark_last("tensor", s_t)
                VE.wait_ge(s_t.h, m_corr)
                if p2 == 0:
                    VE.tensor_add(o2_sb[p2][:, :].rearrange("p (h d) -> p h d", h=16)[:, 8 * r:8 * r + 8, :],
                                  ot_sb[:, c * E:(c + 1) * E].rearrange("p (h d) -> p h d", h=16)[:, 8 * r:8 * r + 8, :],
                                  pO[:, :].rearrange("p (i s) -> p i s", i=8)[:, :, 0:D])
                else:
                    for half in range(2):
                        hsl = slice(8 * r + 4 * half, 8 * r + 4 * half + 4)
                        VE.tensor_add(o2_sb[p2][:, :].rearrange("p (h d) -> p h d", h=16)[:, hsl, :],
                                      ot_sb[:, c * E:(c + 1) * E].rearrange("p (h d) -> p h d", h=16)[:, hsl, :],
                                      pP[half][:, :].rearrange("p (i s) -> p i s", i=4)[:, :, 0:D])
                free_m = mark_last("vector", s_v)
            corr_par_te[p2] = m_corr
            corr_par_ve[p2] = free_m
            VE.tensor_mul(o2_sb[p2][:, :], o2_sb[p2][:, :], sg2_sb[:, c * E:(c + 1) * E])
            VE.tensor_mul(osq_sb[:, :], o2_sb[p2][:, :], o2_sb[p2][:, :])
            m_ssq16_arr[c] = s_v.inc(VE.reduce_sum(rms_sc[:, 48 * p2: 48 * p2 + 16],
                                                   osq_sb[:, :].rearrange("p (h d) -> p h d", h=16),
                                                   axis=AX.X))

        def b_rms(c):
            SC.wait_ge(s_v.h, m_ssq16_arr[c])
            rb = 48 * (c % 2)
            m_rms_arr[c] = s_s.inc(SC.activation(rms_sc[:, rb + 16: rb + 32], rms_sc[:, rb: rb + 16],
                                                 AF.Sqrt, scale=1.0 / D, bias=eps_col[:, 0:1]))

        def b_on(c):
            p2 = c % 2
            rb = 48 * p2
            VE.wait_ge(s_s.h, m_rms_arr[c])
            VE.reciprocal(rms_sc[:, rb + 32: rb + 48], rms_sc[:, rb + 16: rb + 32])
            if isinstance(onbf_free[p2], int) and onbf_free[p2] > 0:
                VE.wait_ge(s_t.h, onbf_free[p2])
            for h in range(H):
                VE.scalar_tensor_tensor(on_bf[p2][:, h * D:(h + 1) * D],
                                        o2_sb[p2][:, h * D:(h + 1) * D], rms_sc[:, rb + 32 + h: rb + 33 + h],
                                        nwbc_sb[:, h * D:(h + 1) * D], ALU.mult, ALU.mult)
            v_on_arr[c] = mark_last("vector", s_v)

        def b_onT(c):
            nonlocal pT_free2
            p2 = c % 2
            TE.wait_ge(s_v.h, v_on_arr[c])
            TE.wait_ge(s_v.h, pT_free2)
            for j in range(8):
                TE.transpose(pT[:, j * 128:(j + 1) * 128],
                             on_bf[p2][:, j * 128:(j + 1) * 128], ident_bf[:, :])
            m_onT = mark_last("tensor", s_t)
            VE.wait_ge(s_t.h, m_onT)
            VE.tensor_copy(onT_sb[p2][:, :], pT[:, :])
            pT_free2 = mark_last("vector", s_v)
            onbf_free[p2] = m_onT

        def b_wo(c):
            nonlocal pAB_free2
            p2 = c % 2
            TE.wait_ge(s_v.h, pT_free2)
            if pAB_free2 > 0:
                TE.wait_ge(s_v.h, pAB_free2)
            for nh, bank in ((0, pA), (1, pB)):
                for j in range(8):
                    TE.matmul(bank[:, :],
                              onT_sb[p2][:, j * 128:(j + 1) * 128],
                              Wo_sb[:, j * E + nh * 512: j * E + (nh + 1) * 512],
                              start=(j == 0), stop=(j == 7))
            m_wo = mark_last("tensor", s_t)
            VE.wait_ge(s_t.h, m_wo)
            if T >= 1 and c == 0:
                VE.wait_ge(s_st.h, st_p2[T - 1])
            VE.tensor_add(xp_sb[:, c * E: c * E + 512], x2_sb[:, c * E: c * E + 512], pA[:, :])
            VE.tensor_add(xp_sb[:, c * E + 512:(c + 1) * E], x2_sb[:, c * E + 512:(c + 1) * E], pB[:, :])
            pAB_free2 = mark_last("vector", s_v)
            v2_xp[c] = pAB_free2

        def b_ln2stats(c):
            lb = 16 * (c % 2)
            xsl = xp_sb[:, c * E:(c + 1) * E]
            SC.wait_ge(s_v.h, v2_xp[c])
            SC.activation(sq_scr[:, :], xsl, AF.Square, accum_out=lnS[:, lb + 2: lb + 3])
            m_ssq_arr[c] = mark_last("scalar", s_s)
            VE.reduce_sum(lnS[:, lb + 0: lb + 1], xsl, axis=AX.X)
            VE.tensor_scalar_mul(lnS[:, lb + 1: lb + 2], lnS[:, lb + 0: lb + 1], 1.0 / E)

        def b_ln2var(c):
            lb = 16 * (c % 2)
            VE.wait_ge(s_s.h, m_ssq_arr[c])
            VE.tensor_mul(lnS[:, lb + 5: lb + 6], lnS[:, lb + 1: lb + 2], lnS[:, lb + 1: lb + 2])
            VE.tensor_scalar(lnS[:, lb + 6: lb + 7], lnS[:, lb + 2: lb + 3], 1.0 / E, None, ALU.mult)
            VE.tensor_sub(lnS[:, lb + 7: lb + 8], lnS[:, lb + 6: lb + 7], lnS[:, lb + 5: lb + 6])
            m_var_arr[c] = s_v.inc(VE.tensor_scalar(lnS[:, lb + 3: lb + 4], lnS[:, lb + 7: lb + 8],
                                                    LN_EPS, None, ALU.add))

        def b_ln2sqrt(c):
            lb = 16 * (c % 2)
            SC.wait_ge(s_v.h, m_var_arr[c])
            m_sqrt_arr[c] = s_s.inc(SC.activation(lnS[:, lb + 4: lb + 5], lnS[:, lb + 3: lb + 4], AF.Sqrt))

        def b_ln2scale(c):
            p2 = c % 2
            lb = 16 * p2
            xsl = xp_sb[:, c * E:(c + 1) * E]
            VE.wait_ge(s_s.h, m_sqrt_arr[c])
            VE.reciprocal(lnS[:, lb + 8: lb + 9], lnS[:, lb + 4: lb + 5])
            VE.tensor_mul(lnS[:, lb + 9: lb + 10], lnS[:, lb + 1: lb + 2], lnS[:, lb + 8: lb + 9])
            if isinstance(ln2bf_free[p2], int) and ln2bf_free[p2] > 0:
                VE.wait_ge(s_t.h, ln2bf_free[p2])
            v_ln2_arr[c] = s_v.inc(VE.tensor_scalar(ln2_bf[p2][:, :], xsl,
                                                    lnS[:, lb + 8: lb + 9], lnS[:, lb + 9: lb + 10],
                                                    ALU.mult, ALU.subtract))

        def b_ln2T(c):
            nonlocal pT_free2
            p2 = c % 2
            TE.wait_ge(s_v.h, v_ln2_arr[c])
            TE.wait_ge(s_v.h, pT_free2)
            for j in range(8):
                TE.transpose(pT[:, j * 128:(j + 1) * 128],
                             ln2_bf[p2][:, j * 128:(j + 1) * 128], ident_bf[:, :])
            m_l2T = mark_last("tensor", s_t)
            VE.wait_ge(s_t.h, m_l2T)
            if T >= 1 and c == 0:
                VE.wait_ge(s_st.h, st_p2[T - 1])
            VE.tensor_copy(
                ln2T_sb[:, :].rearrange("p (j s) -> p j s", j=8)[:, :, c * 128:(c + 1) * 128],
                pT[:, :].rearrange("p (j s) -> p j s", j=8))
            pT_free2 = mark_last("vector", s_v)
            ln2bf_free[p2] = m_l2T
            v2_ln2T[c] = pT_free2

        def b_probs(c):
            nonlocal pS_free2
            TE.wait_ge(s_v.h, v2_ln2T[c])
            if c == 0:
                TE.wait_ge(s_v.h, pS_free2)
            for j in range(8):
                TE.matmul(pS[:, c * 128: c * 128 + 16],
                          ln2T_sb[:, j * TILE + c * 128: j * TILE + (c + 1) * 128],
                          gateT_sb[:, j * NEXP:(j + 1) * NEXP],
                          start=(j == 0), stop=(j == 7))
            m_lg[c] = mark_last("tensor", s_t)

        for pair in range(NCT // 2):
            c0, c1 = 2 * pair, 2 * pair + 1
            front(c0)
            front(c1)
            b_rms(c0); b_rms(c1)
            b_on(c0); b_on(c1)
            b_onT(c0); b_onT(c1)
            b_wo(c0); b_wo(c1)
            b_ln2stats(c0); b_ln2stats(c1)
            b_ln2var(c0); b_ln2var(c1)
            b_ln2sqrt(c0); b_ln2sqrt(c1)
            b_ln2scale(c0); b_ln2scale(c1)
            b_ln2T(c0); b_ln2T(c1)
            b_probs(c0); b_probs(c1)

        VE.wait_ge(s_t.h, m_lg[NCT - 1])
        VE.reduce_max(rt_sc[:, 0:4], pS[:, 0:512].rearrange("p (c k) -> p c k", c=4)[:, :, 0:16],
                      axis=AX.X, negate=True)
        for c in range(NCT):
            VE.tensor_scalar(probs_sb[:, c * 16:(c + 1) * 16], pS[:, c * 128: c * 128 + 16],
                             rt_sc[:, c:c + 1], None, ALU.add)
        m_pre = mark_last("vector", s_v)
        SC.wait_ge(s_v.h, m_pre)
        m_exp = s_s.inc(SC.activation(probs_sb[:, :], probs_sb[:, :], AF.Exp))
        VE.wait_ge(s_s.h, m_exp)
        if T >= 1:
            VE.wait_ge(s_st.h, st_p2[T - 1])
        for c in range(NCT):
            psl = probs_sb[:, c * 16:(c + 1) * 16]
            VE.max(out=mx8_sb[:, :], in_=psl)
            VE.drain()
            VE.tensor_scalar(comb_sb[:, :], psl, mx8_sb[:, 3:4], None, ALU.is_ge)
            VE.tensor_mul(comb_sb[:, :], comb_sb[:, :], psl)
            VE.reduce_sum(rt_sc[:, 4:5], comb_sb[:, :], axis=AX.X)
            VE.reciprocal(rt_sc[:, 5:6], rt_sc[:, 4:5])
            m_cb = s_v.inc(VE.tensor_scalar_mul(comb_sb[:, :], comb_sb[:, :], rt_sc[:, 5:6]))
            TE.wait_ge(s_v.h, m_cb)
            if c > 0:
                TE.wait_ge(s_v.h, pS_free2)
            m_cT = s_t.inc(TE.transpose(pS[0:16, 0:128], comb_sb[:, :], ident_f32[:, :]))
            VE.wait_ge(s_t.h, m_cT)
            VE.tensor_copy(combT_tile[:, c * 128:(c + 1) * 128], pS[0:16, 0:128])
            pS_free2 = mark_last("vector", s_v)

        SY.wait_ge(s_v.h, v2_xp[NCT - 1])
        s_st.inc(SY.dma_start(out=xp_d[T * TILE:(T + 1) * TILE, :].rearrange("(c p) e -> p c e", p=128),
                              in_=xp_sb.rearrange("p (c e) -> p c e", c=4)), 16)
        SY.wait_ge(s_v.h, v2_ln2T[NCT - 1])
        s_st.inc(SY.dma_start(out=ln2T_d[:, T * 8 * TILE:(T + 1) * 8 * TILE], in_=ln2T_sb[:, :]), 16)
        SY.wait_ge(s_v.h, pS_free2)
        st_p2[T] = s_st.inc(SY.dma_start(out=combT_d[:, T * TILE:(T + 1) * TILE], in_=combT_tile[:, :]), 16)
        buf2_free = pAB_free2
        buf2_te = corr_par_te[1]

    if aux and use_ag:
        SY.wait_ge(s_st.h, s_st.n)
        aux_srcs = [
            (128, xp_d[0:128, 0:1024], False),
            (256, xp_d[2048:2176, 0:1024], False),
            (384, ln2T_d[:, 0:2048], True),
            (640, olocal_d[0:128, 0:1024], False),
            (768, sgate_d[0:128, 0:1024], False),
            (896, qte_d[:, 0:2048], True),
        ]
        aux_last = None
        for row, src, isbf in aux_srcs:
            if aux_last is not None:
                SY.wait_ge(s_st.h, aux_last)
            dst = x2_sb[:, 0:1024].bitcast(BF16) if isbf else x2_sb[:, 0:1024]
            m = s_ld.inc(SY.dma_start(out=dst, in_=src), 16)
            SY.wait_ge(s_ld.h, m)
            aux_last = s_st.inc(SY.dma_start(out=aux_d[row:row + 128, 0:1024], in_=x2_sb[:, 0:1024]), 16)
        SY.wait_ge(s_st.h, aux_last)
        m = s_ld.inc(SY.dma_start(out=x2_sb[0:16, 0:4096], in_=combT_d[0:16, 0:4096]), 16)
        SY.wait_ge(s_ld.h, m)
        s_st.inc(SY.dma_start(out=aux_d[512:528, 0:4096], in_=x2_sb[0:16, 0:4096]), 16)
        s_st.inc(SY.dma_start(out=aux_d[1152:1280, 0:4096], in_=Wo_sb[:, :].bitcast(F32)), 16)
        s_st.inc(SY.dma_start(out=aux_d[1408:1536, 0:256], in_=Gres_sb[:, :]), 16)

    if stop_after in ("probe_ol", "probe_sg", "probe_qt", "probe_gres", "probe_sinit"):
        SY.wait_ge(s_v.h, v_sinit)
        src_t = {"probe_ol": olocal_d, "probe_sg": sgate_d}.get(stop_after)
        if src_t is not None:
            for u in range(4):
                m = s_ld.inc(SY.dma_start(out=x2_sb[:, 0:E], in_=src_t[u * 128:(u + 1) * 128, :]), 16)
                SY.wait_ge(s_ld.h, m)
                s_st.inc(SY.dma_start(out=out_d[u * 128:(u + 1) * 128, :], in_=x2_sb[:, 0:E]), 16)
        elif stop_after == "probe_sinit":
            VE.wait_ge(s_ld.h, s_ld.n)
            mv = s_v.inc(VE.tensor_copy(x2_sb[:, 0:512], Sinit_sb[:, :]))
            VE.tensor_copy(x2_sb[:, 512:516], emask_sb[:, :])
            mv = mark_last("vector", s_v)
            SY.wait_ge(s_v.h, mv)
            stj = s_st.inc(SY.dma_start(out=out_d[0:128, 0:516], in_=x2_sb[:, 0:516]), 16)
            for j in range(4):
                VE.wait_ge(s_st.h, stj)
                mv = s_v.inc(VE.tensor_copy(x2_sb[:, 0:520], agland[:, j * 520:(j + 1) * 520]))
                SY.wait_ge(s_v.h, mv)
                stj = s_st.inc(SY.dma_start(out=out_d[128 * (1 + j):128 * (2 + j), 0:520],
                                            in_=x2_sb[:, 0:520]), 16)
            m = s_ld.inc(SY.dma_start(out=x2_sb[:, 0:520], in_=cc_in_d[:, :]), 16)
            SY.wait_ge(s_ld.h, m)
            s_st.inc(SY.dma_start(out=out_d[128 * 5:128 * 6, 0:520], in_=x2_sb[:, 0:520]), 16)
        elif stop_after == "probe_qt":
            m = s_ld.inc(SY.dma_start(out=qt2_sb[0][:, :], in_=qte_d[:, 0:8 * TILE]), 16)
            SY.wait_ge(s_ld.h, m)
            VE.wait_ge(s_ld.h, m)
            mv = s_v.inc(VE.tensor_copy(x2_sb[:, 0:E], qt2_sb[0][:, 0:1024]))
            SY.wait_ge(s_v.h, mv)
            s_st.inc(SY.dma_start(out=out_d[0:128, :], in_=x2_sb[:, 0:E]), 16)
        else:
            mv = s_v.inc(VE.tensor_copy(x2_sb[:, 0:8 * nchunk], Gres_sb[:, :]))
            SY.wait_ge(s_v.h, mv)
            s_st.inc(SY.dma_start(out=out_d[0:128, 0:8 * nchunk], in_=x2_sb[:, 0:8 * nchunk]), 16)
        SY.wait_ge(s_st.h, s_st.n)
        return finish([P2])
    if stop_after == "p2a":
        sentinel(1, 11.0, (s_v, s_v.n))
        VE.wait_ge(s_s.h, s_s.n)
        VE.wait_ge(s_t.h, s_t.n)
        sentinel(2, 11.0)
        return finish([P2])

    barrier()
    P2.close()

    P3 = Scope(nc)
    ln2T_ft = P3.sbuf("ln2T_ft", [128, 8 * FT], BF16)
    moe_acc = P3.sbuf("moe_acc", [128, nsub * E], F32)
    w1e_sb = [P3.sbuf(f"w1e{i}", [128, 8 * HID], BF16) for i in range(2)]
    w3e_sb = [P3.sbuf(f"w3e{i}", [128, 8 * HID], BF16) for i in range(2)]
    w2e_sb = [P3.sbuf(f"w2e{i}", [128, 4 * E], BF16) for i in range(4)]
    w2s_sb = P3.sbuf("w2s_sb", [128, 4 * E], BF16)
    cb_sb = [P3.sbuf(f"cb{i}", [128, FT], F32) for i in range(2)]
    act_sb = [P3.sbuf(f"act{i}", [128, 4 * FT], BF16) for i in range(5)]
    silu_scr = [P3.sbuf(f"silu{i}", [128, 512], BF16) for i in range(2)]
    silu2_scr = [P3.sbuf(f"silu2{i}", [128, 512], BF16) for i in range(2)]

    GROUPS = [[0, 1, 2, 3], [4, 5, 6, 7], [8, 9, 10, 11], [12, 13, 14, 15, None]]

    moe_free = s_v.n
    ln2ft_free = 0
    w13_free_wb = [0, 0]
    cb_free_wb = [0, 0]
    w2_free = 0
    act_free = 0
    st_out = [None] * nftile

    for FTi in range(nftile):
        T0 = FTi * (FT // TILE)
        if FTi >= 1:
            SY.wait_ge(s_t.h, ln2ft_free)
        for th2 in range(FT // TILE):
            s_ld.inc(SY.dma_start(
                out=ln2T_ft[:, :].rearrange("p (j s) -> p j s", j=8)[:, :, th2 * TILE:(th2 + 1) * TILE],
                in_=ln2T_d[:, (T0 + th2) * 8 * TILE:(T0 + th2 + 1) * 8 * TILE]
                .rearrange("p (j s) -> p j s", j=8)), 16)
        if FTi >= 1:
            SY.wait_ge(s_st.h, st_out[FTi - 1])
        ld_ft = s_ld.inc(SY.dma_start(
            out=moe_acc.rearrange("p (u e) -> p u e", e=E),
            in_=xp_d[FTi * FT:(FTi + 1) * FT, :].rearrange("(u p) e -> p u e", p=128)),
            16)

        if stop_after == "f0" and FTi == 0:
            VE.wait_ge(s_ld.h, ld_ft)
            sentinel(1, 13.0)
            return finish([P3])
        if stop_after == "f0x" and FTi == 0:
            SY.wait_ge(s_ld.h, ld_ft)
            for u in range(4):
                s_st.inc(SY.dma_start(out=out_d[u * 128:(u + 1) * 128, :],
                                      in_=moe_acc[:, u * E:(u + 1) * E]), 16)
            VE.wait_ge(s_st.h, s_st.n)
            return finish([P3])
        hstep = 0
        bank_act_mark = {0: 0, 1: 0}
        for gi, grp in enumerate(GROUPS):
            act_marks = {}
            ldw13 = {}
            ldw2 = {}
            ldcb = {}
            for ei, e in enumerate(grp):
                shared = e is None
                wb = (e % 2) if not shared else 0
                if shared:
                    GP.wait_ge(s_t.h, w13_free_wb[wb])
                    s_gp.inc(GP.dma_start(out=w1e_sb[wb].rearrange("p (a n) -> p a n", a=8), in_=sw1T_in.rearrange("(a p) n -> p a n", p=128)), 16)
                    ldw13[ei] = s_gp.inc(
                        GP.dma_start(out=w3e_sb[wb].rearrange("p (a n) -> p a n", a=8), in_=sw3T_in.rearrange("(a p) n -> p a n", p=128)), 16)
                    GP.wait_ge(s_t.h, w2_free)
                    ldw2[ei] = s_gp.inc(
                        GP.dma_start(out=w2s_sb.rearrange("p (a n) -> p a n", a=4), in_=sw2T_in.rearrange("(a p) n -> p a n", p=128)), 16)
                else:
                    SY.wait_ge(s_t.h, w13_free_wb[wb])
                    s_ld.inc(SY.dma_start(out=w1e_sb[wb].rearrange("p (a n) -> p a n", a=8), in_=w1b_d[e].rearrange("(a p) n -> p a n", p=128)), 16)
                    ldw13[ei] = s_ld.inc(
                        SY.dma_start(out=w3e_sb[wb].rearrange("p (a n) -> p a n", a=8), in_=w3b_d[e].rearrange("(a p) n -> p a n", p=128)), 16)
                    SY.wait_ge(s_t.h, w2_free)
                    ldw2[ei] = s_ld.inc(
                        SY.dma_start(out=w2e_sb[ei].rearrange("p (a n) -> p a n", a=4), in_=w2b_d[e].rearrange("(a p) n -> p a n", p=128)), 16)
                    GP.wait_ge(s_v.h, cb_free_wb[wb])
                    ldcb[ei] = s_gp.inc(GP.dma_start(out=cb_sb[wb][:, :],
                                                     in_=combT_d[e:e + 1, FTi * FT:(FTi + 1) * FT]
                                                     .broadcast_to([128, FT])), 16)
                w1 = w1e_sb[wb]
                w3 = w3e_sb[wb]
                act = act_sb[ei]
                TE.wait_ge(s_gp.h if shared else s_ld.h, ldw13[ei])
                TE.wait_ge(s_ld.h, ld_ft)
                first_of_expert = True
                for jj in range(4):
                    for th in range(nth):
                        bk = hstep % 2
                        hstep += 1
                        ph1 = pA if bk == 0 else pB
                        ph3 = pP[0] if bk == 0 else pP[1]
                        TE.wait_ge(s_v.h, bank_act_mark[bk])
                        for dj in range(8):
                            TE.matmul(ph1[:, :],
                                      w1[:, dj * HID + jj * 128: dj * HID + (jj + 1) * 128],
                                      ln2T_ft[:, dj * FT + th * 512: dj * FT + (th + 1) * 512],
                                      start=(dj == 0), stop=(dj == 7))
                        for dj in range(8):
                            TE.matmul(ph3[:, :],
                                      w3[:, dj * HID + jj * 128: dj * HID + (jj + 1) * 128],
                                      ln2T_ft[:, dj * FT + th * 512: dj * FT + (th + 1) * 512],
                                      start=(dj == 0), stop=(dj == 7))
                        m_h = mark_last("tensor", s_t)
                        SC.wait_ge(s_t.h, m_h)
                        SC.wait_ge(s_v.h, bank_act_mark[bk])
                        m_si = s_s.inc(SC.activation(silu_scr[bk][:, :], ph1[:, :], AF.Silu))
                        VE.wait_ge(s_s.h, m_si)
                        if shared:
                            VE.tensor_copy(silu2_scr[bk][:, :], ph3[:, :])
                        else:
                            VE.wait_ge(s_gp.h, ldcb[ei])
                            VE.tensor_mul(silu2_scr[bk][:, :], ph3[:, :],
                                          cb_sb[wb][:, th * 512:(th + 1) * 512])
                        if first_of_expert and act_free > 0:
                            VE.wait_ge(s_t.h, act_free)
                        first_of_expert = False
                        m_act = s_v.inc(VE.tensor_mul(act[:, jj * FT + th * 512: jj * FT + (th + 1) * 512],
                                                      silu_scr[bk][:, :], silu2_scr[bk][:, :]))
                        bank_act_mark[bk] = m_act
                        act_marks[(ei, jj, th)] = m_act
                w13_free_wb[wb] = s_t.n
                if not shared:
                    cb_free_wb[wb] = s_v.n
            if stop_after == "f1" and FTi == 0 and gi == 0:
                VE.wait_ge(s_s.h, s_s.n)
                VE.wait_ge(s_t.h, s_t.n)
                sentinel(1, 13.0)
                return finish([P3])
            for ei, e in enumerate(grp):
                TE.wait_ge(s_v.h, act_marks[(ei, 3, nth - 1)])
                TE.wait_ge(s_gp.h if e is None else s_ld.h, ldw2[ei])
            moe_free_par = [moe_free, moe_free]
            for u in range(nsub):
                up = u % 2
                banks = (pO[:, 0:512], pO[:, 512:1024]) if up == 0 else (pP[0][:, :], pP[1][:, :])
                if moe_free_par[up] > 0:
                    TE.wait_ge(s_v.h, moe_free_par[up])
                first_half = [True, True]
                for ei, e in enumerate(grp):
                    w2 = w2s_sb if e is None else w2e_sb[ei]
                    act = act_sb[ei]
                    for jj in range(4):
                        last = (ei == len(grp) - 1) and (jj == 3)
                        for nh2 in range(2):
                            TE.matmul(banks[nh2],
                                      act[:, jj * FT + u * 128: jj * FT + (u + 1) * 128],
                                      w2[:, jj * E + nh2 * 512: jj * E + (nh2 + 1) * 512],
                                      start=first_half[nh2], stop=last)
                            first_half[nh2] = False
                m_eo = mark_last("tensor", s_t)
                VE.wait_ge(s_t.h, m_eo)
                VE.wait_ge(s_ld.h, ld_ft)
                VE.tensor_add(moe_acc[:, u * E: u * E + 512],
                              moe_acc[:, u * E: u * E + 512], banks[0])
                moe_free_par[up] = s_v.inc(VE.tensor_add(moe_acc[:, u * E + 512:(u + 1) * E],
                                                         moe_acc[:, u * E + 512:(u + 1) * E], banks[1]))
            moe_free = moe_free_par[(nsub - 1) % 2]
            bank_act_mark[0] = max(bank_act_mark[0], moe_free_par[1])
            bank_act_mark[1] = max(bank_act_mark[1], moe_free_par[1])
            if stop_after in ("f2", "f2g1", "f2g2", "f2g3") and FTi == 0 and gi == {"f2": 0, "f2g1": 1, "f2g2": 2, "f2g3": 3}[stop_after]:
                VE.wait_ge(s_s.h, s_s.n)
                VE.wait_ge(s_t.h, s_t.n)
                sentinel(1, 13.0)
                return finish([P3])
            act_free = s_t.n
            w2_free = s_t.n
        if stop_after == "f3" and FTi == 0:
            VE.wait_ge(s_s.h, s_s.n)
            VE.wait_ge(s_t.h, s_t.n)
            sentinel(1, 13.0)
            return finish([P3])
        if stop_after == "f3s" and FTi == 0:
            SY.wait_ge(s_v.h, moe_free)
            s_st.inc(SY.dma_start(out=out_d[0:128, :], in_=moe_acc[:, 0:E]), 16)
            VE.wait_ge(s_st.h, s_st.n)
            sentinel(1, 13.0)
            return finish([P3])
        SY.wait_ge(s_v.h, moe_free)
        for u in range(nsub):
            st_out[FTi] = s_st.inc(SY.dma_start(
                out=out_d[FTi * FT + u * 128: FTi * FT + (u + 1) * 128, :],
                in_=moe_acc[:, u * E:(u + 1) * E]), 16)
        ln2ft_free = s_t.n

    return finish([P3])


def _host_inputs(inputs, core, tpc, use_ag):
    x = np.asarray(inputs["L0_attn_x"])
    if use_ag:
        b, q = core // 4, core % 4
        xs = np.ascontiguousarray(x[b].reshape(-1, E)[q * tpc:(q + 1) * tpc], dtype=np.float32)
    else:
        q = 0
        xs = np.ascontiguousarray(np.asarray(x, np.float32).reshape(-1, E)[:tpc])
    return {
        "x": xs,
        "Wq": np.ascontiguousarray(inputs["L0_attn_Wq"], np.float32),
        "Wk": np.ascontiguousarray(inputs["L0_attn_Wk"], np.float32),
        "Wv": np.ascontiguousarray(inputs["L0_attn_Wv"], np.float32),
        "Wo": np.ascontiguousarray(inputs["L0_attn_Wo"], np.float32),
        "G1": np.ascontiguousarray(inputs["L0_attn_G1"], np.float32),
        "G2": np.ascontiguousarray(inputs["L0_attn_G2"], np.float32),
        "delta_c": np.ascontiguousarray(np.asarray(inputs["L0_attn_delta"], np.float32).reshape(8, 128).T),
        "normw_c": np.ascontiguousarray(np.asarray(inputs["L0_attn_norm_w"], np.float32).reshape(8, 128).T),
        "normw_row": np.ascontiguousarray(np.asarray(inputs["L0_attn_norm_w"], np.float32).reshape(1, E)),
        "gateT": np.ascontiguousarray(np.asarray(inputs["L0_ffn_gate_w"], np.float32).T),
        "w1T": np.ascontiguousarray(np.asarray(inputs["L0_ffn_w1"], np.float32).transpose(0, 2, 1)),
        "w3T": np.ascontiguousarray(np.asarray(inputs["L0_ffn_w3"], np.float32).transpose(0, 2, 1)),
        "w2T": np.ascontiguousarray(np.asarray(inputs["L0_ffn_w2"], np.float32).transpose(0, 2, 1)),
        "sw1T": np.ascontiguousarray(np.asarray(inputs["L0_ffn_sw1"], np.float32).T),
        "sw3T": np.ascontiguousarray(np.asarray(inputs["L0_ffn_sw3"], np.float32).T),
        "sw2T": np.ascontiguousarray(np.asarray(inputs["L0_ffn_sw2"], np.float32).T),
        "ident_bf": np.eye(128, dtype=ml_dtypes.bfloat16),
        "ident_f32": np.eye(128, dtype=np.float32),
        "mask_st": np.ascontiguousarray(np.triu(np.ones((128, 128), np.float32))),
        "emask": np.ascontiguousarray(np.repeat((np.arange(4) == q).astype(np.float32)[None, :], 128, axis=0)),
    }


def kernel(**inputs):
    tpc = 4096
    nc = build_program(tpc, use_ag=True)
    in_maps = [_host_inputs(inputs, c, tpc, True) for c in range(8)]
    res = run_bass_kernel_spmd(nc, in_maps, core_ids=list(range(8)))
    out = np.empty((2, 16384, E), np.float32)
    for c in range(8):
        b, q = c // 4, c % 4
        out[b, q * tpc:(q + 1) * tpc, :] = res.results[c]["out"]
    return out


# version 11
# speedup vs baseline: 1.0314x; 1.0041x over previous
import numpy as np
import ml_dtypes
import concourse.bass as bass
import concourse.mybir as mybir
from concourse.bass_utils import run_bass_kernel_spmd

F32 = mybir.dt.float32
BF16 = mybir.dt.bfloat16
AF = mybir.ActivationFunctionType
ALU = mybir.AluOpType
AX = mybir.AxisListType

E = 1024
D = 64
H = 16
HID = 512
NEXP = 16
C = 128
TILE = 512
NCT = TILE // C
LN_EPS = 1e-5
RMS_EPS = 1e-5


class _RecInstr:
    __slots__ = ("incs",)

    def __init__(self):
        self.incs = []

    def then_inc(self, sem_handle, k):
        self.incs.append((sem_handle, k))
        return self


_SEQ_OPS = ("wait_ge", "nop", "drain")


def _ap_tensor_names(x):
    t = getattr(x, "tensor", None)
    if t is not None:
        return [t.name]
    return []


class Rec:

    def __init__(self, queue, hazard=False):
        self._q = queue
        self._hazard = hazard
        self._prev_outs = set()

    def __getattr__(self, op):
        q = self._q
        rec = self

        def call(*a, **kw):
            if rec._hazard and op not in _SEQ_OPS:
                aps = []
                outs = set()
                ins = set()
                pos_aps = [x for x in a if getattr(x, "tensor", None) is not None]
                if "out" in kw and getattr(kw["out"], "tensor", None) is not None:
                    outs.add(kw["out"].tensor.name)
                    pos_in_start = 0
                elif pos_aps:
                    outs.add(pos_aps[0].tensor.name)
                    pos_aps = pos_aps[1:]
                for x in pos_aps:
                    ins.add(x.tensor.name)
                for kname, x in kw.items():
                    if getattr(x, "tensor", None) is not None:
                        if kname in ("out", "accum_out", "out_max", "out_indices"):
                            outs.add(x.tensor.name)
                        else:
                            ins.add(x.tensor.name)
                if rec._prev_outs & (ins | outs):
                    q.append(("drain", (), {}, _RecInstr()))
                rec._prev_outs = outs
            ri = _RecInstr()
            q.append((op, a, kw, ri))
            return ri

        return call


class Sem:
    def __init__(self, handle):
        self.h = handle
        self.n = 0

    def inc(self, instr, k=1):
        instr.then_inc(self.h, k)
        self.n += k
        return self.n


def _replay(engine, queue):
    for op, a, kw, ri in queue:
        instr = getattr(engine, op)(*a, **kw)
        for sem_handle, k in ri.incs:
            instr.then_inc(sem_handle, k)


class Scope:
    def __init__(self, nc):
        self.nc = nc
        self.items = []

    def sbuf(self, name, shape, dtype):
        t = self.nc.sbuf_tensor(name, shape, dtype)
        h = t.__enter__()
        self.items.append(t)
        return h

    def close(self):
        for t in reversed(self.items):
            t.__exit__(None, None, None)
        self.items = []


def build_program(tpc, use_ag, stop_after=None, debug_tap=None, paranoid=0, aux=False):
    ntile = tpc // TILE
    nchunk = tpc // C
    FT = min(1024, tpc)
    nftile = tpc // FT
    nsub = FT // 128
    nth = FT // 512

    nc = bass.Bass("TRN2", target_bir_lowering=False, detect_race_conditions=False)

    def din(name, shape, dtype=F32):
        return nc.dram_tensor(name, shape, dtype, kind="ExternalInput").ap()

    def dint(name, shape, dtype=F32):
        return nc.dram_tensor(name, shape, dtype, kind="Internal").ap()

    x_in = din("x", [tpc, E])
    Wq_in = din("Wq", [E, E])
    Wk_in = din("Wk", [E, E])
    Wv_in = din("Wv", [E, E])
    Wo_in = din("Wo", [E, E])
    G1_in = din("G1", [E, D])
    G2_in = din("G2", [D, E])
    delta_in = din("delta_c", [128, 8])
    normw_in = din("normw_c", [128, 8])
    normw_row_in = din("normw_row", [1, E])
    gateT_in = din("gateT", [E, NEXP])
    w1T_in = din("w1T", [NEXP, E, HID])
    w3T_in = din("w3T", [NEXP, E, HID])
    w2T_in = din("w2T", [NEXP, HID, E])
    sw1T_in = din("sw1T", [E, HID])
    sw3T_in = din("sw3T", [E, HID])
    sw2T_in = din("sw2T", [HID, E])
    ident_bf_in = din("ident_bf", [128, 128], BF16)
    ident_f32_in = din("ident_f32", [128, 128])
    mask_st_in = din("mask_st", [128, 128])
    emask_in = din("emask", [128, 4])

    out_d = nc.dram_tensor("out", [tpc, E], F32, kind="ExternalOutput").ap()
    aux_d = nc.dram_tensor("aux", [1536, 4096], F32, kind="ExternalOutput").ap() if aux else None

    olocal_d = dint("olocal_d", [tpc, E])
    sgate_d = dint("sgate_d", [tpc, E])
    qte_d = dint("qte_d", [128, 8 * tpc], BF16)
    qto_d = dint("qto_d", [128, 8 * tpc], BF16)
    xp_d = dint("xp_d", [tpc, E])
    ln2T_d = dint("ln2T_d", [128, 8 * tpc], BF16)
    combT_d = dint("combT_d", [NEXP, tpc])
    w1b_d = dint("w1b_d", [NEXP, E, HID], BF16)
    w3b_d = dint("w3b_d", [NEXP, E, HID], BF16)
    w2b_d = dint("w2b_d", [NEXP, HID, E], BF16)
    if use_ag:
        cc_in_d = dint("cc_in", [128, 520])
        cc_out_d = dint("cc_out", [512, 520])

    P0 = Scope(nc)
    delta_neg = P0.sbuf("delta_neg", [128, 8], F32)
    normw_sb = P0.sbuf("normw_sb", [128, 8], F32)
    ident_bf = P0.sbuf("ident_bf_sb", [128, 128], BF16)
    ident_f32 = P0.sbuf("ident_f32_sb", [128, 128], F32)
    mask_st = P0.sbuf("mask_st_sb", [128, 128], F32)
    emask_sb = P0.sbuf("emask_sb", [128, 4], F32)
    ones_col = P0.sbuf("ones_col", [128, 1], F32)
    eps_col = P0.sbuf("eps_col", [128, 1], F32)
    S_sb = P0.sbuf("S_sb", [128, 512], F32)
    S_bf = P0.sbuf("S_bf_sb", [128, 512], BF16)
    Gamma_sb = P0.sbuf("Gamma_sb", [128, 8], F32)
    Gres_sb = P0.sbuf("Gres_sb", [128, 8 * nchunk], F32)
    lnS = P0.sbuf("lnS", [128, 32], F32)
    sq_scr = P0.sbuf("sq_scr", [128, E], F32)
    nwbc_sb = P0.sbuf("nwbc_sb", [128, E], F32)
    Wo_sb = P0.sbuf("Wo_sb", [128, 8 * E], BF16)
    gateT_sb = P0.sbuf("gateT_sb", [128, 8 * NEXP], BF16)

    psum_items = []

    def psum(name, shape, dtype):
        t = nc.psum_tensor(name, shape, dtype)
        h = t.__enter__()
        psum_items.append(t)
        return h

    pA = psum("pA", [128, 512], F32)
    pB = psum("pB", [128, 512], F32)
    pT = psum("pT", [128, 1024], BF16)
    pP = [psum("pP0", [128, 512], F32), psum("pP1", [128, 512], F32)]
    pO = psum("pO", [128, 1024], F32)
    pS = psum("pS", [128, 512], F32)

    sem_items = []

    def sem(name):
        t = nc.semaphore(name)
        h = t.__enter__()
        sem_items.append(t)
        return Sem(h)

    s_ld = sem("s_ld")
    s_gp = sem("s_gp")
    s_t = sem("s_t")
    s_v = sem("s_v")
    s_s = sem("s_s")
    s_cc = sem("s_cc")
    s_st = sem("s_st")

    s_bar = sem("s_bar")

    Q = {k: [] for k in ("tensor", "vector", "scalar", "sync", "gpsimd")}
    TE, VE, SC, SY, GP = (Rec(Q["tensor"]), Rec(Q["vector"], hazard=True),
                          Rec(Q["scalar"], hazard=True),
                          Rec(Q["sync"]), Rec(Q["gpsimd"]))
    ENGS = {"tensor": (TE, s_t), "vector": (VE, s_v), "scalar": (SC, s_s),
            "sync": (SY, s_ld), "gpsimd": (GP, s_gp)}

    def mark_last(qname, sem_obj, k=1):
        for op, a, kw, ri in reversed(Q[qname]):
            if op not in ("wait_ge", "nop", "drain", "engine_nop"):
                assert not ri.incs, f"double sync update on {qname}:{op}"
                ri.incs.append((sem_obj.h, k))
                sem_obj.n += k
                return sem_obj.n
        raise RuntimeError(f"no real instruction to mark in {qname}")

    def barrier():
        for name, s in (("tensor", s_t), ("vector", s_v), ("scalar", s_s)):
            for op, a, kw, ri in reversed(Q[name]):
                if op not in ("wait_ge", "nop", "drain", "engine_nop"):
                    if not any(h is s.h for h, _ in ri.incs):
                        assert not ri.incs, f"barrier collision on {name}:{op}"
                        ri.incs.append((s.h, 1))
                        s.n += 1
                    break
        waits = [(s_t, s_t.n), (s_v, s_v.n), (s_s, s_s.n),
                 (s_ld, s_ld.n), (s_st, s_st.n), (s_gp, s_gp.n), (s_cc, s_cc.n)]
        for name, (R_, _) in ENGS.items():
            for s, v in waits:
                R_.wait_ge(s.h, v)

    def finish(scopes):
        SY.wait_ge(s_st.h, s_st.n)
        block_cm = nc.Block()
        block = block_cm.__enter__()

        @block.tensor
        def _(t):
            _replay(t, Q["tensor"])

        @block.vector
        def _(v):
            _replay(v, Q["vector"])

        @block.scalar
        def _(sc):
            _replay(sc, Q["scalar"])

        @block.sync
        def _(sy):
            _replay(sy, Q["sync"])

        @block.gpsimd
        def _(g):
            _replay(g, Q["gpsimd"])

        block_cm.__exit__(None, None, None)
        for sc_ in scopes:
            sc_.close()
        for t in reversed(psum_items):
            t.__exit__(None, None, None)
        P0.close()
        for t in reversed(sem_items):
            t.__exit__(None, None, None)
        return nc

    P1 = Scope(nc)
    Wq_sb = P1.sbuf("Wq_sb", [128, 8 * E], BF16)
    Wk_sb = P1.sbuf("Wk_sb", [128, 8 * E], BF16)
    Wv_sb = P1.sbuf("Wv_sb", [128, 8 * E], BF16)
    G1_sb = P1.sbuf("G1_sb", [128, 8 * D], BF16)
    G2_sb = P1.sbuf("G2_sb", [64, E], BF16)
    xs_sb = P1.sbuf("xs", [128, 4 * E], F32)
    ln1_bf = [P1.sbuf(f"ln1_bf{i}", [128, E], BF16) for i in range(2)]
    ln1T_sb = P1.sbuf("ln1T_sb", [128, 8 * TILE], BF16)
    kT_j = [P1.sbuf(f"kTj{i}", [128, TILE], F32) for i in range(2)]
    qs_j = [P1.sbuf(f"qsj{i}", [128, TILE], F32) for i in range(2)]
    f_j = P1.sbuf("fj", [128, TILE], F32)
    Lraw_j = [P1.sbuf(f"Lrawj{i}", [128, TILE], F32) for i in range(2)]
    rrev_j = [P1.sbuf(f"rrevj{i}", [128, TILE + 4], F32) for i in range(2)]
    tmpq_j = [P1.sbuf(f"tmpqj{i}", [128, TILE], F32) for i in range(2)]
    colst = P1.sbuf("colst", [128, 64], F32)
    colst2_dbg = P1.sbuf("colst2_dbg", [128, 48], F32)
    Dcols = P1.sbuf("Dcols", [128, 32], F32)
    kscale = P1.sbuf("kscale", [128, 32], F32)
    qt_pf = [P1.sbuf("qt_pe", [128, 8 * TILE], BF16), P1.sbuf("qt_po", [128, 8 * TILE], BF16)]
    kt_pf = [P1.sbuf("kt_pe", [128, 8 * TILE], BF16), P1.sbuf("kt_po", [128, 8 * TILE], BF16)]
    khat_bf = P1.sbuf("khat_bf", [128, 8 * TILE], BF16)
    khat_tok = P1.sbuf("khat_tok", [128, E], BF16)
    V_tok = [P1.sbuf(f"V_tok{i}", [128, E], BF16) for i in range(2)]
    P_bf = P1.sbuf("P_bf", [128, 16 * C], BF16)
    o_scr = [P1.sbuf(f"o_scr{i}", [128, E], F32) for i in range(2)]
    sg_scr = [P1.sbuf(f"sg_scr{i}", [128, E], F32) for i in range(2)]
    ghT_sb = P1.sbuf("ghT_sb", [64, TILE], BF16)
    wcb_sb = [P1.sbuf(f"wcb{i}", [128, 2048], BF16) for i in range(2)]

    s_ld.inc(SY.dma_start(out=ident_bf[:, :], in_=ident_bf_in), 16)
    s_ld.inc(SY.dma_start(out=ident_f32[:, :], in_=ident_f32_in), 16)
    s_ld.inc(SY.dma_start(out=mask_st[:, :], in_=mask_st_in), 16)
    s_ld.inc(SY.dma_start(out=emask_sb[:, :], in_=emask_in), 16)
    s_ld.inc(SY.dma_start(out=delta_neg[:, :], in_=delta_in), 16)
    s_ld.inc(SY.dma_start(out=normw_sb[:, :], in_=normw_in), 16)
    ld_const = s_ld.inc(SY.dma_start(out=nwbc_sb[:, :], in_=normw_row_in.broadcast_to([128, E])), 16)

    s_gp.inc(GP.dma_start(out=Wq_sb.rearrange("p (a n) -> p a n", a=8), in_=Wq_in.rearrange("(a p) n -> p a n", p=128)), 16)
    s_gp.inc(GP.dma_start(out=Wk_sb.rearrange("p (a n) -> p a n", a=8), in_=Wk_in.rearrange("(a p) n -> p a n", p=128)), 16)
    s_gp.inc(GP.dma_start(out=Wv_sb.rearrange("p (a n) -> p a n", a=8), in_=Wv_in.rearrange("(a p) n -> p a n", p=128)), 16)
    s_gp.inc(GP.dma_start(out=G1_sb.rearrange("p (a n) -> p a n", a=8), in_=G1_in.rearrange("(a p) n -> p a n", p=128)), 16)
    gp_attnw = s_gp.inc(GP.dma_start(out=G2_sb[:, :], in_=G2_in), 16)
    gp_wo = s_gp.inc(GP.dma_start(out=Wo_sb.rearrange("p (a n) -> p a n", a=8),
                                  in_=Wo_in.rearrange("(a p) n -> p a n", p=128)), 16)
    gp_gateT = s_gp.inc(GP.dma_start(out=gateT_sb.rearrange("p (a n) -> p a n", a=8),
                                     in_=gateT_in.rearrange("(a p) n -> p a n", p=128)), 16)
    if stop_after == "p0b":
        GP.wait_ge(s_gp.h, s_gp.n)
        return finish([P1])

    VE.wait_ge(s_ld.h, ld_const)
    VE.tensor_scalar_mul(delta_neg[:, :], delta_neg[:, :], -1.0)
    VE.memset(ones_col[:, :], 1.0)
    VE.memset(eps_col[:, :], RMS_EPS)
    VE.memset(S_sb[:, :], 0.0)
    VE.memset(S_bf[:, :], 0.0)
    VE.memset(Gamma_sb[:, :], 1.0)
    VE.memset(rrev_j[0][:, TILE:], 1.0)
    VE.memset(rrev_j[1][:, TILE:], 1.0)
    VE.memset(qt_pf[0][:, :], 0.0)
    VE.memset(qt_pf[1][:, :], 0.0)
    VE.memset(kt_pf[0][:, :], 0.0)
    VE.memset(kt_pf[1][:, :], 0.0)
    v_const = mark_last("vector", s_v)
    if stop_after == "p0a":
        SY.wait_ge(s_ld.h, s_ld.n)
        return finish([P1])

    if stop_after == "sent1":
        VE.wait_ge(s_gp.h, gp_attnw)
        sentinel(1, 7.0)
        return finish([P1])
    cast_jobs = []
    for src, dst in [(w1T_in, w1b_d), (w3T_in, w3b_d), (w2T_in, w2b_d)]:
        fs = src.rearrange("e a b -> (e a b)")
        fd = dst.rearrange("e a b -> (e a b)")
        total = NEXP * E * HID
        step = 128 * 2048
        for off in range(0, total, step):
            cast_jobs.append((fs[off:off + step].rearrange("(p n) -> p n", p=128),
                              fd[off:off + step].rearrange("(p n) -> p n", p=128)))
    cast_marks = []
    for i, (src, dst) in enumerate(cast_jobs):
        buf = wcb_sb[i % 2]
        if i >= 2:
            GP.wait_ge(s_gp.h, cast_marks[i - 2])
        m = s_gp.inc(GP.dma_start(out=buf[:, :], in_=src), 16)
        GP.wait_ge(s_gp.h, m)
        cast_marks.append(s_gp.inc(GP.dma_start(out=dst, in_=buf[:, :]), 16))
    def sentinel(blk, val, eng_wait=None):
        if eng_wait is not None:
            VE.wait_ge(eng_wait[0].h, eng_wait[1])
        m = s_v.inc(VE.memset(sq_scr[:, :], float(val)))
        SY.wait_ge(s_v.h, m)
        s_st.inc(SY.dma_start(out=out_d[blk * 128:(blk + 1) * 128, :], in_=sq_scr[:, :]), 16)

    if stop_after in ("sent", "sent0", "sent1", "sent2"):
        sentinel(0, 7.0)
    if stop_after == "sent0":
        return finish([P1])
    gp_cast_done = s_gp.n
    if stop_after == "sent2":
        VE.wait_ge(s_gp.h, gp_cast_done)
        sentinel(1, 7.0)
        return finish([P1])
    if stop_after == "p0":
        GP.wait_ge(s_gp.h, s_gp.n)
        SY.wait_ge(s_ld.h, s_ld.n)
        return finish([P1])

    x_free = v_const
    pT_free = v_const
    pS_free = v_const
    pO_free = v_const
    pAB_free = None
    last_P_evacs = None
    st_p1 = [None] * ntile
    ln1bf_free = [0, 0]
    Vtok_free = [0, 0]
    ost_free = [0, 0]
    sgst_free = [0, 0]
    s_kT_m = [None, None]
    s_qs_m = [None, None]
    v_j_m = [None, None]
    scale_free = [None, None]
    p1_te_prev = 0

    for T in range(ntile):
        SY.wait_ge(s_v.h, x_free)
        ld_x = s_ld.inc(SY.dma_start(
            out=xs_sb.rearrange("p (c e) -> p c e", c=4),
            in_=x_in[T * TILE:(T + 1) * TILE, :].rearrange("(c p) e -> p c e", p=128)), 16)
        if T == 0:
            TE.wait_ge(s_gp.h, gp_attnw)
            TE.wait_ge(s_ld.h, ld_const)
        if stop_after == "probe_stats" and T == 0:
            VE.wait_ge(s_ld.h, ld_x)
            SC.wait_ge(s_ld.h, ld_x)
            for c in range(4):
                VE.reduce_sum(colst[:, c:c + 1], xs_sb[:, c * E:(c + 1) * E], axis=AX.X)
                SC.activation(sq_scr[:, :], xs_sb[:, c * E:(c + 1) * E], AF.Square,
                              accum_out=colst2_dbg[:, c:c + 1])
            mv = mark_last("vector", s_v)
            ms = mark_last("scalar", s_s)
            VE.wait_ge(s_s.h, ms)
            mv2 = s_v.inc(VE.tensor_copy(xs_sb[:, 0:4], colst[:, 0:4]))
            mv3 = s_v.inc(VE.tensor_copy(xs_sb[:, 4:8], colst2_dbg[:, 0:4]))
            SY.wait_ge(s_v.h, mv3)
            s_st.inc(SY.dma_start(out=out_d[0:128, 0:8], in_=xs_sb[:, 0:8]), 16)
            SY.wait_ge(s_st.h, s_st.n)
            return finish([P1])

        if stop_after == "sent" and T == 0:
            sentinel(1, 7.0)
        v_ln = [None] * NCT
        for c in range(NCT):
            xsl = xs_sb[:, c * E:(c + 1) * E]
            SC.wait_ge(s_ld.h, ld_x)
            SC.activation(sq_scr[:, :], xsl, AF.Square, accum_out=lnS[:, 2:3])
            m_ssq = mark_last("scalar", s_s)
            VE.wait_ge(s_ld.h, ld_x)
            VE.reduce_sum(lnS[:, 0:1], xsl, axis=AX.X)
            VE.tensor_scalar_mul(lnS[:, 1:2], lnS[:, 0:1], 1.0 / E)
            VE.wait_ge(s_s.h, m_ssq)
            VE.tensor_mul(lnS[:, 5:6], lnS[:, 1:2], lnS[:, 1:2])
            VE.tensor_scalar(lnS[:, 6:7], lnS[:, 2:3], 1.0 / E, None, ALU.mult)
            VE.tensor_sub(lnS[:, 7:8], lnS[:, 6:7], lnS[:, 5:6])
            m_var = s_v.inc(VE.tensor_scalar(lnS[:, 3:4], lnS[:, 7:8], LN_EPS, None, ALU.add))
            SC.wait_ge(s_v.h, m_var)
            m_sqrt = s_s.inc(SC.activation(lnS[:, 4:5], lnS[:, 3:4], AF.Sqrt))
            VE.wait_ge(s_s.h, m_sqrt)
            VE.reciprocal(lnS[:, 8:9], lnS[:, 4:5])
            VE.tensor_mul(lnS[:, 9:10], lnS[:, 1:2], lnS[:, 8:9])
            if ln1bf_free[c % 2] > 0:
                VE.wait_ge(s_t.h, ln1bf_free[c % 2])
            if stop_after == "probe_lnx" and T == 0:
                VE.tensor_copy(colst2_dbg[:, c * 10:(c + 1) * 10], lnS[:, 0:10])
            v_ln[c] = s_v.inc(VE.tensor_scalar(ln1_bf[c % 2][:, :], xsl,
                                               lnS[:, 8:9], lnS[:, 9:10], ALU.mult, ALU.subtract))
            TE.wait_ge(s_v.h, v_ln[c])
            TE.wait_ge(s_v.h, pT_free)
            for j in range(8):
                TE.transpose(pT[:, j * 128:(j + 1) * 128],
                             ln1_bf[c % 2][:, j * 128:(j + 1) * 128], ident_bf[:, :])
            m_t = mark_last("tensor", s_t)
            VE.wait_ge(s_t.h, m_t)
            if stop_after == "probe_pT" and T == 0 and c == 0:
                mv = s_v.inc(VE.tensor_copy(xs_sb[:, 0:E], pT[:, :]))
                SY.wait_ge(s_v.h, mv)
                m2 = s_st.inc(SY.dma_start(out=out_d[0:128, :], in_=xs_sb[:, 0:E]), 16)
                VE.wait_ge(s_st.h, m2)
                mv = s_v.inc(VE.tensor_copy(xs_sb[:, 0:128], ident_bf[:, :]))
                SY.wait_ge(s_v.h, mv)
                m2 = s_st.inc(SY.dma_start(out=out_d[128:256, 0:128], in_=xs_sb[:, 0:128]), 16)
                VE.wait_ge(s_st.h, m2)
                mv = s_v.inc(VE.tensor_copy(xs_sb[:, 0:E], ln1_bf[0][:, :]))
                SY.wait_ge(s_v.h, mv)
                s_st.inc(SY.dma_start(out=out_d[256:384, :], in_=xs_sb[:, 0:E]), 16)
                SY.wait_ge(s_st.h, s_st.n)
                return finish([P1])
            VE.tensor_copy(
                ln1T_sb[:, :].rearrange("p (j s) -> p j s", j=8)[:, :, c * 128:(c + 1) * 128],
                pT[:, :].rearrange("p (j s) -> p j s", j=8))
            pT_free = mark_last("vector", s_v)
            ln1bf_free[c % 2] = m_t
            barrier() if paranoid else None
        v_lnT_done = s_v.n
        if stop_after == "probe_lnx" and T == 0:
            VE.wait_ge(s_v.h, s_v.n)
            mv = s_v.inc(VE.tensor_copy(xs_sb[:, 0:40], colst2_dbg[:, 0:40]))
            SY.wait_ge(s_v.h, mv)
            s_st.inc(SY.dma_start(out=out_d[0:128, 0:40], in_=xs_sb[:, 0:40]), 16)
            VE.wait_ge(s_st.h, s_st.n)
            mv = s_v.inc(VE.tensor_copy(xs_sb[:, 0:E], ln1_bf[1][:, :]))
            SY.wait_ge(s_v.h, mv)
            s_st.inc(SY.dma_start(out=out_d[128:256, :], in_=xs_sb[:, 0:E]), 16)
            SY.wait_ge(s_st.h, s_st.n)
            return finish([P1])
        if stop_after == "probe_V2":
            TE.wait_ge(s_v.h, v_lnT_done)
            for nh in range(2):
                for j_in in range(8):
                    TE.matmul(pO[:, nh * 512:(nh + 1) * 512],
                              ln1T_sb[:, j_in * TILE + 0 * 128: j_in * TILE + 1 * 128],
                              Wv_sb[:, j_in * E + nh * 512: j_in * E + (nh + 1) * 512],
                              start=(j_in == 0), stop=(j_in == 7))
            mV = mark_last("tensor", s_t)
            VE.wait_ge(s_t.h, mV)
            mv = s_v.inc(VE.tensor_copy(xs_sb[:, 0:E], pO[:, :]))
            SY.wait_ge(s_v.h, mv)
            s_st.inc(SY.dma_start(out=out_d[0:128, :], in_=xs_sb[:, 0:E]), 16)
            SY.wait_ge(s_st.h, s_st.n)
            return finish([P1])
        if stop_after == "p1a":
            sentinel(1, 9.0, (s_v, v_lnT_done))
            return finish([P1])
        if stop_after == "probe_ln1T":
            SY.wait_ge(s_v.h, v_lnT_done)
            mv = s_v.inc(VE.tensor_copy(xs_sb[:, 0:E], ln1T_sb[:, 0:E]))
            SY.wait_ge(s_v.h, mv)
            s_st.inc(SY.dma_start(out=out_d[0:128, :], in_=xs_sb[:, 0:E]), 16)
            mv2 = s_v.inc(VE.tensor_copy(xs_sb[:, 0:E], ln1_bf[1][:, :]))
            SY.wait_ge(s_v.h, mv2)
            s_st.inc(SY.dma_start(out=out_d[128:256, :], in_=xs_sb[:, 0:E]), 16)
            SY.wait_ge(s_st.h, s_st.n)
            return finish([P1])

        jlvl = {"j_mm": 0, "j_ve1": 1, "j_ve2": 2, "j_sc": 3}.get(stop_after, 99)
        s_scale = [None] * 8
        TE.wait_ge(s_v.h, v_lnT_done)
        if p1_te_prev > 0:
            VE.wait_ge(s_t.h, p1_te_prev)
        for j in range(8):
            jb = j % 2
            if j >= 1:
                TE.wait_ge(s_s.h, s_qs_m[(j - 1) % 2])
            elif pAB_free is not None:
                TE.wait_ge(s_s.h, pAB_free)
            for j_in in range(8):
                TE.matmul(pA[:, :], Wk_sb[:, j_in * E + j * 128: j_in * E + j * 128 + 128],
                          ln1T_sb[:, j_in * TILE:(j_in + 1) * TILE],
                          start=(j_in == 0), stop=(j_in == 7))
            m_xk = mark_last("tensor", s_t)
            for j_in in range(8):
                TE.matmul(pB[:, :], Wq_sb[:, j_in * E + j * 128: j_in * E + j * 128 + 128],
                          ln1T_sb[:, j_in * TILE:(j_in + 1) * TILE],
                          start=(j_in == 0), stop=(j_in == 7))
            m_q = mark_last("tensor", s_t)
            SC.wait_ge(s_t.h, m_xk)
            if v_j_m[jb] is not None:
                SC.wait_ge(s_v.h, v_j_m[jb])
            SC.activation(kT_j[jb][:, :], pA[:, :], AF.Sigmoid,
                          bias=delta_neg[:, j:j + 1], scale=-1.0)
            s_kT_m[jb] = mark_last("scalar", s_s)
            SC.wait_ge(s_t.h, m_q)
            SC.activation(qs_j[jb][:, :], pB[:, :], AF.Silu)
            s_qs_m[jb] = mark_last("scalar", s_s)
            if jlvl < 1:
                v_j_m[jb] = s_v.n if v_j_m[jb] is None else v_j_m[jb]
                s_scale[j] = s_v.n
                scale_free[jb] = s_scale[j]
                continue
            VE.wait_ge(s_s.h, s_kT_m[jb])
            VE.tensor_scalar(f_j[:, :], kT_j[jb][:, :], -1.0, 1.0, ALU.mult, ALU.add)
            if scale_free[jb] is not None:
                VE.wait_ge(s_s.h, scale_free[jb])
            VE.tensor_tensor_scan(Lraw_j[jb][:, :], f_j[:, :], f_j[:, :], 1.0, ALU.mult, ALU.bypass)
            VE.tensor_tensor_scan(rrev_j[jb][:, 0:TILE][:, ::-1], f_j[:, ::-1], f_j[:, ::-1],
                                  1.0, ALU.mult, ALU.bypass)
            if jlvl < 2:
                v_j_m[jb] = mark_last("vector", s_v)
                s_scale[j] = s_v.n
                scale_free[jb] = None
                continue
            ecols = Lraw_j[jb][:, 127::128]
            VE.reciprocal(colst[:, j * 8: j * 8 + 4], ecols)
            VE.reciprocal(colst[:, j * 8 + 4: j * 8 + 7], rrev_j[jb][:, 128::128][:, 0:3])
            VE.wait_ge(s_s.h, s_qs_m[jb])
            for c in range(NCT):
                rr = colst[:, j * 8 + 4 + c: j * 8 + 5 + c] if c < 3 else ones_col[:, 0:1]
                VE.scalar_tensor_tensor(khat_bf[:, j * TILE + c * 128: j * TILE + (c + 1) * 128],
                                        rrev_j[jb][:, c * 128 + 1:(c + 1) * 128 + 1],
                                        rr, kT_j[jb][:, c * 128:(c + 1) * 128],
                                        ALU.mult, ALU.mult)
                bl = colst[:, j * 8 + c - 1: j * 8 + c] if c > 0 else ones_col[:, 0:1]
                VE.scalar_tensor_tensor(Dcols[:, c * 8 + j: c * 8 + j + 1],
                                        Lraw_j[jb][:, (c + 1) * 128 - 1:(c + 1) * 128],
                                        1.0, bl, ALU.mult, ALU.mult)
                blv = Lraw_j[jb][:, c * 128 - 1: c * 128] if c > 0 else ones_col[:, 0:1]
                VE.scalar_tensor_tensor(kscale[:, c * 8 + j: c * 8 + j + 1],
                                        blv, 1.0, colst[:, j * 8 + c: j * 8 + c + 1],
                                        ALU.mult, ALU.mult)
            if jlvl < 3:
                v_j_m[jb] = mark_last("vector", s_v)
                s_scale[j] = s_v.n
                scale_free[jb] = None
                continue
            if T >= 1 and j == 0:
                VE.wait_ge(s_st.h, st_p1[T - 1])
            for c in range(NCT):
                sl = slice(j * TILE + c * 128, j * TILE + (c + 1) * 128)
                lsl = slice(c * 128, (c + 1) * 128)
                bl = colst[:, j * 8 + c - 1: j * 8 + c] if c > 0 else ones_col[:, 0:1]
                el = kscale[:, c * 8 + j: c * 8 + j + 1]
                VE.scalar_tensor_tensor(qt_pf[0][0:64, sl], qs_j[jb][0:64, lsl],
                                        bl[0:64, :], Lraw_j[jb][0:64, lsl], ALU.mult, ALU.mult)
                VE.scalar_tensor_tensor(qt_pf[1][64:128, sl], qs_j[jb][64:128, lsl],
                                        bl[64:128, :], Lraw_j[jb][64:128, lsl], ALU.mult, ALU.mult)
                VE.tensor_scalar_mul(kt_pf[0][0:64, sl], khat_bf[0:64, sl], el[0:64, :])
                VE.tensor_scalar_mul(kt_pf[1][64:128, sl], khat_bf[64:128, sl], el[64:128, :])
            v_j_m[jb] = mark_last("vector", s_v)
            s_scale[j] = v_j_m[jb]
            scale_free[jb] = None
            barrier() if paranoid else None
        pAB_free = s_qs_m[1]
        if stop_after == "probe_wv":
            SY.wait_ge(s_s.h, s_s.n)
            for half in range(4):
                VE.wait_ge(s_st.h, s_st.n)
                mv = s_v.inc(VE.tensor_copy(xs_sb[:, 0:E], Wv_sb[:, half * E:(half + 1) * E]))
                SY.wait_ge(s_v.h, mv)
                s_st.inc(SY.dma_start(out=out_d[half * 128:(half + 1) * 128, :], in_=xs_sb[:, 0:E]), 16)
            SY.wait_ge(s_st.h, s_st.n)
            return finish([P1])
        if stop_after in ("probe_j", "j_mm", "j_ve1", "j_ve2", "j_sc"):
            SY.wait_ge(s_s.h, s_s.n)
            SY.wait_ge(s_v.h, s_v.n)
            dumps = [qs_j[0][:, :], kT_j[0][:, :], Lraw_j[0][:, :], tmpq_j[0][:, :]]
            for i, dsrc in enumerate(dumps):
                VE.wait_ge(s_st.h, s_st.n)
                mv = s_v.inc(VE.tensor_copy(xs_sb[:, i * TILE:(i + 1) * TILE], dsrc))
                SY.wait_ge(s_v.h, mv)
                s_st.inc(SY.dma_start(out=out_d[i * 128:(i + 1) * 128, 0:TILE],
                                      in_=xs_sb[:, i * TILE:(i + 1) * TILE]), 16)
            VE.wait_ge(s_st.h, s_st.n)
            mv = s_v.inc(VE.tensor_copy(xs_sb[:, 0:TILE], qt_pf[0][:, 7 * TILE:8 * TILE]))
            SY.wait_ge(s_v.h, mv)
            s_st.inc(SY.dma_start(out=out_d[0:128, TILE:2 * TILE], in_=xs_sb[:, 0:TILE]), 16)
            VE.wait_ge(s_st.h, s_st.n)
            mv = s_v.inc(VE.tensor_copy(xs_sb[:, 0:64], colst[:, :]))
            SY.wait_ge(s_v.h, mv)
            s_st.inc(SY.dma_start(out=out_d[128:256, TILE:TILE + 64], in_=xs_sb[:, 0:64]), 16)
            SY.wait_ge(s_st.h, s_st.n)
            return finish([P1])
        if stop_after == "p1b":
            VE.wait_ge(s_v.h, s_scale[7])
            sentinel(1, 9.0)
            return finish([P1])

        if stop_after == "sent" and T == 0:
            sentinel(2, 7.0)
        TE.wait_ge(s_s.h, s_kT_m[0])
        TE.wait_ge(s_s.h, s_kT_m[1])
        for j_in in range(8):
            TE.matmul(pA[0:64, :], G1_sb[:, j_in * D:(j_in + 1) * D],
                      ln1T_sb[:, j_in * TILE:(j_in + 1) * TILE],
                      start=(j_in == 0), stop=(j_in == 7))
        m_g1 = mark_last("tensor", s_t)
        VE.wait_ge(s_t.h, m_g1)
        v_ghT = s_v.inc(VE.tensor_copy(ghT_sb[:, :], pA[0:64, :]))
        barrier() if paranoid else None
        if stop_after == "p1c":
            VE.wait_ge(s_v.h, v_ghT)
            sentinel(1, 9.0)
            return finish([P1])

        if stop_after == "sent" and T == 0:
            sentinel(3, 7.0)
        glvl = {"g0": 0, "g1": 1, "g2": 2, "g3": 3}.get(stop_after, 99)
        v_O_evac = [None] * NCT
        s_gate = [None] * NCT
        for c in range(NCT):
            g = T * NCT + c
            cb2 = c % 2
            TE.wait_ge(s_v.h, pO_free)
            TE.wait_ge(s_v.h, s_scale[7])
            for nh in range(2):
                for j_in in range(8):
                    TE.matmul(pO[:, nh * 512:(nh + 1) * 512],
                              ln1T_sb[:, j_in * TILE + c * 128: j_in * TILE + (c + 1) * 128],
                              Wv_sb[:, j_in * E + nh * 512: j_in * E + (nh + 1) * 512],
                              start=(j_in == 0), stop=(j_in == 7))
            m_V = mark_last("tensor", s_t)
            VE.wait_ge(s_t.h, m_V)
            if Vtok_free[cb2] > 0:
                VE.wait_ge(s_t.h, Vtok_free[cb2])
            VE.tensor_copy(V_tok[cb2][:, :], pO[:, :])
            pO_free = mark_last("vector", s_v)
            v_V = pO_free
            barrier() if paranoid else None
            TE.wait_ge(s_v.h, v_ghT)
            if c > 0:
                TE.wait_ge(s_s.h, s_gate[c - 1])
            for nh in range(2):
                TE.matmul(pB[:, :] if nh == 0 else pA[:, :],
                          ghT_sb[:, c * 128:(c + 1) * 128],
                          G2_sb[:, nh * 512:(nh + 1) * 512], start=True, stop=True)
            m_gt = mark_last("tensor", s_t)
            SC.wait_ge(s_t.h, m_gt)
            if sgst_free[cb2] > 0:
                SC.wait_ge(s_st.h, sgst_free[cb2])
            SC.activation(sg_scr[cb2][:, 0:512], pB[:, :], AF.Sigmoid)
            SC.activation(sg_scr[cb2][:, 512:1024], pA[:, :], AF.Sigmoid)
            s_gate[c] = mark_last("scalar", s_s)
            SY.wait_ge(s_s.h, s_gate[c])
            sgst_free[cb2] = s_st.inc(SY.dma_start(
                out=sgate_d[(T * NCT + c) * C:(T * NCT + c + 1) * C, :],
                in_=sg_scr[cb2][:, :]), 16)

            if glvl < 1:
                continue
            VE.tensor_copy(Gres_sb[:, g * 8:(g + 1) * 8], Gamma_sb[:, :])
            v_Sbf = s_v.inc(VE.tensor_copy(S_bf[:, :], S_sb[:, :]))

            TE.wait_ge(s_v.h, pT_free)
            for j in range(8):
                TE.transpose(pT[:, j * 128:(j + 1) * 128],
                             khat_bf[:, j * TILE + c * 128: j * TILE + (c + 1) * 128],
                             ident_bf[:, :])
            m_kh = mark_last("tensor", s_t)
            VE.wait_ge(s_t.h, m_kh)
            VE.tensor_copy(khat_tok[:, :], pT[:, :])
            pT_free = mark_last("vector", s_v)
            v_khat = pT_free
            barrier() if paranoid else None

            if glvl < 2:
                continue
            v_P = [None] * 4
            for wv in range(4):
                pw = pP[wv % 2]
                if wv >= 2:
                    TE.wait_ge(s_v.h, v_P[wv - 2])
                elif last_P_evacs is not None:
                    TE.wait_ge(s_v.h, last_P_evacs[wv % 2])
                for hh in range(4):
                    h = wv * 4 + hh
                    j, par = h // 2, h % 2
                    TE.matmul(pw[:, hh * 128:(hh + 1) * 128],
                              kt_pf[par][:, j * TILE + c * 128: j * TILE + (c + 1) * 128],
                              qt_pf[par][:, j * TILE + c * 128: j * TILE + (c + 1) * 128],
                              start=True, stop=True)
                m_P = mark_last("tensor", s_t)
                VE.wait_ge(s_t.h, m_P)
                VE.tensor_tensor(P_bf[:, wv * 512:(wv + 1) * 512], pw[:, :],
                                 mask_st[:, :].unsqueeze(1).broadcast_to([128, 4, 128]),
                                 ALU.mult)
                v_P[wv] = mark_last("vector", s_v)
            last_P_evacs = [v_P[2], v_P[3]]
            barrier() if paranoid else None
            if stop_after == "probe_P" and T == 0 and c == 0:
                VE.wait_ge(s_v.h, v_P[3])
                for half in range(2):
                    mv = s_v.inc(VE.tensor_copy(xs_sb[:, 0:E], P_bf[:, half * 1024:(half + 1) * 1024]))
                    SY.wait_ge(s_v.h, mv)
                    m2 = s_st.inc(SY.dma_start(out=out_d[half * 128:(half + 1) * 128, :],
                                               in_=xs_sb[:, 0:E]), 16)
                    VE.wait_ge(s_st.h, m2)
                mv = s_v.inc(VE.tensor_copy(xs_sb[:, 0:E], V_tok[cb2][:, :]))
                SY.wait_ge(s_v.h, mv)
                s_st.inc(SY.dma_start(out=out_d[256:384, :], in_=xs_sb[:, 0:E]), 16)
                mv = s_v.inc(VE.tensor_copy(xs_sb[:, 0:E], qt_pf[0][:, 0:E].bitcast(F32).bitcast(BF16)[:, 0:1024] if False else qt_pf[0][:, 0:1024]))
                SY.wait_ge(s_v.h, mv)
                s_st.inc(SY.dma_start(out=out_d[384:512, :], in_=xs_sb[:, 0:E]), 16)
                SY.wait_ge(s_st.h, s_st.n)
                return finish([P1])

            if glvl < 3:
                continue
            TE.wait_ge(s_v.h, pO_free)
            TE.wait_ge(s_v.h, v_Sbf)
            TE.wait_ge(s_v.h, v_P[3])
            for h in range(H):
                j, par = h // 2, h % 2
                TE.matmul(pO[:, h * D:(h + 1) * D],
                          P_bf[:, h * 128:(h + 1) * 128],
                          V_tok[cb2][:, h * D:(h + 1) * D],
                          start=True, stop=False)
                TE.matmul(pO[:, h * D:(h + 1) * D],
                          qt_pf[par][:, j * TILE + c * 128: j * TILE + (c + 1) * 128],
                          S_bf[:, j * D:(j + 1) * D],
                          start=False, stop=True)
            m_O = mark_last("tensor", s_t)
            VE.wait_ge(s_t.h, m_O)
            if ost_free[cb2] > 0:
                VE.wait_ge(s_st.h, ost_free[cb2])
            VE.tensor_copy(o_scr[cb2][:, :], pO[:, :])
            pO_free = mark_last("vector", s_v)
            v_O_evac[c] = pO_free
            barrier() if paranoid else None
            SY.wait_ge(s_v.h, v_O_evac[c])
            ost_free[cb2] = s_st.inc(SY.dma_start(
                out=olocal_d[(T * NCT + c) * C:(T * NCT + c + 1) * C, :],
                in_=o_scr[cb2][:, :]), 16)

            if glvl < 4:
                continue
            TE.wait_ge(s_v.h, v_khat)
            for r in range(2):
                TE.wait_ge(s_v.h, pS_free)
                for jb2 in range(4):
                    j = 4 * r + jb2
                    for par in range(2):
                        h = 2 * j + par
                        po = par * 64
                        TE.matmul(pS[po:po + 64, jb2 * 128: jb2 * 128 + D],
                                  khat_tok[:, h * D:(h + 1) * D],
                                  V_tok[cb2][:, h * D:(h + 1) * D],
                                  start=True, stop=True,
                                  tile_position=(0, po))
                m_Sg = mark_last("tensor", s_t)
                VE.wait_ge(s_t.h, m_Sg)
                for jb2 in range(4):
                    j = 4 * r + jb2
                    VE.scalar_tensor_tensor(S_sb[:, j * D:(j + 1) * D], S_sb[:, j * D:(j + 1) * D],
                                            Dcols[:, c * 8 + j: c * 8 + j + 1],
                                            pS[:, jb2 * 128: jb2 * 128 + D],
                                            ALU.mult, ALU.add)
                    VE.tensor_scalar_mul(Gamma_sb[:, j:j + 1], Gamma_sb[:, j:j + 1],
                                         Dcols[:, c * 8 + j: c * 8 + j + 1])
                pS_free = mark_last("vector", s_v)
            Vtok_free[cb2] = m_Sg
            barrier() if paranoid else None

        x_free = v_ln[NCT - 1]
        pAB_free = s_gate[NCT - 1]
        p1_te_prev = s_t.n
        SY.wait_ge(s_v.h, s_scale[7])
        s_st.inc(SY.dma_start(out=qte_d[:, T * 8 * TILE:(T + 1) * 8 * TILE], in_=qt_pf[0][:, :]), 16)
        if glvl < 99 and T == ntile - 1:
            sentinel(1, 9.0, (s_v, s_v.n))
            VE.wait_ge(s_s.h, s_s.n)
            VE.wait_ge(s_t.h, s_t.n)
            sentinel(2, 9.0)
            return finish([P1])
        elif glvl < 99:
            pass
        st_p1[T] = s_st.inc(SY.dma_start(out=qto_d[:, T * 8 * TILE:(T + 1) * 8 * TILE], in_=qt_pf[1][:, :]), 16)

    if stop_after == "p1":
        sentinel(1, 9.0, (s_v, s_v.n))
        VE.wait_ge(s_s.h, s_s.n)
        VE.wait_ge(s_t.h, s_t.n)
        sentinel(2, 9.0)
        return finish([P1])

    if stop_after == "sent":
        sentinel(0, 17.0)
    if use_ag:
        SY.wait_ge(s_v.h, s_v.n)
        s_ld.inc(SY.dma_start(out=cc_in_d[:, 0:512], in_=S_sb[:, :]), 16)
        ag_in = s_ld.inc(SY.dma_start(out=cc_in_d[:, 512:520], in_=Gamma_sb[:, :]), 16)
        GP.wait_ge(s_ld.h, ag_in)
        cc_mark = s_cc.inc(GP.collective_compute(
            "AllGather", ALU.bypass,
            replica_groups=[[0, 1, 2, 3], [4, 5, 6, 7]],
            ins=[cc_in_d[:, :].opt()],
            outs=[cc_out_d[:, :].opt()],
        ))

    barrier()
    P1.close()

    P2 = Scope(nc)
    agland = P2.sbuf("agland", [128, 4 * 520], F32)
    Sinit_sb = P2.sbuf("Sinit_sb", [128, 512], F32)
    Srun_sb = P2.sbuf("Srun_sb", [128, 512], F32)
    St_sb = [P2.sbuf(f"St_sb{i}", [128, 512], F32) for i in range(2)]
    St_bf = [P2.sbuf(f"St_bf{i}", [128, 512], BF16) for i in range(2)]
    x2_sb = P2.sbuf("x2", [128, 4 * E], F32)
    ot_sb = P2.sbuf("ot", [128, 4 * E], F32)
    sg2_sb = P2.sbuf("sg2", [128, 4 * E], F32)
    qt2_sb = [P2.sbuf("qt2e", [128, 8 * TILE], BF16), P2.sbuf("qt2o", [128, 8 * TILE], BF16)]
    xp_sb = P2.sbuf("xp", [128, 4 * E], F32)
    on_bf = [P2.sbuf(f"on_bf{i}", [128, E], BF16) for i in range(2)]
    onT_sb = [P2.sbuf(f"onT_sb{i}", [128, 8 * C], BF16) for i in range(2)]
    o2_sb = [P2.sbuf(f"o2_sb{i}", [128, E], F32) for i in range(2)]
    osq_sb = P2.sbuf("osq_sb", [128, E], F32)
    ln2_bf = [P2.sbuf(f"ln2_bf{i}", [128, E], BF16) for i in range(2)]
    ln2T_sb = P2.sbuf("ln2T_sb", [128, 8 * TILE], BF16)
    probs_sb = P2.sbuf("probs_sb", [128, 4 * 16], F32)
    mx8_sb = P2.sbuf("mx8_sb", [128, 8], F32)
    rt_sc = P2.sbuf("rt_sc", [128, 8], F32)
    comb_sb = P2.sbuf("comb_sb", [128, 16], F32)
    combT_tile = P2.sbuf("combT_tile", [16, 4 * C], F32)
    rms_sc = P2.sbuf("rms_sc", [128, 96], F32)

    if aux and use_ag:
        SY.wait_ge(s_gp.h, gp_wo)
        s_st.inc(SY.dma_start(out=aux_d[1024:1152, 0:4096], in_=Wo_sb[:, :].bitcast(F32)), 16)
        SY.wait_ge(s_gp.h, gp_gateT)
        s_st.inc(SY.dma_start(out=aux_d[1280:1408, 0:64], in_=gateT_sb[:, :].bitcast(F32)), 16)

    def tile_loads(T):
        s_ld.inc(SY.dma_start(out=x2_sb.rearrange("p (c e) -> p c e", c=4),
                     in_=x_in[T * TILE:(T + 1) * TILE, :].rearrange("(c p) e -> p c e", p=128)), 16)
        s_ld.inc(SY.dma_start(out=ot_sb.rearrange("p (c e) -> p c e", c=4),
                     in_=olocal_d[T * TILE:(T + 1) * TILE, :].rearrange("(c p) e -> p c e", p=128)), 16)
        s_ld.inc(SY.dma_start(out=sg2_sb.rearrange("p (c e) -> p c e", c=4),
                     in_=sgate_d[T * TILE:(T + 1) * TILE, :].rearrange("(c p) e -> p c e", p=128)), 16)
        s_ld.inc(SY.dma_start(out=qt2_sb[0][:, :], in_=qte_d[:, T * 8 * TILE:(T + 1) * 8 * TILE]), 16)
        return s_ld.inc(SY.dma_start(out=qt2_sb[1][:, :],
                                     in_=qto_d[:, T * 8 * TILE:(T + 1) * 8 * TILE]), 16)

    ld2_t0 = tile_loads(0)

    if use_ag:
        SY.wait_ge(s_cc.h, cc_mark)
        ag_out = None
        for j in range(4):
            ag_out = s_ld.inc(SY.dma_start(out=agland[:, j * 520:(j + 1) * 520],
                                           in_=cc_out_d[j * 128:(j + 1) * 128, :]), 16)
        VE.wait_ge(s_ld.h, ag_out)
        VE.memset(Srun_sb[:, :], 0.0)
        VE.memset(Sinit_sb[:, :], 0.0)
        for j in range(4):
            Lj = agland[:, j * 520: j * 520 + 512]
            Gj = agland[:, j * 520 + 512: j * 520 + 520]
            VE.scalar_tensor_tensor(Sinit_sb[:, :], Srun_sb[:, :], emask_sb[:, j:j + 1],
                                    Sinit_sb[:, :], ALU.mult, ALU.add)
            for b in range(8):
                VE.scalar_tensor_tensor(Srun_sb[:, b * D:(b + 1) * D], Srun_sb[:, b * D:(b + 1) * D],
                                        Gj[:, b:b + 1], Lj[:, b * D:(b + 1) * D], ALU.mult, ALU.add)
        v_sinit = mark_last("vector", s_v)
    else:
        VE.memset(Sinit_sb[:, :], 0.0)
        v_sinit = mark_last("vector", s_v)

    if aux and use_ag:
        SY.wait_ge(s_v.h, v_sinit)
        s_st.inc(SY.dma_start(out=aux_d[0:128, 0:512], in_=Sinit_sb[:, :]), 16)
        s_st.inc(SY.dma_start(out=aux_d[0:128, 512:516], in_=emask_sb[:, :]), 16)
        s_st.inc(SY.dma_start(out=aux_d[0:128, 516:2596], in_=agland[:, :]), 16)

    if stop_after == "sent":
        sentinel(1, 17.0)
    st_p2 = [None] * ntile
    buf2_free = v_sinit
    buf2_te = 0
    pT_free2 = v_const
    pS_free2 = v_const
    pAB_free2 = 0
    corr_par_te = [0, 0]
    corr_par_ve = [0, 0]
    onbf_free = [0, 0]
    ln2bf_free = [0, 0]

    TE.wait_ge(s_gp.h, gp_wo)
    TE.wait_ge(s_gp.h, gp_gateT)
    for T in range(ntile):
        if T == 0:
            ld2 = ld2_t0
        else:
            SY.wait_ge(s_v.h, buf2_free)
            if buf2_te > 0:
                SY.wait_ge(s_t.h, buf2_te)
            ld2 = tile_loads(T)

        v2_ln2T = [None] * NCT
        v2_xp = [None] * NCT
        m_lg = [None] * NCT
        m_ssq16_arr = [None] * NCT
        m_rms_arr = [None] * NCT
        v_on_arr = [None] * NCT
        m_ssq_arr = [None] * NCT
        m_var_arr = [None] * NCT
        m_sqrt_arr = [None] * NCT
        v_ln2_arr = [None] * NCT

        def front(c):
            nonlocal corr_par_te, corr_par_ve
            p2 = c % 2
            g = T * NCT + c
            if c == 0:
                VE.wait_ge(s_ld.h, ld2)
            if corr_par_te[p2] > 0:
                VE.wait_ge(s_t.h, corr_par_te[p2])
            if c >= 2 and m_rms_arr[c - 2] is not None:
                VE.wait_ge(s_s.h, m_rms_arr[c - 2])
            for blk in range(8):
                VE.tensor_scalar_mul(St_sb[p2][:, blk * D:(blk + 1) * D],
                                     Sinit_sb[:, blk * D:(blk + 1) * D],
                                     Gres_sb[:, g * 8 + blk: g * 8 + blk + 1])
            v_St = s_v.inc(VE.tensor_copy(St_bf[p2][:, :], St_sb[p2][:, :]))
            TE.wait_ge(s_v.h, v_St)
            if c == 0:
                TE.wait_ge(s_ld.h, ld2)
            free_m = corr_par_ve[p2]
            for r in range(2):
                if free_m > 0:
                    TE.wait_ge(s_v.h, free_m)
                for i in range(8):
                    h = 8 * r + i
                    j, par = h // 2, h % 2
                    if p2 == 0:
                        dst = pO[:, i * 128: i * 128 + D]
                    else:
                        dst = pP[i // 4][:, (i % 4) * 128: (i % 4) * 128 + D]
                    TE.matmul(dst,
                              qt2_sb[par][:, j * TILE + c * 128: j * TILE + (c + 1) * 128],
                              St_bf[p2][:, j * D:(j + 1) * D],
                              start=True, stop=True)
                m_corr = mark_last("tensor", s_t)
                VE.wait_ge(s_t.h, m_corr)
                if p2 == 0:
                    VE.tensor_add(o2_sb[p2][:, :].rearrange("p (h d) -> p h d", h=16)[:, 8 * r:8 * r + 8, :],
                                  ot_sb[:, c * E:(c + 1) * E].rearrange("p (h d) -> p h d", h=16)[:, 8 * r:8 * r + 8, :],
                                  pO[:, :].rearrange("p (i s) -> p i s", i=8)[:, :, 0:D])
                else:
                    for half in range(2):
                        hsl = slice(8 * r + 4 * half, 8 * r + 4 * half + 4)
                        VE.tensor_add(o2_sb[p2][:, :].rearrange("p (h d) -> p h d", h=16)[:, hsl, :],
                                      ot_sb[:, c * E:(c + 1) * E].rearrange("p (h d) -> p h d", h=16)[:, hsl, :],
                                      pP[half][:, :].rearrange("p (i s) -> p i s", i=4)[:, :, 0:D])
                free_m = mark_last("vector", s_v)
            corr_par_te[p2] = m_corr
            corr_par_ve[p2] = free_m
            VE.tensor_mul(o2_sb[p2][:, :], o2_sb[p2][:, :], sg2_sb[:, c * E:(c + 1) * E])
            VE.tensor_mul(osq_sb[:, :], o2_sb[p2][:, :], o2_sb[p2][:, :])
            m_ssq16_arr[c] = s_v.inc(VE.reduce_sum(rms_sc[:, 48 * p2: 48 * p2 + 16],
                                                   osq_sb[:, :].rearrange("p (h d) -> p h d", h=16),
                                                   axis=AX.X))

        def b_rms(c):
            SC.wait_ge(s_v.h, m_ssq16_arr[c])
            rb = 48 * (c % 2)
            m_rms_arr[c] = s_s.inc(SC.activation(rms_sc[:, rb + 16: rb + 32], rms_sc[:, rb: rb + 16],
                                                 AF.Sqrt, scale=1.0 / D, bias=eps_col[:, 0:1]))

        def b_on(c):
            p2 = c % 2
            rb = 48 * p2
            VE.wait_ge(s_s.h, m_rms_arr[c])
            VE.reciprocal(rms_sc[:, rb + 32: rb + 48], rms_sc[:, rb + 16: rb + 32])
            if isinstance(onbf_free[p2], int) and onbf_free[p2] > 0:
                VE.wait_ge(s_t.h, onbf_free[p2])
            for h in range(H):
                VE.scalar_tensor_tensor(on_bf[p2][:, h * D:(h + 1) * D],
                                        o2_sb[p2][:, h * D:(h + 1) * D], rms_sc[:, rb + 32 + h: rb + 33 + h],
                                        nwbc_sb[:, h * D:(h + 1) * D], ALU.mult, ALU.mult)
            v_on_arr[c] = mark_last("vector", s_v)

        def b_onT(c):
            nonlocal pT_free2
            p2 = c % 2
            TE.wait_ge(s_v.h, v_on_arr[c])
            TE.wait_ge(s_v.h, pT_free2)
            for j in range(8):
                TE.transpose(pT[:, j * 128:(j + 1) * 128],
                             on_bf[p2][:, j * 128:(j + 1) * 128], ident_bf[:, :])
            m_onT = mark_last("tensor", s_t)
            VE.wait_ge(s_t.h, m_onT)
            VE.tensor_copy(onT_sb[p2][:, :], pT[:, :])
            pT_free2 = mark_last("vector", s_v)
            onbf_free[p2] = m_onT

        def b_wo(c):
            nonlocal pAB_free2
            p2 = c % 2
            TE.wait_ge(s_v.h, pT_free2)
            if pAB_free2 > 0:
                TE.wait_ge(s_v.h, pAB_free2)
            for nh, bank in ((0, pA), (1, pB)):
                for j in range(8):
                    TE.matmul(bank[:, :],
                              onT_sb[p2][:, j * 128:(j + 1) * 128],
                              Wo_sb[:, j * E + nh * 512: j * E + (nh + 1) * 512],
                              start=(j == 0), stop=(j == 7))
            m_wo = mark_last("tensor", s_t)
            VE.wait_ge(s_t.h, m_wo)
            if T >= 1 and c == 0:
                VE.wait_ge(s_st.h, st_p2[T - 1])
            VE.tensor_add(xp_sb[:, c * E: c * E + 512], x2_sb[:, c * E: c * E + 512], pA[:, :])
            VE.tensor_add(xp_sb[:, c * E + 512:(c + 1) * E], x2_sb[:, c * E + 512:(c + 1) * E], pB[:, :])
            pAB_free2 = mark_last("vector", s_v)
            v2_xp[c] = pAB_free2

        def b_ln2stats(c):
            lb = 16 * (c % 2)
            xsl = xp_sb[:, c * E:(c + 1) * E]
            SC.wait_ge(s_v.h, v2_xp[c])
            SC.activation(sq_scr[:, :], xsl, AF.Square, accum_out=lnS[:, lb + 2: lb + 3])
            m_ssq_arr[c] = mark_last("scalar", s_s)
            VE.reduce_sum(lnS[:, lb + 0: lb + 1], xsl, axis=AX.X)
            VE.tensor_scalar_mul(lnS[:, lb + 1: lb + 2], lnS[:, lb + 0: lb + 1], 1.0 / E)

        def b_ln2var(c):
            lb = 16 * (c % 2)
            VE.wait_ge(s_s.h, m_ssq_arr[c])
            VE.tensor_mul(lnS[:, lb + 5: lb + 6], lnS[:, lb + 1: lb + 2], lnS[:, lb + 1: lb + 2])
            VE.tensor_scalar(lnS[:, lb + 6: lb + 7], lnS[:, lb + 2: lb + 3], 1.0 / E, None, ALU.mult)
            VE.tensor_sub(lnS[:, lb + 7: lb + 8], lnS[:, lb + 6: lb + 7], lnS[:, lb + 5: lb + 6])
            m_var_arr[c] = s_v.inc(VE.tensor_scalar(lnS[:, lb + 3: lb + 4], lnS[:, lb + 7: lb + 8],
                                                    LN_EPS, None, ALU.add))

        def b_ln2sqrt(c):
            lb = 16 * (c % 2)
            SC.wait_ge(s_v.h, m_var_arr[c])
            m_sqrt_arr[c] = s_s.inc(SC.activation(lnS[:, lb + 4: lb + 5], lnS[:, lb + 3: lb + 4], AF.Sqrt))

        def b_ln2scale(c):
            p2 = c % 2
            lb = 16 * p2
            xsl = xp_sb[:, c * E:(c + 1) * E]
            VE.wait_ge(s_s.h, m_sqrt_arr[c])
            VE.reciprocal(lnS[:, lb + 8: lb + 9], lnS[:, lb + 4: lb + 5])
            VE.tensor_mul(lnS[:, lb + 9: lb + 10], lnS[:, lb + 1: lb + 2], lnS[:, lb + 8: lb + 9])
            if isinstance(ln2bf_free[p2], int) and ln2bf_free[p2] > 0:
                VE.wait_ge(s_t.h, ln2bf_free[p2])
            v_ln2_arr[c] = s_v.inc(VE.tensor_scalar(ln2_bf[p2][:, :], xsl,
                                                    lnS[:, lb + 8: lb + 9], lnS[:, lb + 9: lb + 10],
                                                    ALU.mult, ALU.subtract))

        def b_ln2T(c):
            nonlocal pT_free2
            p2 = c % 2
            TE.wait_ge(s_v.h, v_ln2_arr[c])
            TE.wait_ge(s_v.h, pT_free2)
            for j in range(8):
                TE.transpose(pT[:, j * 128:(j + 1) * 128],
                             ln2_bf[p2][:, j * 128:(j + 1) * 128], ident_bf[:, :])
            m_l2T = mark_last("tensor", s_t)
            VE.wait_ge(s_t.h, m_l2T)
            if T >= 1 and c == 0:
                VE.wait_ge(s_st.h, st_p2[T - 1])
            VE.tensor_copy(
                ln2T_sb[:, :].rearrange("p (j s) -> p j s", j=8)[:, :, c * 128:(c + 1) * 128],
                pT[:, :].rearrange("p (j s) -> p j s", j=8))
            pT_free2 = mark_last("vector", s_v)
            ln2bf_free[p2] = m_l2T
            v2_ln2T[c] = pT_free2

        def b_probs(c):
            nonlocal pS_free2
            TE.wait_ge(s_v.h, v2_ln2T[c])
            if c == 0:
                TE.wait_ge(s_v.h, pS_free2)
            for j in range(8):
                TE.matmul(pS[:, c * 128: c * 128 + 16],
                          ln2T_sb[:, j * TILE + c * 128: j * TILE + (c + 1) * 128],
                          gateT_sb[:, j * NEXP:(j + 1) * NEXP],
                          start=(j == 0), stop=(j == 7))
            m_lg[c] = mark_last("tensor", s_t)

        for pair in range(NCT // 2):
            c0, c1 = 2 * pair, 2 * pair + 1
            front(c0)
            front(c1)
            b_rms(c0); b_rms(c1)
            b_on(c0); b_on(c1)
            b_onT(c0); b_onT(c1)
            b_wo(c0); b_wo(c1)
            b_ln2stats(c0); b_ln2stats(c1)
            b_ln2var(c0); b_ln2var(c1)
            b_ln2sqrt(c0); b_ln2sqrt(c1)
            b_ln2scale(c0); b_ln2scale(c1)
            b_ln2T(c0); b_ln2T(c1)
            b_probs(c0); b_probs(c1)

        VE.wait_ge(s_t.h, m_lg[NCT - 1])
        VE.reduce_max(rt_sc[:, 0:4], pS[:, 0:512].rearrange("p (c k) -> p c k", c=4)[:, :, 0:16],
                      axis=AX.X, negate=True)
        for c in range(NCT):
            VE.tensor_scalar(probs_sb[:, c * 16:(c + 1) * 16], pS[:, c * 128: c * 128 + 16],
                             rt_sc[:, c:c + 1], None, ALU.add)
        m_pre = mark_last("vector", s_v)
        SC.wait_ge(s_v.h, m_pre)
        m_exp = s_s.inc(SC.activation(probs_sb[:, :], probs_sb[:, :], AF.Exp))
        VE.wait_ge(s_s.h, m_exp)
        if T >= 1:
            VE.wait_ge(s_st.h, st_p2[T - 1])
        for c in range(NCT):
            psl = probs_sb[:, c * 16:(c + 1) * 16]
            VE.max(out=mx8_sb[:, :], in_=psl)
            VE.drain()
            VE.tensor_scalar(comb_sb[:, :], psl, mx8_sb[:, 3:4], None, ALU.is_ge)
            VE.tensor_mul(comb_sb[:, :], comb_sb[:, :], psl)
            VE.reduce_sum(rt_sc[:, 4:5], comb_sb[:, :], axis=AX.X)
            VE.reciprocal(rt_sc[:, 5:6], rt_sc[:, 4:5])
            m_cb = s_v.inc(VE.tensor_scalar_mul(comb_sb[:, :], comb_sb[:, :], rt_sc[:, 5:6]))
            TE.wait_ge(s_v.h, m_cb)
            if c > 0:
                TE.wait_ge(s_v.h, pS_free2)
            m_cT = s_t.inc(TE.transpose(pS[0:16, 0:128], comb_sb[:, :], ident_f32[:, :]))
            VE.wait_ge(s_t.h, m_cT)
            VE.tensor_copy(combT_tile[:, c * 128:(c + 1) * 128], pS[0:16, 0:128])
            pS_free2 = mark_last("vector", s_v)

        SY.wait_ge(s_v.h, v2_xp[NCT - 1])
        s_st.inc(SY.dma_start(out=xp_d[T * TILE:(T + 1) * TILE, :].rearrange("(c p) e -> p c e", p=128),
                              in_=xp_sb.rearrange("p (c e) -> p c e", c=4)), 16)
        SY.wait_ge(s_v.h, v2_ln2T[NCT - 1])
        s_st.inc(SY.dma_start(out=ln2T_d[:, T * 8 * TILE:(T + 1) * 8 * TILE], in_=ln2T_sb[:, :]), 16)
        SY.wait_ge(s_v.h, pS_free2)
        st_p2[T] = s_st.inc(SY.dma_start(out=combT_d[:, T * TILE:(T + 1) * TILE], in_=combT_tile[:, :]), 16)
        buf2_free = pAB_free2
        buf2_te = corr_par_te[1]

    if aux and use_ag:
        SY.wait_ge(s_st.h, s_st.n)
        aux_srcs = [
            (128, xp_d[0:128, 0:1024], False),
            (256, xp_d[2048:2176, 0:1024], False),
            (384, ln2T_d[:, 0:2048], True),
            (640, olocal_d[0:128, 0:1024], False),
            (768, sgate_d[0:128, 0:1024], False),
            (896, qte_d[:, 0:2048], True),
        ]
        aux_last = None
        for row, src, isbf in aux_srcs:
            if aux_last is not None:
                SY.wait_ge(s_st.h, aux_last)
            dst = x2_sb[:, 0:1024].bitcast(BF16) if isbf else x2_sb[:, 0:1024]
            m = s_ld.inc(SY.dma_start(out=dst, in_=src), 16)
            SY.wait_ge(s_ld.h, m)
            aux_last = s_st.inc(SY.dma_start(out=aux_d[row:row + 128, 0:1024], in_=x2_sb[:, 0:1024]), 16)
        SY.wait_ge(s_st.h, aux_last)
        m = s_ld.inc(SY.dma_start(out=x2_sb[0:16, 0:4096], in_=combT_d[0:16, 0:4096]), 16)
        SY.wait_ge(s_ld.h, m)
        s_st.inc(SY.dma_start(out=aux_d[512:528, 0:4096], in_=x2_sb[0:16, 0:4096]), 16)
        s_st.inc(SY.dma_start(out=aux_d[1152:1280, 0:4096], in_=Wo_sb[:, :].bitcast(F32)), 16)
        s_st.inc(SY.dma_start(out=aux_d[1408:1536, 0:256], in_=Gres_sb[:, :]), 16)

    if stop_after in ("probe_ol", "probe_sg", "probe_qt", "probe_gres", "probe_sinit"):
        SY.wait_ge(s_v.h, v_sinit)
        src_t = {"probe_ol": olocal_d, "probe_sg": sgate_d}.get(stop_after)
        if src_t is not None:
            for u in range(4):
                m = s_ld.inc(SY.dma_start(out=x2_sb[:, 0:E], in_=src_t[u * 128:(u + 1) * 128, :]), 16)
                SY.wait_ge(s_ld.h, m)
                s_st.inc(SY.dma_start(out=out_d[u * 128:(u + 1) * 128, :], in_=x2_sb[:, 0:E]), 16)
        elif stop_after == "probe_sinit":
            VE.wait_ge(s_ld.h, s_ld.n)
            mv = s_v.inc(VE.tensor_copy(x2_sb[:, 0:512], Sinit_sb[:, :]))
            VE.tensor_copy(x2_sb[:, 512:516], emask_sb[:, :])
            mv = mark_last("vector", s_v)
            SY.wait_ge(s_v.h, mv)
            stj = s_st.inc(SY.dma_start(out=out_d[0:128, 0:516], in_=x2_sb[:, 0:516]), 16)
            for j in range(4):
                VE.wait_ge(s_st.h, stj)
                mv = s_v.inc(VE.tensor_copy(x2_sb[:, 0:520], agland[:, j * 520:(j + 1) * 520]))
                SY.wait_ge(s_v.h, mv)
                stj = s_st.inc(SY.dma_start(out=out_d[128 * (1 + j):128 * (2 + j), 0:520],
                                            in_=x2_sb[:, 0:520]), 16)
            m = s_ld.inc(SY.dma_start(out=x2_sb[:, 0:520], in_=cc_in_d[:, :]), 16)
            SY.wait_ge(s_ld.h, m)
            s_st.inc(SY.dma_start(out=out_d[128 * 5:128 * 6, 0:520], in_=x2_sb[:, 0:520]), 16)
        elif stop_after == "probe_qt":
            m = s_ld.inc(SY.dma_start(out=qt2_sb[0][:, :], in_=qte_d[:, 0:8 * TILE]), 16)
            SY.wait_ge(s_ld.h, m)
            VE.wait_ge(s_ld.h, m)
            mv = s_v.inc(VE.tensor_copy(x2_sb[:, 0:E], qt2_sb[0][:, 0:1024]))
            SY.wait_ge(s_v.h, mv)
            s_st.inc(SY.dma_start(out=out_d[0:128, :], in_=x2_sb[:, 0:E]), 16)
        else:
            mv = s_v.inc(VE.tensor_copy(x2_sb[:, 0:8 * nchunk], Gres_sb[:, :]))
            SY.wait_ge(s_v.h, mv)
            s_st.inc(SY.dma_start(out=out_d[0:128, 0:8 * nchunk], in_=x2_sb[:, 0:8 * nchunk]), 16)
        SY.wait_ge(s_st.h, s_st.n)
        return finish([P2])
    if stop_after == "p2a":
        sentinel(1, 11.0, (s_v, s_v.n))
        VE.wait_ge(s_s.h, s_s.n)
        VE.wait_ge(s_t.h, s_t.n)
        sentinel(2, 11.0)
        return finish([P2])

    barrier()
    P2.close()

    P3 = Scope(nc)
    ln2T_ft = P3.sbuf("ln2T_ft", [128, 8 * FT], BF16)
    moe_acc = P3.sbuf("moe_acc", [128, nsub * E], F32)
    w1e_sb = [P3.sbuf(f"w1e{i}", [128, 8 * HID], BF16) for i in range(2)]
    w3e_sb = [P3.sbuf(f"w3e{i}", [128, 8 * HID], BF16) for i in range(2)]
    w2e_sb = [P3.sbuf(f"w2e{i}", [128, 4 * E], BF16) for i in range(4)]
    w2s_sb = P3.sbuf("w2s_sb", [128, 4 * E], BF16)
    cb_sb = [P3.sbuf(f"cb{i}", [128, FT], F32) for i in range(2)]
    act_sb = [P3.sbuf(f"act{i}", [128, 4 * FT], BF16) for i in range(5)]
    silu_scr = [P3.sbuf(f"silu{i}", [128, 512], BF16) for i in range(3)]
    silu2_scr = [P3.sbuf(f"silu2{i}", [128, 512], BF16) for i in range(3)]

    GROUPS = [[0, 1, 2, 3], [4, 5, 6, 7], [8, 9, 10, 11], [12, 13, 14, 15, None]]

    moe_free = s_v.n
    ln2ft_free = 0
    w13_free_wb = [0, 0]
    cb_free_wb = [0, 0]
    w2_free = 0
    act_free = 0
    st_out = [None] * nftile

    for FTi in range(nftile):
        T0 = FTi * (FT // TILE)
        if FTi >= 1:
            SY.wait_ge(s_t.h, ln2ft_free)
        for th2 in range(FT // TILE):
            s_ld.inc(SY.dma_start(
                out=ln2T_ft[:, :].rearrange("p (j s) -> p j s", j=8)[:, :, th2 * TILE:(th2 + 1) * TILE],
                in_=ln2T_d[:, (T0 + th2) * 8 * TILE:(T0 + th2 + 1) * 8 * TILE]
                .rearrange("p (j s) -> p j s", j=8)), 16)
        if FTi >= 1:
            SY.wait_ge(s_st.h, st_out[FTi - 1])
        ld_ft = s_ld.inc(SY.dma_start(
            out=moe_acc.rearrange("p (u e) -> p u e", e=E),
            in_=xp_d[FTi * FT:(FTi + 1) * FT, :].rearrange("(u p) e -> p u e", p=128)),
            16)

        if stop_after == "f0" and FTi == 0:
            VE.wait_ge(s_ld.h, ld_ft)
            sentinel(1, 13.0)
            return finish([P3])
        if stop_after == "f0x" and FTi == 0:
            SY.wait_ge(s_ld.h, ld_ft)
            for u in range(4):
                s_st.inc(SY.dma_start(out=out_d[u * 128:(u + 1) * 128, :],
                                      in_=moe_acc[:, u * E:(u + 1) * E]), 16)
            VE.wait_ge(s_st.h, s_st.n)
            return finish([P3])
        hstep = 0
        bank_act_mark = {0: 0, 1: 0, 2: 0}
        for gi, grp in enumerate(GROUPS):
            act_marks = {}
            ldw13 = {}
            ldw2 = {}
            ldcb = {}
            for ei, e in enumerate(grp):
                shared = e is None
                wb = (e % 2) if not shared else 0
                if shared:
                    GP.wait_ge(s_t.h, w13_free_wb[wb])
                    s_gp.inc(GP.dma_start(out=w1e_sb[wb].rearrange("p (a n) -> p a n", a=8), in_=sw1T_in.rearrange("(a p) n -> p a n", p=128)), 16)
                    ldw13[ei] = s_gp.inc(
                        GP.dma_start(out=w3e_sb[wb].rearrange("p (a n) -> p a n", a=8), in_=sw3T_in.rearrange("(a p) n -> p a n", p=128)), 16)
                    GP.wait_ge(s_t.h, w2_free)
                    ldw2[ei] = s_gp.inc(
                        GP.dma_start(out=w2s_sb.rearrange("p (a n) -> p a n", a=4), in_=sw2T_in.rearrange("(a p) n -> p a n", p=128)), 16)
                else:
                    SY.wait_ge(s_t.h, w13_free_wb[wb])
                    s_ld.inc(SY.dma_start(out=w1e_sb[wb].rearrange("p (a n) -> p a n", a=8), in_=w1b_d[e].rearrange("(a p) n -> p a n", p=128)), 16)
                    ldw13[ei] = s_ld.inc(
                        SY.dma_start(out=w3e_sb[wb].rearrange("p (a n) -> p a n", a=8), in_=w3b_d[e].rearrange("(a p) n -> p a n", p=128)), 16)
                    SY.wait_ge(s_t.h, w2_free)
                    ldw2[ei] = s_ld.inc(
                        SY.dma_start(out=w2e_sb[ei].rearrange("p (a n) -> p a n", a=4), in_=w2b_d[e].rearrange("(a p) n -> p a n", p=128)), 16)
                    GP.wait_ge(s_v.h, cb_free_wb[wb])
                    ldcb[ei] = s_gp.inc(GP.dma_start(out=cb_sb[wb][:, :],
                                                     in_=combT_d[e:e + 1, FTi * FT:(FTi + 1) * FT]
                                                     .broadcast_to([128, FT])), 16)
                w1 = w1e_sb[wb]
                w3 = w3e_sb[wb]
                act = act_sb[ei]
                TE.wait_ge(s_gp.h if shared else s_ld.h, ldw13[ei])
                TE.wait_ge(s_ld.h, ld_ft)
                first_of_expert = True
                for jj in range(4):
                    for th in range(nth):
                        bk = hstep % 3
                        hstep += 1
                        ph1 = (pA[:, :], pB[:, :], pO[:, 0:512])[bk]
                        ph3 = (pP[0][:, :], pP[1][:, :], pO[:, 512:1024])[bk]
                        TE.wait_ge(s_v.h, bank_act_mark[bk])
                        for dj in range(8):
                            TE.matmul(ph1[:, :],
                                      w1[:, dj * HID + jj * 128: dj * HID + (jj + 1) * 128],
                                      ln2T_ft[:, dj * FT + th * 512: dj * FT + (th + 1) * 512],
                                      start=(dj == 0), stop=(dj == 7))
                        for dj in range(8):
                            TE.matmul(ph3[:, :],
                                      w3[:, dj * HID + jj * 128: dj * HID + (jj + 1) * 128],
                                      ln2T_ft[:, dj * FT + th * 512: dj * FT + (th + 1) * 512],
                                      start=(dj == 0), stop=(dj == 7))
                        m_h = mark_last("tensor", s_t)
                        SC.wait_ge(s_t.h, m_h)
                        SC.wait_ge(s_v.h, bank_act_mark[bk])
                        m_si = s_s.inc(SC.activation(silu_scr[bk][:, :], ph1[:, :], AF.Silu))
                        VE.wait_ge(s_s.h, m_si)
                        if shared:
                            VE.tensor_copy(silu2_scr[bk][:, :], ph3[:, :])
                        else:
                            VE.wait_ge(s_gp.h, ldcb[ei])
                            VE.tensor_mul(silu2_scr[bk][:, :], ph3[:, :],
                                          cb_sb[wb][:, th * 512:(th + 1) * 512])
                        if first_of_expert and act_free > 0:
                            VE.wait_ge(s_t.h, act_free)
                        first_of_expert = False
                        m_act = s_v.inc(VE.tensor_mul(act[:, jj * FT + th * 512: jj * FT + (th + 1) * 512],
                                                      silu_scr[bk][:, :], silu2_scr[bk][:, :]))
                        bank_act_mark[bk] = m_act
                        act_marks[(ei, jj, th)] = m_act
                w13_free_wb[wb] = s_t.n
                if not shared:
                    cb_free_wb[wb] = s_v.n
            if stop_after == "f1" and FTi == 0 and gi == 0:
                VE.wait_ge(s_s.h, s_s.n)
                VE.wait_ge(s_t.h, s_t.n)
                sentinel(1, 13.0)
                return finish([P3])
            for ei, e in enumerate(grp):
                TE.wait_ge(s_v.h, act_marks[(ei, 3, nth - 1)])
                TE.wait_ge(s_gp.h if e is None else s_ld.h, ldw2[ei])
            moe_free_par = [moe_free, moe_free]
            for u in range(nsub):
                up = u % 2
                banks = (pO[:, 0:512], pO[:, 512:1024]) if up == 0 else (pP[0][:, :], pP[1][:, :])
                if moe_free_par[up] > 0:
                    TE.wait_ge(s_v.h, moe_free_par[up])
                first_half = [True, True]
                for ei, e in enumerate(grp):
                    w2 = w2s_sb if e is None else w2e_sb[ei]
                    act = act_sb[ei]
                    for jj in range(4):
                        last = (ei == len(grp) - 1) and (jj == 3)
                        for nh2 in range(2):
                            TE.matmul(banks[nh2],
                                      act[:, jj * FT + u * 128: jj * FT + (u + 1) * 128],
                                      w2[:, jj * E + nh2 * 512: jj * E + (nh2 + 1) * 512],
                                      start=first_half[nh2], stop=last)
                            first_half[nh2] = False
                m_eo = mark_last("tensor", s_t)
                VE.wait_ge(s_t.h, m_eo)
                VE.wait_ge(s_ld.h, ld_ft)
                VE.tensor_add(moe_acc[:, u * E: u * E + 512],
                              moe_acc[:, u * E: u * E + 512], banks[0])
                moe_free_par[up] = s_v.inc(VE.tensor_add(moe_acc[:, u * E + 512:(u + 1) * E],
                                                         moe_acc[:, u * E + 512:(u + 1) * E], banks[1]))
            moe_free = moe_free_par[(nsub - 1) % 2]
            for _bk in (0, 1, 2):
                bank_act_mark[_bk] = max(bank_act_mark[_bk], moe_free_par[1])
            if stop_after in ("f2", "f2g1", "f2g2", "f2g3") and FTi == 0 and gi == {"f2": 0, "f2g1": 1, "f2g2": 2, "f2g3": 3}[stop_after]:
                VE.wait_ge(s_s.h, s_s.n)
                VE.wait_ge(s_t.h, s_t.n)
                sentinel(1, 13.0)
                return finish([P3])
            act_free = s_t.n
            w2_free = s_t.n
        if stop_after == "f3" and FTi == 0:
            VE.wait_ge(s_s.h, s_s.n)
            VE.wait_ge(s_t.h, s_t.n)
            sentinel(1, 13.0)
            return finish([P3])
        if stop_after == "f3s" and FTi == 0:
            SY.wait_ge(s_v.h, moe_free)
            s_st.inc(SY.dma_start(out=out_d[0:128, :], in_=moe_acc[:, 0:E]), 16)
            VE.wait_ge(s_st.h, s_st.n)
            sentinel(1, 13.0)
            return finish([P3])
        SY.wait_ge(s_v.h, moe_free)
        for u in range(nsub):
            st_out[FTi] = s_st.inc(SY.dma_start(
                out=out_d[FTi * FT + u * 128: FTi * FT + (u + 1) * 128, :],
                in_=moe_acc[:, u * E:(u + 1) * E]), 16)
        ln2ft_free = s_t.n

    return finish([P3])


def _host_inputs(inputs, core, tpc, use_ag):
    x = np.asarray(inputs["L0_attn_x"])
    if use_ag:
        b, q = core // 4, core % 4
        xs = np.ascontiguousarray(x[b].reshape(-1, E)[q * tpc:(q + 1) * tpc], dtype=np.float32)
    else:
        q = 0
        xs = np.ascontiguousarray(np.asarray(x, np.float32).reshape(-1, E)[:tpc])
    return {
        "x": xs,
        "Wq": np.ascontiguousarray(inputs["L0_attn_Wq"], np.float32),
        "Wk": np.ascontiguousarray(inputs["L0_attn_Wk"], np.float32),
        "Wv": np.ascontiguousarray(inputs["L0_attn_Wv"], np.float32),
        "Wo": np.ascontiguousarray(inputs["L0_attn_Wo"], np.float32),
        "G1": np.ascontiguousarray(inputs["L0_attn_G1"], np.float32),
        "G2": np.ascontiguousarray(inputs["L0_attn_G2"], np.float32),
        "delta_c": np.ascontiguousarray(np.asarray(inputs["L0_attn_delta"], np.float32).reshape(8, 128).T),
        "normw_c": np.ascontiguousarray(np.asarray(inputs["L0_attn_norm_w"], np.float32).reshape(8, 128).T),
        "normw_row": np.ascontiguousarray(np.asarray(inputs["L0_attn_norm_w"], np.float32).reshape(1, E)),
        "gateT": np.ascontiguousarray(np.asarray(inputs["L0_ffn_gate_w"], np.float32).T),
        "w1T": np.ascontiguousarray(np.asarray(inputs["L0_ffn_w1"], np.float32).transpose(0, 2, 1)),
        "w3T": np.ascontiguousarray(np.asarray(inputs["L0_ffn_w3"], np.float32).transpose(0, 2, 1)),
        "w2T": np.ascontiguousarray(np.asarray(inputs["L0_ffn_w2"], np.float32).transpose(0, 2, 1)),
        "sw1T": np.ascontiguousarray(np.asarray(inputs["L0_ffn_sw1"], np.float32).T),
        "sw3T": np.ascontiguousarray(np.asarray(inputs["L0_ffn_sw3"], np.float32).T),
        "sw2T": np.ascontiguousarray(np.asarray(inputs["L0_ffn_sw2"], np.float32).T),
        "ident_bf": np.eye(128, dtype=ml_dtypes.bfloat16),
        "ident_f32": np.eye(128, dtype=np.float32),
        "mask_st": np.ascontiguousarray(np.triu(np.ones((128, 128), np.float32))),
        "emask": np.ascontiguousarray(np.repeat((np.arange(4) == q).astype(np.float32)[None, :], 128, axis=0)),
    }


def kernel(**inputs):
    tpc = 4096
    nc = build_program(tpc, use_ag=True)
    in_maps = [_host_inputs(inputs, c, tpc, True) for c in range(8)]
    res = run_bass_kernel_spmd(nc, in_maps, core_ids=list(range(8)))
    out = np.empty((2, 16384, E), np.float32)
    for c in range(8):
        b, q = c // 4, c % 4
        out[b, q * tpc:(q + 1) * tpc, :] = res.results[c]["out"]
    return out
